# version 08
# baseline (speedup 1.0000x reference)
import numpy as np
import ml_dtypes
from contextlib import ExitStack
import concourse.bass as bass
import concourse.mybir as mybir
from concourse.bass_utils import run_bass_kernel_spmd

F32 = mybir.dt.float32
BF16 = mybir.dt.bfloat16
I32 = mybir.dt.int32
U8 = mybir.dt.uint8
AF = mybir.ActivationFunctionType
ALU = mybir.AluOpType
AX = mybir.AxisListType

S = 2048
H = 2048
II = 8192

NCORES = 8


class Sched:
    ENGINES = ("sync", "gpsimd", "vector", "scalar", "tensor")

    def __init__(self):
        self.q = {e: [] for e in self.ENGINES}
        self.c = {}

    def bump(self, sem, by=1):
        self.c[sem] = self.c.get(sem, 0) + by
        return self.c[sem]

    def cur(self, sem):
        return self.c.get(sem, 0)

    def dma(self, qeng, sem, out, in_, **kw):
        n = self.bump(sem, 16)
        def fn(e, SM, out=out, in_=in_, sem=sem, kw=kw):
            e.dma_start(out=out, in_=in_, **kw).then_inc(SM[sem], 16)
        self.q[qeng].append(fn)
        return n

    def wait(self, eng, sem, val):
        if val and val > 0:
            def fn(e, SM, sem=sem, val=val):
                e.wait_ge(SM[sem], val)
            self.q[eng].append(fn)

    def op(self, eng, emit, sem=None, inc=1):
        n = None
        if sem is not None:
            n = self.bump(sem, inc)
            def fn(e, SM, emit=emit, sem=sem, inc=inc):
                emit(e).then_inc(SM[sem], inc)
        else:
            def fn(e, SM, emit=emit):
                emit(e)
        self.q[eng].append(fn)
        return n

    def emit(self, block, sems):
        for eng in self.ENGINES:
            if not self.q[eng]:
                continue
            stream = self.q[eng]
            def body(e, stream=stream, sems=sems):
                for fn in stream:
                    fn(e, sems)
            getattr(block, eng)(body)



def t_row(sc, B, src_tile_fn, nk, dst_fn, ident):
    evh = B.setdefault("_evh", [])
    for k in range(nk):
        pt = B["ptr"][k % 2]
        prev = B["_ptr_evac"][k % 2]
        if prev:
            sc.wait("tensor", "sa", prev)
        sc.op("tensor", lambda e, pt=pt, k=k: e.transpose(pt[:], src_tile_fn(k), ident), "st")
        ntr = sc.cur("st")
        sc.wait("scalar", "st", ntr)
        ne = sc.op("scalar", lambda e, pt=pt, k=k: e.copy(dst_fn(k), pt[:]), "sa")
        B["_ptr_evac"][k % 2] = ne
    return sc.cur("sa")


def emit_const_prep(sc, B, scales):
    n_id = sc.dma("sync", "idm", B["identf"][:], B["in_idf"])
    sc.wait("vector", "idm", n_id)
    nid_b = sc.op("vector", lambda e: e.tensor_copy(B["identb"][:], B["identf"][:]), "sv")
    last_tr = 0
    last_ev = 0
    for (name_in, name_out, nblk) in scales:
        if last_tr:
            sc.wait("sync", "st", last_tr)
        n = sc.dma("sync", "scl", B["stmp"][0:nblk, :],
                   B[name_in].rearrange("(b p) -> b p", p=128))
        sc.wait("tensor", "scl", n)
        sc.wait("tensor", "idm", n_id)
        if last_ev:
            sc.wait("tensor", "sv", last_ev)
        sc.op("tensor", lambda e, nb=nblk: e.transpose(
            B["pscl"][:, 0:nb], B["stmp"][0:nb, :], B["identf"][0:nb, 0:nb]), "st")
        last_tr = sc.cur("st")
        sc.wait("vector", "st", last_tr)
        last_ev = sc.op("vector", lambda e, no=name_out, nb=nblk: e.tensor_copy(
            B[no][:, 0:nb], B["pscl"][:, 0:nb]), "sv")
    return nid_b


def emit_ln_pass(sc, B, x_ap, zdst, tag):
    nb = B.get("xin_nbuf", 2)
    for st16 in range(16):
        db = st16 % nb
        xin = B["xin"][db]
        if st16 >= nb:
            sc.wait("sync", "sv", B.get(f"{tag}_lnv_{st16-nb}", 0))
            sc.wait("sync", "sa", B.get(f"{tag}_lna_{st16-nb}", 0))
        n_dma = sc.dma("sync", f"ldx{db}", xin[:], x_ap[st16 * 128:(st16 + 1) * 128, :])
        sc.wait("scalar", f"ldx{db}", n_dma)
        if st16 >= 1:
            sc.wait("scalar", "sv", B.get(f"{tag}_lnss_{st16-1}", 0))
        sqdst = B["sqtmp"] if "sqtmp" in B else B["xlnb"]
        if "sqtmp" not in B:
            sc.wait("scalar", "st", B.get(f"{tag}_lntr_{st16-1}", 0))
        na = sc.op("scalar", lambda e, xin=xin, sqdst=sqdst: e.activation(
            sqdst[:], xin[:], AF.Square, accum_out=B["st_s"][:]), "sa")
        B[f"{tag}_lna_{st16}"] = na
        sc.wait("vector", f"ldx{db}", n_dma)
        sc.op("vector", lambda e, xin=xin: e.reduce_sum(B["st_m"][:], xin[:], AX.X), "sv")
        sc.wait("vector", "sa", na)
        B[f"{tag}_lnss_{st16}"] = sc.op("vector", lambda e: e.tensor_scalar(
            B["st_v"][:], B["st_s"][:], 1.0 / H, None, ALU.mult), "sv")
        n_ = sc.op("vector", lambda e: e.tensor_scalar(B["st_m"][:], B["st_m"][:], 1.0 / H, None, ALU.mult), "sv")
        sc.wait("vector", "sv", n_)
        n_ = sc.op("vector", lambda e: e.tensor_mul(B["st_t"][:], B["st_m"][:], B["st_m"][:]), "sv")
        sc.wait("vector", "sv", n_)
        n_ = sc.op("vector", lambda e: e.tensor_sub(B["st_v"][:], B["st_v"][:], B["st_t"][:]), "sv")
        sc.wait("vector", "sv", n_)
        nv = sc.op("vector", lambda e: e.tensor_scalar(B["st_v"][:], B["st_v"][:], 1e-5, None, ALU.add), "sv")
        sc.wait("scalar", "sv", nv)
        sc.op("scalar", lambda e: e.activation(B["st_t"][:], B["st_v"][:], AF.Ln), "sa")
        nr = sc.op("scalar", lambda e: e.activation(B["st_r"][:], B["st_t"][:], AF.Exp, scale=-0.5), "sa")
        sc.wait("vector", "sa", nr)
        nl = sc.op("vector", lambda e, xin=xin: e.tensor_scalar(
            B["xlnb"][:], xin[:], B["st_m"][:, 0:1], B["st_r"][:, 0:1], ALU.subtract, ALU.mult), "sv")
        B[f"{tag}_lnv_{st16}"] = nl
        stg = B["sT"][st16 % 2]
        prev_zs = B.get(f"{tag}_lng_{st16-2}", B["_zs_last"][st16 % 2])
        if prev_zs:
            sc.wait("tensor", f"zs{st16 % 2}", prev_zs)
        sc.wait("tensor", "sv", nl)
        nev = t_row(sc, B, lambda k: B["xlnb"][:, k * 128:(k + 1) * 128], 16,
                    lambda k, stg=stg: stg[:, k * 128:(k + 1) * 128], B["identb"][:])
        B[f"{tag}_lntr_{st16}"] = sc.cur("st")
        sc.wait("gpsimd", "sa", nev)
        ng = sc.dma("gpsimd", f"zs{st16 % 2}",
                    zdst[st16 // 4].rearrange("hc p c -> p hc c")[
                        :, :, (st16 % 4) * 128:(st16 % 4 + 1) * 128],
                    stg[:].rearrange("p (hc c) -> p hc c", c=128))
        B[f"{tag}_lng_{st16}"] = ng
        B["_zs_last"][st16 % 2] = ng


def emit_ffn_weight_prep_int(sc, B, zwgT, zwuT, zwdT):
    mats = [("in_gw", "gsT", zwgT), ("in_uw", "usT", zwuT)]
    k = 0
    for ib in range(64):
        for mi, (wname, sname, zdst) in enumerate(mats):
            db = k % 2
            if k >= 2:
                sc.wait("sync", "sv", B.get(f"dq_{k-2}", 0))
            nd = sc.dma("sync", f"wgu{db}", B["wgu_int"][db][:], B[wname][ib * 128:(ib + 1) * 128, :])
            sc.wait("vector", f"wgu{db}", nd)
            if k >= 2:
                sc.wait("vector", "st", B.get(f"dqtr_{k-2}", 0))
            ndq = sc.op("vector", lambda e, db=db, sname=sname, ib=ib: e.tensor_scalar(
                B["wdq"][db][:], B["wgu_int"][db][:], 128.0, B[sname][:, ib:ib + 1],
                ALU.subtract, ALU.mult), "sv")
            B[f"dq_{k}"] = ndq
            slot = k % 2
            stg = B["sT"][slot]
            if B["_zs_last"][slot]:
                sc.wait("tensor", f"zs{slot}", B["_zs_last"][slot])
            sc.wait("tensor", "sv", ndq)
            nev = t_row(sc, B, lambda kk, db=db: B["wdq"][db][:, kk * 128:(kk + 1) * 128], 16,
                        lambda kk, stg=stg: stg[:, kk * 128:(kk + 1) * 128], B["identb"][:])
            B[f"dqtr_{k}"] = sc.cur("st")
            sc.wait("gpsimd", "sa", nev)
            nz = sc.dma("gpsimd", f"zs{slot}", zdst[ib, :, :], stg[:])
            B["_zs_last"][slot] = nz
            k += 1

    for hb in range(16):
        for half in range(2):
            if hb == 0 and half == 0:
                sc.wait("sync", "sv", B.get("dq_127", 0))
            if hb + half >= 1:
                sc.wait("sync", "sv", B.get(f"ddq_{hb}_{half-1}", B.get(f"ddq_{hb-1}_{1}", 0)))
            nd = sc.dma("sync", "wd", B["wd_int"][:],
                        B["in_dw"][hb * 128:(hb + 1) * 128, half * 4096:(half + 1) * 4096])
            sc.wait("vector", "wd", nd)
            if hb == 0 and half == 0:
                sc.wait("vector", "st", B.get("dqtr_127", 0))
            if hb >= 1 and half == 0:
                sc.wait("vector", "st", B.get(f"ddqtr_{hb-1}", 0))
            B[f"ddq_{hb}_{half}"] = sc.op("vector", lambda e, hb=hb, half=half: e.tensor_scalar(
                B["wdstg"][:, half * 4096:(half + 1) * 4096], B["wd_int"][:], 128.0,
                B["dsT"][:, hb:hb + 1], ALU.subtract, ALU.mult), "sv")
        ndq = B[f"ddq_{hb}_{1}"]
        if hb >= 1:
            sc.wait("tensor", "zd", B.get(f"dz_{hb-1}", 0))
        sc.wait("tensor", "sv", ndq)
        nev = t_row(sc, B, lambda kk: B["wdstg"][:, kk * 128:(kk + 1) * 128], 64,
                    lambda kk: B["xlnTq"][:, kk * 128:(kk + 1) * 128], B["identb"][:])
        B[f"ddqtr_{hb}"] = sc.cur("st")
        sc.wait("gpsimd", "sa", nev)
        nz = sc.dma("gpsimd", "zd",
                    zwdT[:, hb // 4].rearrange("ic p c -> p ic c")[
                        :, :, (hb % 4) * 128:(hb % 4 + 1) * 128],
                    B["xlnTq"][:].rearrange("p (ic c) -> p ic c", c=128))
        B[f"dz_{hb}"] = nz
    return sc.cur("zd")


def emit_ffn_matmuls(sc, B, zxlnT, zwgT, zwuT, zwdT, x_ap, y_ap, tag, out_scale=0.5):
    po_sets = B.get("po_sets") or [[B["pg"][0], B["pg"][1], B["pu"][0], B["pu"][1]]]
    nset = len(po_sets)
    nwd = len(B["wdT_sb"])
    WDQ = ("sync", "gpsimd", "scalar")
    gidx = 0
    for q in range(4):
        if q > 0:
            sc.wait("sync", "st", B.get(f"{tag}_amm_{q-1}_63", 0))
        nxq = sc.dma("sync", "xq", B["xlnTq"][:].rearrange("p (hc c) -> p hc c", c=512),
                     zxlnT[q].rearrange("hc p c -> p hc c"))

        for ib in range(64):
            db = ib % 2
            w_prev = B.get(f"{tag}_amm_{q}_{ib-2}", B.get(f"{tag}_amm_{q-1}_{62+ib}", 0))
            if w_prev:
                sc.wait("sync", "st", w_prev)
                sc.wait("gpsimd", "st", w_prev)
            n1 = sc.dma("sync", f"wg{db}", B["wT_sb"][0][db][:], zwgT[ib, :, :])
            n2 = sc.dma("gpsimd", f"wu{db}", B["wT_sb"][1][db][:], zwuT[ib, :, :])
            sc.wait("tensor", f"wg{db}", n1)
            sc.wait("tensor", f"wu{db}", n2)
            sc.wait("tensor", "xq", nxq)
            if ib < 2 and q >= 1:
                for sblk in range(4):
                    sc.wait("tensor", "sv", B.get(f"{tag}_bev_{q-1}_{4-nset}_{sblk}", 0))
            sc.wait("tensor", "sa", B.get(f"{tag}_asil_{q}_{ib-2}", B.get(f"{tag}_asil_{q-1}_{62+ib}", 0)))
            sc.wait("tensor", "sv", B.get(f"{tag}_amul_{q}_{ib-2}", B.get(f"{tag}_amul_{q-1}_{62+ib}", 0)))
            for hc in range(16):
                sc.op("tensor", lambda e, db=db, hc=hc: e.matmul(
                    B["pg"][db][:], B["wT_sb"][0][db][:, hc * 128:(hc + 1) * 128],
                    B["xlnTq"][:, hc * 512:(hc + 1) * 512],
                    start=(hc == 0), stop=(hc == 15)), "st" if hc == 15 else None)
            ng = sc.cur("st")
            for hc in range(16):
                sc.op("tensor", lambda e, db=db, hc=hc: e.matmul(
                    B["pu"][db][:], B["wT_sb"][1][db][:, hc * 128:(hc + 1) * 128],
                    B["xlnTq"][:, hc * 512:(hc + 1) * 512],
                    start=(hc == 0), stop=(hc == 15)), "st" if hc == 15 else None)
            nu = sc.cur("st")
            B[f"{tag}_amm_{q}_{ib}"] = nu
            sc.wait("scalar", "st", ng)
            sc.wait("scalar", "sv", B.get(f"{tag}_amul_{q}_{ib-2}", B.get(f"{tag}_amul_{q-1}_{62+ib}", 0)))
            nsil = sc.op("scalar", lambda e, db=db: e.activation(
                B["sact"][db][:], B["pg"][db][:], AF.Silu), "sa")
            B[f"{tag}_asil_{q}_{ib}"] = nsil
            sc.wait("vector", "st", nu)
            sc.wait("vector", "sa", nsil)
            nmul = sc.op("vector", lambda e, db=db, ib=ib: e.tensor_mul(
                B["actT"][:, ib * 512:(ib + 1) * 512], B["sact"][db][:], B["pu"][db][:]), "sv")
            B[f"{tag}_amul_{q}_{ib}"] = nmul

        for hhs in range(4):
            ps = po_sets[hhs % nset]
            for sblk in range(4):
                prev_ev = B.get(f"{tag}_xru_{gidx-1}", 0)
                if prev_ev:
                    sc.wait("scalar", "sv", prev_ev)
                sc.dma("scalar", "xres", B["xres"][sblk][:],
                       x_ap[q * 512 + sblk * 128: q * 512 + (sblk + 1) * 128,
                            hhs * 512:(hhs + 1) * 512])
            nxres = sc.cur("xres")
            for ic in range(64):
                db = ic % nwd
                qeng = WDQ[db % 3]
                wfree = B.get(f"{tag}_bmm_{q}_{hhs}_{ic-nwd}",
                              B.get(f"{tag}_bmm_{q}_{hhs-1}_{64-nwd+ic}",
                                    B.get(f"{tag}_bmm_{q-1}_3_{64-nwd+ic}", 0)))
                if wfree:
                    sc.wait(qeng, "st", wfree)
                ndw = sc.dma(qeng, f"wdt{db}", B["wdT_sb"][db][:], zwdT[ic, hhs, :, :])
                sc.wait("tensor", f"wdt{db}", ndw)
                sc.wait("tensor", "sv", B.get(f"{tag}_amul_{q}_{ic}", 0))
                if ic == 0 and hhs == 0:
                    sc.wait("tensor", "sa", B.get(f"{tag}_asil_{q}_{63}", 0))
                    sc.wait("tensor", "sv", B.get(f"{tag}_amul_{q}_{63}", 0))
                if ic == 0:
                    if hhs >= nset:
                        for sblk in range(4):
                            sc.wait("tensor", "sv", B.get(f"{tag}_bev_{q}_{hhs-nset}_{sblk}", 0))
                    elif hhs >= 1:
                        for sblk in range(4):
                            sc.wait("tensor", "sv", B.get(f"{tag}_bev_{q-1}_{4-nset+hhs}_{sblk}", 0))
                for sblk in range(4):
                    sc.op("tensor", lambda e, db=db, ic=ic, sblk=sblk, ps=ps: e.matmul(
                        ps[sblk][:], B["actT"][:, ic * 512 + sblk * 128: ic * 512 + (sblk + 1) * 128],
                        B["wdT_sb"][db][:], start=(ic == 0), stop=(ic == 63)),
                        "st" if sblk == 3 else None)
                B[f"{tag}_bmm_{q}_{hhs}_{ic}"] = sc.cur("st")
            nlast = sc.cur("st")
            for sblk in range(4):
                sc.wait("vector", "st", nlast)
                sc.wait("vector", "xres", nxres)
                if gidx > 0:
                    sc.wait("vector", "yqd", 16 * 4 * gidx)
                nev = sc.op("vector", lambda e, sblk=sblk, ps=ps: e.scalar_tensor_tensor(
                    B["yq"][sblk][:], ps[sblk][:], out_scale, B["xres"][sblk][:],
                    ALU.mult, ALU.add), "sv")
                B[f"{tag}_bev_{q}_{hhs}_{sblk}"] = nev
                sc.wait("gpsimd", "sv", nev)
                sc.dma("gpsimd", "yqd", y_ap[q * 512 + sblk * 128: q * 512 + (sblk + 1) * 128,
                                             hhs * 512:(hhs + 1) * 512], B["yq"][sblk][:])
            B[f"{tag}_xru_{gidx}"] = sc.cur("sv")
            gidx += 1



def build_ffn0():
    nc = bass.Bass("TRN2", target_bir_lowering=False, detect_race_conditions=False)
    B = {}
    B["in_x"] = nc.dram_tensor("x", [S, H], F32, kind="ExternalInput").ap()
    B["in_gw"] = nc.dram_tensor("gw", [II, H], I32, kind="ExternalInput").ap()
    B["in_uw"] = nc.dram_tensor("uw", [II, H], I32, kind="ExternalInput").ap()
    B["in_dw"] = nc.dram_tensor("dw", [H, II], I32, kind="ExternalInput").ap()
    B["in_gs"] = nc.dram_tensor("gs", [II], F32, kind="ExternalInput").ap()
    B["in_us"] = nc.dram_tensor("us", [II], F32, kind="ExternalInput").ap()
    B["in_ds"] = nc.dram_tensor("ds", [H], F32, kind="ExternalInput").ap()
    B["in_idf"] = nc.dram_tensor("idf", [128, 128], F32, kind="ExternalInput").ap()
    B["out_y"] = nc.dram_tensor("y", [S, H], F32, kind="ExternalOutput").ap()
    zxlnT = nc.dram_tensor("zxlnT", [4, 16, 128, 512], BF16, kind="Internal").ap()
    zwgT = nc.dram_tensor("zwgT", [64, 128, H], BF16, kind="Internal").ap()
    zwuT = nc.dram_tensor("zwuT", [64, 128, H], BF16, kind="Internal").ap()
    zwdT = nc.dram_tensor("zwdT", [64, 4, 128, 512], BF16, kind="Internal").ap()

    with ExitStack() as es:
        E = es.enter_context
        sb = lambda n, shp, dt: E(nc.sbuf_tensor(n, shp, dt))
        B["identf"] = sb("identf", [128, 128], F32)
        B["identb"] = sb("identb", [128, 128], BF16)
        B["stmp"] = sb("stmp", [128, 128], F32)
        B["gsT"] = sb("gsT", [128, 64], F32)
        B["usT"] = sb("usT", [128, 64], F32)
        B["dsT"] = sb("dsT", [128, 16], F32)
        B["xin"] = [sb(f"xin{i}", [128, H], F32) for i in range(2)]
        B["sqtmp"] = sb("sqtmp", [128, H], BF16)
        B["xlnb"] = sb("xlnb", [128, H], BF16)
        B["sT"] = [sb(f"sT{i}", [128, H], BF16) for i in range(2)]
        for nm in ("st_m", "st_s", "st_v", "st_t", "st_r"):
            B[nm] = sb(nm, [128, 1], F32)
        B["wgu_int"] = [sb(f"wgu{i}", [128, H], I32) for i in range(2)]
        B["wdq"] = [sb(f"wdq{i}", [128, H], BF16) for i in range(2)]
        B["wd_int"] = sb("wd_int", [128, II // 2], I32)
        B["wdstg"] = sb("wdstg", [128, II], BF16)
        B["xlnTq"] = sb("xlnTq", [128, 16 * 512], BF16)
        B["wT_sb"] = [[sb(f"wT{m}{i}", [128, H], BF16) for i in range(2)] for m in range(2)]
        B["actT"] = sb("actT", [128, 64 * 512], BF16)
        B["wdT_sb"] = [sb(f"wdT{i}", [128, 512], BF16) for i in range(2)]
        B["xres"] = [sb(f"xres{i}", [128, 512], F32) for i in range(4)]
        B["yq"] = [sb(f"yq{i}", [128, 512], F32) for i in range(4)]
        B["sact"] = [sb(f"sact{i}", [128, 512], BF16) for i in range(2)]
        B["ptr"] = [E(nc.psum_tensor(f"ptr{i}", [128, 128], BF16)) for i in range(2)]
        B["pscl"] = E(nc.psum_tensor("pscl", [128, 64], F32))
        B["pg"] = [E(nc.psum_tensor(f"pg{i}", [128, 512], F32)) for i in range(2)]
        B["pu"] = [E(nc.psum_tensor(f"pu{i}", [128, 512], F32)) for i in range(2)]
        B["_ptr_evac"] = [0, 0]
        B["_zs_last"] = [0, 0]

        sc = Sched()
        emit_const_prep(sc, B, [("in_gs", "gsT", 64), ("in_us", "usT", 64), ("in_ds", "dsT", 16)])
        emit_ln_pass(sc, B, B["in_x"], zxlnT, "ln0")
        emit_ffn_weight_prep_int(sc, B, zwgT, zwuT, zwdT)
        for sem in ("zs0", "zs1", "zd"):
            sc.wait("sync", sem, sc.cur(sem))
        emit_ffn_matmuls(sc, B, zxlnT, zwgT, zwuT, zwdT, B["in_x"], B["out_y"], "f0")
        for sem in sorted(sc.c.keys()):
            sc.wait("sync", sem, sc.cur(sem))
        sems = {name: E(nc.semaphore(name)) for name in sorted(sc.c.keys())}
        block = E(nc.Block())
        sc.emit(block, sems)
    return nc


_CACHE = {}

def get_nc(name, builder):
    if name not in _CACHE:
        _CACHE[name] = builder()
    return _CACHE[name]


def run_ffn0(x_full, gw, uw, dw, gs, us, ds, cores=None):
    nc = get_nc("ffn0", build_ffn0)
    idf = np.eye(128, dtype=np.float32)
    cores = cores or list(range(NCORES))
    in_maps = []
    for c in cores:
        in_maps.append(dict(x=np.ascontiguousarray(x_full[c]), gw=gw, uw=uw, dw=dw,
                            gs=gs, us=us, ds=ds, idf=idf))
    res = run_bass_kernel_spmd(nc, in_maps, core_ids=cores)
    return np.stack([res.results[i]["y"] for i in range(len(cores))], axis=0)



def emit_ln_sbuf(sc, B, x_ap, dstbuf, tag):
    nb = B.get("xin_nbuf", 2)
    for st16 in range(16):
        db = st16 % nb
        xin = B["xin"][db]
        if st16 >= nb:
            sc.wait("sync", "sv", B.get(f"{tag}_lnv_{st16-nb}", 0))
            sc.wait("sync", "sa", B.get(f"{tag}_lna_{st16-nb}", 0))
        n_dma = sc.dma("sync", f"ldx{db}", xin[:], x_ap[st16 * 128:(st16 + 1) * 128, :])
        sc.wait("scalar", f"ldx{db}", n_dma)
        if st16 >= 1:
            sc.wait("scalar", "sv", B.get(f"{tag}_lnss_{st16-1}", 0))
        sqdst = B["sqtmp"] if "sqtmp" in B else B["xlnb"]
        if "sqtmp" not in B:
            sc.wait("scalar", "st", B.get(f"{tag}_lntr_{st16-1}", 0))
        na = sc.op("scalar", lambda e, xin=xin, sqdst=sqdst: e.activation(
            sqdst[:], xin[:], AF.Square, accum_out=B["st_s"][:]), "sa")
        B[f"{tag}_lna_{st16}"] = na
        sc.wait("vector", f"ldx{db}", n_dma)
        sc.op("vector", lambda e, xin=xin: e.reduce_sum(B["st_m"][:], xin[:], AX.X), "sv")
        sc.wait("vector", "sa", na)
        B[f"{tag}_lnss_{st16}"] = sc.op("vector", lambda e: e.tensor_scalar(
            B["st_v"][:], B["st_s"][:], 1.0 / H, None, ALU.mult), "sv")
        n_ = sc.op("vector", lambda e: e.tensor_scalar(B["st_m"][:], B["st_m"][:], 1.0 / H, None, ALU.mult), "sv")
        sc.wait("vector", "sv", n_)
        n_ = sc.op("vector", lambda e: e.tensor_mul(B["st_t"][:], B["st_m"][:], B["st_m"][:]), "sv")
        sc.wait("vector", "sv", n_)
        n_ = sc.op("vector", lambda e: e.tensor_sub(B["st_v"][:], B["st_v"][:], B["st_t"][:]), "sv")
        sc.wait("vector", "sv", n_)
        nv = sc.op("vector", lambda e: e.tensor_scalar(B["st_v"][:], B["st_v"][:], 1e-5, None, ALU.add), "sv")
        sc.wait("scalar", "sv", nv)
        sc.op("scalar", lambda e: e.activation(B["st_t"][:], B["st_v"][:], AF.Ln), "sa")
        nr = sc.op("scalar", lambda e: e.activation(B["st_r"][:], B["st_t"][:], AF.Exp, scale=-0.5), "sa")
        sc.wait("vector", "sa", nr)
        nl = sc.op("vector", lambda e, xin=xin: e.tensor_scalar(
            B["xlnb"][:], xin[:], B["st_m"][:, 0:1], B["st_r"][:, 0:1], ALU.subtract, ALU.mult), "sv")
        B[f"{tag}_lnv_{st16}"] = nl
        sc.wait("tensor", "sv", nl)
        t_row(sc, B, lambda k: B["xlnb"][:, k * 128:(k + 1) * 128], 16,
              lambda k, st16=st16: dstbuf[:, k * 2048 + st16 * 128:k * 2048 + (st16 + 1) * 128],
              B["identb"][:])
        B[f"{tag}_lntr_{st16}"] = sc.cur("st")
    return sc.cur("sa")


def emit_attn0(sc, B, nc, debug=False):
    n_id = sc.dma("sync", "idm", B["identf"][:], B["in_idf"])
    n_lam = sc.dma("sync", "idm", B["lamB"][:], B["in_lam"])
    n_msk = sc.dma("sync", "idm", B["mskb"][:], B["in_msk"])
    sc.wait("vector", "idm", n_msk)
    sc.op("vector", lambda e: e.tensor_copy(B["identb"][:], B["identf"][:]), "sv")

    ln_tail_sa = emit_ln_sbuf(sc, B, B["in_x"], B["bigT"], "lna")
    sc.wait("tensor", "sa", ln_tail_sa)

    qkv_jobs = [("q", B["in_wq"], 4, True, "cq", "sq"),
                ("k", B["in_wk"], 1, True, "ck", "sk"),
                ("v", B["in_wv"], 1, False, None, None)]
    for jname, wap, nslab, rope, ctab, stab in qkv_jobs:
        for qds in range(nslab):
            prev = B.get("w_consumed", 0)
            if prev:
                sc.wait("gpsimd", "st", prev)
            nw = sc.dma("gpsimd", "wld", B["wsb"][:].rearrange("p (hc c) -> p hc c", c=512),
                        wap.rearrange("(hc p) c -> p hc c", p=128)[:, :, qds * 512:(qds + 1) * 512])
            for st16 in range(16):
                db = st16 % 2
                pq = B["pqkv"][db]
                if rope:
                    nrp = B.get("rope_used", 0)
                    if nrp:
                        sc.wait("sync", "sv", nrp)
                    nct = sc.dma("sync", "rpt", B["ropc"][:], B[f"in_{ctab}"][st16, :, :])
                    nst = sc.dma("sync", "rpt", B["rops"][:], B[f"in_{stab}"][st16, :, :])
                sc.wait("tensor", "wld", nw)
                prev_ev = B.get(f"qkv_ev_{jname}_{qds}_{st16-2}",
                                B.get(f"qkv_ev_{jname}_{qds-1}_{14+st16}", B.get("qkv_prev_tail", 0)))
                if prev_ev:
                    sc.wait("tensor", "sv", prev_ev)
                for hc in range(16):
                    sc.op("tensor", lambda e, pq=pq, hc=hc, st16=st16: e.matmul(
                        pq[:], B["bigT"][:, hc * 2048 + st16 * 128:hc * 2048 + (st16 + 1) * 128],
                        B["wsb"][:, hc * 512:(hc + 1) * 512],
                        start=(hc == 0), stop=(hc == 15)), "st" if hc == 15 else None)
                nmm = sc.cur("st")
                B[f"qkv_mm_{jname}_{qds}_{st16}"] = nmm
                sc.wait("vector", "st", nmm)
                if rope:
                    sc.wait("vector", "rpt", nst)
                    sc.op("vector", lambda e, pq=pq: e.tensor_mul(B["qa"][:], pq[:], B["ropc"][:]), "sv")
                    pqv = pq[:].rearrange("p (h d) -> p h d", d=128)
                    qbv = B["qb"][:].rearrange("p (h d) -> p h d", d=128)
                    snv = B["rops"][:].rearrange("p (h d) -> p h d", d=128)
                    sc.op("vector", lambda e, pqv=pqv, qbv=qbv, snv=snv: e.tensor_mul(
                        qbv[:, :, 0:64], pqv[:, :, 64:128], snv[:, :, 0:64]), "sv")
                    nrope = sc.op("vector", lambda e, pqv=pqv, qbv=qbv, snv=snv: e.tensor_mul(
                        qbv[:, :, 64:128], pqv[:, :, 0:64], snv[:, :, 64:128]), "sv")
                    B["rope_used"] = nrope
                    nqr = sc.op("vector", lambda e: e.tensor_add(B["qr"][:], B["qa"][:], B["qb"][:]), "sv")
                    B[f"qkv_ev_{jname}_{qds}_{st16}"] = nqr
                    sc.wait("tensor", "sv", nqr)
                    if jname == "q":
                        t_row(sc, B, lambda kk: B["qr"][:, kk * 128:(kk + 1) * 128], 4,
                              lambda kk, qds=qds, st16=st16: B["qT"][
                                  :, (qds * 4 + kk) * 2048 + st16 * 128:(qds * 4 + kk) * 2048 + (st16 + 1) * 128],
                              B["identb"][:])
                    else:
                        t_row(sc, B, lambda kk: B["qr"][:, kk * 128:(kk + 1) * 128], 4,
                              lambda kk, st16=st16: B["kT"][
                                  :, kk * 2048 + st16 * 128:kk * 2048 + (st16 + 1) * 128],
                              B["identb"][:])
                else:
                    for kvh in range(4):
                        nev = sc.op("vector", lambda e, pq=pq, kvh=kvh, st16=st16: e.tensor_copy(
                            B["vP"][:, kvh * 2048 + st16 * 128:kvh * 2048 + (st16 + 1) * 128],
                            pq[:, kvh * 128:(kvh + 1) * 128]), "sv")
                    B[f"qkv_ev_{jname}_{qds}_{st16}"] = nev
            B["w_consumed"] = B[f"qkv_mm_{jname}_{qds}_{15}"]
        B["qkv_prev_tail"] = max(B.get(f"qkv_ev_{jname}_{nslab-1}_{14}", 0),
                                 B.get(f"qkv_ev_{jname}_{nslab-1}_{15}", 0))

    qkv_tail_sv = sc.cur("sv")
    qkv_tail_sa = sc.cur("sa")

    chains = [dict(psc=B["psc"], pav=B["pav"], zsb=B["zsb"], esb=B["esb"],
                   dif=B["dif"], aout=B["aout"], sm=B["sm"][0:8], ci=0)]
    if "psc2" in B:
        chains.append(dict(psc=B["psc2"], pav=B["pav2"], zsb=B["zsb2"], esb=B["esb2"],
                           dif=B["dif2"], aout=B["aout2"], sm=B["sm"][8:16], ci=1))
    nch = len(chains)

    def seg1(CB, h, qb, first):
        ci = CB["ci"]
        kvh = h // 4
        c0 = max(0, 4 - qb)
        for half in range(2):
            base = half * 64
            pv = B.get(f"at_z_prev_{ci}", 0)
            if first and half == 0:
                sc.wait("tensor", "sv", qkv_tail_sv)
                sc.wait("tensor", "sa", qkv_tail_sa)
            if pv:
                sc.wait("tensor", "sv", pv)
            for c in range(c0, 5):
                sc.op("tensor", lambda e, base=base, h=h, qb=qb, kvh=kvh, c=c, CB=CB: e.matmul(
                    CB["psc"][:, c * 128:(c + 1) * 128],
                    B["qT"][base:base + 64, h * 2048 + qb * 128:h * 2048 + (qb + 1) * 128],
                    B["kT"][base:base + 64, kvh * 2048 + (qb + c - 4) * 128:kvh * 2048 + (qb + c - 3) * 128],
                    start=True, stop=True), "st" if c == 4 else None)
            nsc = sc.cur("st")
            sc.wait("vector", "st", nsc)
            sc.wait("vector", "sa", B.get(f"at_exp_prev_{ci}", 0))
            if c0 > 0:
                sc.op("vector", lambda e, c0=c0, CB=CB: e.memset(
                    CB["psc"][:, 0:c0 * 128], -1e30), "sv")
            nz = sc.op("vector", lambda e, CB=CB: e.tensor_add(
                CB["zsb"][:], CB["psc"][:], B["mskb"][:]), "sv")
            B[f"at_z_prev_{ci}"] = nz
            sc.wait("vector", "sv", nz)
            sc.op("vector", lambda e, half=half, CB=CB: e.reduce_max(
                CB["sm"][half][:], CB["zsb"][:], AX.X, negate=True), "sv")
            nmx = sc.cur("sv")
            sc.wait("scalar", "sv", nmx)
            ne = sc.op("scalar", lambda e, half=half, CB=CB: e.activation(
                CB["esb"][half][:], CB["zsb"][:], AF.Exp,
                bias=CB["sm"][half][:, 0:1], accum_out=CB["sm"][2 + half][:]), "sa")
            B[f"at_exp_prev_{ci}"] = ne

    def seg2(CB, h):
        ci = CB["ci"]
        sm = CB["sm"]
        sc.wait("vector", "sa", B[f"at_exp_prev_{ci}"])
        n_ = sc.op("vector", lambda e, sm=sm: e.reciprocal(sm[4][:], sm[2][:]), "sv")
        n_ = sc.op("vector", lambda e, sm=sm: e.reciprocal(sm[5][:], sm[3][:]), "sv")
        sc.wait("vector", "sv", n_)
        n_ = sc.op("vector", lambda e, h=h, sm=sm: e.tensor_scalar(
            sm[5][:], sm[5][:], B["lamB"][:, h:h + 1], None, ALU.mult), "sv")
        sc.wait("vector", "sv", n_)
        n_ = sc.op("vector", lambda e, CB=CB, sm=sm: e.tensor_scalar(
            CB["esb"][1][:], CB["esb"][1][:], sm[5][:, 0:1], None, ALU.mult), "sv")
        sc.wait("vector", "sv", n_)
        n_ = sc.op("vector", lambda e, CB=CB, sm=sm: e.scalar_tensor_tensor(
            CB["dif"][:], CB["esb"][0][:], sm[4][:, 0:1], CB["esb"][1][:],
            ALU.mult, ALU.subtract), "sv")
        sc.wait("vector", "sv", n_)
        n_ = sc.op("vector", lambda e, CB=CB: e.tensor_scalar(
            CB["dif"][:], CB["dif"][:], 0.0, None, ALU.max), "sv")
        sc.wait("vector", "sv", n_)
        n_ = sc.op("vector", lambda e, CB=CB, sm=sm: e.reduce_sum(
            sm[6][:], CB["dif"][:], AX.X), "sv")
        sc.wait("vector", "sv", n_)
        n_ = sc.op("vector", lambda e, sm=sm: e.tensor_scalar(
            sm[6][:], sm[6][:], 1e-6, None, ALU.add), "sv")
        sc.wait("vector", "sv", n_)
        ndif = sc.op("vector", lambda e, sm=sm: e.reciprocal(sm[7][:], sm[6][:]), "sv")
        B[f"at_dif_{ci}"] = ndif

    def seg3(CB, h, qb):
        ci = CB["ci"]
        kvh = h // 4
        c0 = max(0, 4 - qb)
        sc.wait("tensor", "sv", B[f"at_dif_{ci}"])
        ndt = t_row(sc, B, lambda kk, c0=c0, CB=CB: CB["dif"][:, (c0 + kk) * 128:(c0 + kk + 1) * 128],
                    5 - c0,
                    lambda kk, c0=c0, CB=CB: CB["dif"][:, (c0 + kk) * 128:(c0 + kk + 1) * 128],
                    B["identb"][:])
        sc.wait("tensor", "sa", ndt)
        pvav = B.get(f"at_av_prev_{ci}", 0)
        if pvav:
            sc.wait("tensor", "sv", pvav)
        for c in range(c0, 5):
            sc.op("tensor", lambda e, c=c, kvh=kvh, qb=qb, c0=c0, CB=CB: e.matmul(
                CB["pav"][:], CB["dif"][:, c * 128:(c + 1) * 128],
                B["vP"][:, kvh * 2048 + (qb + c - 4) * 128:kvh * 2048 + (qb + c - 3) * 128],
                start=(c == c0), stop=(c == 4)), "st" if c == 4 else None)
        nav = sc.cur("st")
        sc.wait("vector", "st", nav)
        nao = sc.op("vector", lambda e, CB=CB: e.tensor_scalar(
            CB["aout"][:], CB["pav"][:], CB["sm"][7][:, 0:1], None, ALU.mult), "sv")
        B[f"at_av_prev_{ci}"] = nao
        sc.wait("tensor", "sv", nao)
        t_row(sc, B, lambda kk, CB=CB: CB["aout"][:], 1,
              lambda kk, h=h, qb=qb: B["bigT"][:, h * 2048 + qb * 128:h * 2048 + (qb + 1) * 128],
              B["identb"][:])

    tiles = [(h, qb) for h in range(16) for qb in range(16)]
    i = 0
    while i < len(tiles):
        grp = [(chains[j], tiles[i + j]) for j in range(min(nch, len(tiles) - i))]
        for (CB, (h, qb)) in grp:
            seg1(CB, h, qb, first=(i == 0 and CB["ci"] == 0))
        for (CB, (h, qb)) in grp:
            seg2(CB, h)
        for (CB, (h, qb)) in grp:
            seg3(CB, h, qb)
        i += len(grp)

    attn_tail_sa = sc.cur("sa")

    sc.wait("sync", "sv", sc.cur("sv"))
    sc.wait("sync", "sa", sc.cur("sa"))
    sc.wait("gpsimd", "sv", sc.cur("sv"))
    sc.wait("gpsimd", "sa", sc.cur("sa"))
    sc.wait("gpsimd", "st", sc.cur("st"))
    for hs in range(4):
        prev = B.get("wo_consumed", B.get("w_consumed", 0))
        if prev:
            sc.wait("gpsimd", "st", prev)
        nw = sc.dma("gpsimd", "wld", B["wsb"][:].rearrange("p (hc c) -> p hc c", c=512),
                    B["in_wo"].rearrange("(hc p) c -> p hc c", p=128)[:, :, hs * 512:(hs + 1) * 512])
        for st16 in range(16):
            db = st16 % 2
            pq = B["pqkv"][db]
            prev_ev = B.get(f"wo_ev_{hs}_{st16-1}", B.get(f"wo_ev_{hs-1}_{15}", 0))
            if prev_ev:
                sc.wait("sync", "sv", prev_ev)
            nxr = sc.dma("sync", "xres", B["xres"][0][:],
                         B["in_x"][st16 * 128:(st16 + 1) * 128, hs * 512:(hs + 1) * 512])
            sc.wait("tensor", "wld", nw)
            if st16 < 2 and hs == 0:
                sc.wait("tensor", "sa", attn_tail_sa)
            pv2 = B.get(f"wo_ev_{hs}_{st16-2}", B.get(f"wo_ev_{hs-1}_{14+st16}", 0))
            if pv2:
                sc.wait("tensor", "sv", pv2)
            for hc in range(16):
                sc.op("tensor", lambda e, pq=pq, hc=hc, st16=st16: e.matmul(
                    pq[:], B["bigT"][:, hc * 2048 + st16 * 128:hc * 2048 + (st16 + 1) * 128],
                    B["wsb"][:, hc * 512:(hc + 1) * 512],
                    start=(hc == 0), stop=(hc == 15)), "st" if hc == 15 else None)
            nmm = sc.cur("st")
            B[f"wo_mm_{hs}_{st16}"] = nmm
            sc.wait("vector", "st", nmm)
            sc.wait("vector", "xres", nxr)
            sc.wait("vector", "yqd", sc.cur("yqd"))
            nev = sc.op("vector", lambda e, pq=pq: e.tensor_add(
                B["yq"][0][:], pq[:], B["xres"][0][:]), "sv")
            B[f"wo_ev_{hs}_{st16}"] = nev
            sc.wait("gpsimd", "sv", nev)
            sc.dma("gpsimd", "yqd", B["out_y"][st16 * 128:(st16 + 1) * 128, hs * 512:(hs + 1) * 512],
                   B["yq"][0][:])
        B["wo_consumed"] = B[f"wo_mm_{hs}_{15}"]



def build_attn0(debug=False):
    nc = bass.Bass("TRN2", target_bir_lowering=False, detect_race_conditions=False)
    B = {}
    B["in_x"] = nc.dram_tensor("x", [S, H], F32, kind="ExternalInput").ap()
    B["in_wq"] = nc.dram_tensor("wq", [H, 2048], F32, kind="ExternalInput").ap()
    B["in_wk"] = nc.dram_tensor("wk", [H, 512], F32, kind="ExternalInput").ap()
    B["in_wv"] = nc.dram_tensor("wv", [H, 512], F32, kind="ExternalInput").ap()
    B["in_wo"] = nc.dram_tensor("wo", [2048, H], F32, kind="ExternalInput").ap()
    B["in_lam"] = nc.dram_tensor("lamb", [128, 16], F32, kind="ExternalInput").ap()
    B["in_idf"] = nc.dram_tensor("idf", [128, 128], F32, kind="ExternalInput").ap()
    B["in_cq"] = nc.dram_tensor("cq", [16, 128, 512], BF16, kind="ExternalInput").ap()
    B["in_sq"] = nc.dram_tensor("sq", [16, 128, 512], BF16, kind="ExternalInput").ap()
    B["in_ck"] = nc.dram_tensor("ck", [16, 128, 512], BF16, kind="ExternalInput").ap()
    B["in_sk"] = nc.dram_tensor("sk", [16, 128, 512], BF16, kind="ExternalInput").ap()
    B["in_msk"] = nc.dram_tensor("msk", [128, 640], BF16, kind="ExternalInput").ap()
    B["out_y"] = nc.dram_tensor("y", [S, H], F32, kind="ExternalOutput").ap()

    with ExitStack() as es:
        E = es.enter_context
        sb = lambda n, shp, dt: E(nc.sbuf_tensor(n, shp, dt))
        B["identf"] = sb("identf", [128, 128], F32)
        B["identb"] = sb("identb", [128, 128], BF16)
        _xin = sb("xin0", [128, H], F32)
        B["xin"] = [_xin]
        B["xin_nbuf"] = 1
        B["xlnb"] = sb("xlnb", [128, H], BF16)
        for nm in ("st_m", "st_s", "st_v", "st_t", "st_r"):
            B[nm] = sb(nm, [128, 1], F32)
        B["bigT"] = sb("bigT", [128, 16 * 2048], BF16)
        B["qT"] = sb("qT", [128, 16 * 2048], BF16)
        B["kT"] = sb("kT", [128, 4 * 2048], BF16)
        B["vP"] = sb("vP", [128, 4 * 2048], BF16)
        B["wsb"] = sb("wsb", [128, 16 * 512], BF16)
        B["ropc"] = sb("ropc", [128, 512], BF16)
        B["rops"] = sb("rops", [128, 512], BF16)
        B["qa"] = sb("qa", [128, 512], BF16)
        B["qb"] = sb("qb", [128, 512], BF16)
        B["qr"] = sb("qr", [128, 512], BF16)
        B["lamB"] = sb("lamB", [128, 16], F32)
        B["mskb"] = sb("mskb", [128, 640], BF16)
        B["zsb"] = sb("zsb", [128, 640], F32)
        B["esb"] = [sb(f"esb{i}", [128, 640], BF16) for i in range(2)]

        B["dif"] = sb("dif", [128, 640], BF16)
        B["aout"] = sb("aout", [128, 128], BF16)
        B["sm"] = [sb(f"sm{i}", [128, 1], F32) for i in range(8)]
        B["xres"] = [sb("xres0", [128, 512], F32)]
        B["yq"] = [sb("yq0", [128, 512], F32)]
        B["ptr"] = [E(nc.psum_tensor(f"ptr{i}", [128, 128], BF16)) for i in range(2)]
        B["pqkv"] = [E(nc.psum_tensor(f"pqkv{i}", [128, 512], F32)) for i in range(2)]
        B["psc"] = E(nc.psum_tensor("psc", [128, 640], F32))
        B["pav"] = E(nc.psum_tensor("pav", [128, 128], F32))
        B["_ptr_evac"] = [0, 0]
        B["_zs_last"] = [0, 0]

        sc = Sched()
        emit_attn0(sc, B, nc, debug=debug)
        if debug:
            dbg = {}
            for nm, buf, w in [("dqT", "qT", 32768), ("dkT", "kT", 8192),
                               ("dvP", "vP", 8192), ("dbigT", "bigT", 32768)]:
                dap = nc.dram_tensor(nm, [128, w], BF16, kind="ExternalOutput").ap()
                sc.dma("gpsimd", "yqd", dap, B[buf][:])
        for sem in sorted(sc.c.keys()):
            sc.wait("sync", sem, sc.cur(sem))
        sems = {name: E(nc.semaphore(name)) for name in sorted(sc.c.keys())}
        block = E(nc.Block())
        sc.emit(block, sems)
    return nc


def make_attn0_consts():
    hd = 128
    inv = 1.0 / (10000.0 ** (np.arange(0, hd, 2, dtype=np.float64) / hd))
    t = np.arange(S, dtype=np.float64)
    fr = np.outer(t, inv)
    emb = np.concatenate([fr, fr], axis=-1)
    cos = np.cos(emb).astype(np.float32)
    sin = np.sin(emb).astype(np.float32)
    sneg = sin.copy()
    sneg[:, 0:64] *= -1.0
    scale = 1.0 / np.sqrt(64.0)
    def tile(tab, sc_):
        tt = (tab * sc_).astype(np.float32)
        rep = np.tile(tt, (1, 4))
        return rep.reshape(16, 128, 512)
    import ml_dtypes
    cq = tile(cos, scale).astype(ml_dtypes.bfloat16); sq = tile(sneg, scale).astype(ml_dtypes.bfloat16)
    ck = tile(cos, 1.0).astype(ml_dtypes.bfloat16); sk = tile(sneg, 1.0).astype(ml_dtypes.bfloat16)
    r = np.arange(128)[:, None]
    c = np.arange(640)[None, :]
    allowed = (c >= r + 1) & (c <= r + 512)
    msk = np.where(allowed, 0.0, -1e30).astype(np.float32)
    return cq, sq, ck, sk, msk.astype(ml_dtypes.bfloat16)


def run_attn0(x_full, wq, wk, wv, wo, lam, cores=None):
    nc = get_nc("attn0", build_attn0)
    idf = np.eye(128, dtype=np.float32)
    cq, sq, ck, sk, msk = make_attn0_consts()
    lamB = np.tile(lam[None, :], (128, 1)).astype(np.float32)
    cores = cores or list(range(NCORES))
    in_maps = []
    for c in cores:
        in_maps.append(dict(x=np.ascontiguousarray(x_full[c]), wq=wq, wk=wk, wv=wv,
                            wo=wo, lamb=lamB, idf=idf, cq=cq, sq=sq, ck=ck, sk=sk, msk=msk))
    res = run_bass_kernel_spmd(nc, in_maps, core_ids=cores)
    return np.stack([res.results[i]["y"] for i in range(len(cores))], axis=0)



def emit_attn1(sc, B, nc, debug=False):
    n_id = sc.dma("sync", "idm", B["identf"][:], B["in_idf"])
    sc.wait("vector", "idm", n_id)
    sc.op("vector", lambda e: e.tensor_copy(B["identb"][:], B["identf"][:]), "sv")

    ln_tail_sa = emit_ln_sbuf(sc, B, B["in_x"], B["bigT"], "lnr")

    for rb in range(32):
        db = rb % 2
        if rb >= 2:
            sc.wait("gpsimd", "st", B.get(f"p1tr_{rb-2}", 0))
        nw = sc.dma("gpsimd", f"wxl{db}", B["bfp"][db][:],
                    B["in_wxy"][rb * 128:(rb + 1) * 128, :])
        stg = B["bfp"][2]
        if rb >= 1:
            sc.wait("tensor", f"zs0", B["_zs_last"][0])
        sc.wait("tensor", f"wxl{db}", nw)
        nev = t_row(sc, B, lambda k, db=db: B["bfp"][db][:, k * 128:(k + 1) * 128], 16,
                    lambda k, stg=stg: stg[:, k * 128:(k + 1) * 128], B["identb"][:])
        B[f"p1tr_{rb}"] = sc.cur("st")
        sc.wait("gpsimd", "sa", nev)
        B["_zs_last"][0] = sc.dma("gpsimd", "zs0", B["zwxyT"][rb, :, :], stg[:])
    for hb in range(16):
        db = hb % 2
        if hb >= 2:
            sc.wait("gpsimd", "st", B.get(f"p2tr_{hb-2}", 0))
        nw = sc.dma("gpsimd", f"wxl{db}", B["bfp"][db][:],
                    B["in_wo"][hb * 128:(hb + 1) * 128, :])
        stg = B["bfp"][2]
        sc.wait("tensor", "zs0", B["_zs_last"][0])
        sc.wait("tensor", f"wxl{db}", nw)
        nev = t_row(sc, B, lambda k, db=db: B["bfp"][db][:, k * 128:(k + 1) * 128], 16,
                    lambda k, stg=stg: stg[:, k * 128:(k + 1) * 128], B["identb"][:])
        B[f"p2tr_{hb}"] = sc.cur("st")
        sc.wait("gpsimd", "sa", nev)
        B["_zs_last"][0] = sc.dma("gpsimd", "zs0",
                                  B["zwoT"].rearrange("rc p c -> p rc c")[:, :, hb * 128:(hb + 1) * 128],
                                  stg[:].rearrange("p (rc c) -> p rc c", c=128))
    n1 = sc.dma("sync", "stt", B["fpp"][1][0:4, :], B["in_cw"])
    sc.wait("tensor", "stt", n1)
    sc.wait("tensor", "idm", n_id)
    prevev = 0
    for rc in range(16):
        sc.wait("tensor", "sv", prevev)
        sc.op("tensor", lambda e, rc=rc: e.transpose(
            B["psmall"][:, 0:4], B["fpp"][1][0:4, rc * 128:(rc + 1) * 128], B["identf"][0:4, 0:4]), "st")
        ntr = sc.cur("st")
        sc.wait("vector", "st", ntr)
        prevev = sc.op("vector", lambda e, rc=rc: e.tensor_copy(
            B["cwT"][:, rc * 4:(rc + 1) * 4], B["psmall"][:, 0:4]), "sv")
    for (name, src) in (("biT", B["in_bi"]), ("baT", B["in_ba"]),
                        ("apT", B["in_ap"].rearrange("(a b) -> a b", b=128)),
                        ("cbT", B["in_cb"].rearrange("(a b) -> a b", b=128))):
        sc.wait("sync", "st", sc.cur("st"))
        nt = sc.dma("sync", "stt", B["fpp"][1][0:16, 0:128], src)
        sc.wait("tensor", "stt", nt)
        sc.wait("tensor", "sv", prevev)
        sc.op("tensor", lambda e: e.transpose(
            B["psmall"][:, 0:16], B["fpp"][1][0:16, 0:128], B["identf"][0:16, 0:16]), "st")
        ntr = sc.cur("st")
        sc.wait("vector", "st", ntr)
        prevev = sc.op("vector", lambda e, name=name: e.tensor_copy(
            B[name][:], B["psmall"][:, 0:16]), "sv")
    sc.wait("scalar", "sv", prevev)
    na = sc.op("scalar", lambda e: e.activation(B["t16a"][:], B["apT"][:], AF.Abs), "sa")
    sc.wait("scalar", "sa", na)
    na = sc.op("scalar", lambda e: e.activation(B["t16a"][:], B["t16a"][:], AF.Exp, scale=-1.0), "sa")
    sc.wait("scalar", "sa", na)
    na = sc.op("scalar", lambda e: e.activation(B["t16a"][:], B["t16a"][:], AF.Ln, bias=1.0), "sa")
    sc.wait("vector", "sa", na)
    nv = sc.op("vector", lambda e: e.tensor_scalar(B["t16b"][:], B["apT"][:], 0.0, None, ALU.max), "sv")
    sc.wait("vector", "sv", nv)
    nv = sc.op("vector", lambda e: e.tensor_add(B["t16b"][:], B["t16b"][:], B["t16a"][:]), "sv")
    sc.wait("vector", "sv", nv)
    nv = sc.op("vector", lambda e: e.tensor_scalar(B["m8T"][:], B["t16b"][:], -8.0, None, ALU.mult), "sv")
    m8_done = nv

    prep_st = sc.cur("st")
    sc.wait("tensor", "sa", ln_tail_sa)
    for rb in range(32):
        db = rb % 2
        wprev = B.get(f"xy_mm_{rb-2}", prep_st)
        if wprev:
            sc.wait("gpsimd", "st", wprev)
        nw = sc.dma("gpsimd", f"wxl{db}", B["bfp"][db][:], B["zwxyT"][rb, :, :])
        sc.wait("tensor", f"wxl{db}", nw)
        for ss in range(4):
            pdb = B["pxy"][ss % 2]
            pv = B.get(f"xy_ev_{rb}_{ss-2}", B.get(f"xy_ev_{rb-1}_{2+ss}", 0))
            if pv:
                sc.wait("tensor", "sv", pv)
            for hc in range(16):
                sc.op("tensor", lambda e, pdb=pdb, db=db, hc=hc, ss=ss: e.matmul(
                    pdb[:], B["bfp"][db][:, hc * 128:(hc + 1) * 128],
                    B["bigT"][:, hc * 2048 + ss * 512:hc * 2048 + (ss + 1) * 512],
                    start=(hc == 0), stop=(hc == 15)), "st" if hc == 15 else None)
            nmm = sc.cur("st")
            sc.wait("vector", "st", nmm)
            if rb < 16:
                if ss == 0:
                    sc.wait("vector", "xrz", B.get(f"xr_z_{rb-1}", 0))
                nev = sc.op("vector", lambda e, pdb=pdb, ss=ss: e.tensor_copy(
                    B["fpp"][1][:, ss * 512:(ss + 1) * 512], pdb[:]), "sv")
            else:
                nev = sc.op("vector", lambda e, pdb=pdb, rb=rb, ss=ss: e.tensor_copy(
                    B["yT"][:, (rb - 16) * 2048 + ss * 512:(rb - 16) * 2048 + (ss + 1) * 512],
                    pdb[:]), "sv")
            B[f"xy_ev_{rb}_{ss}"] = nev
        B[f"xy_mm_{rb}"] = sc.cur("st")
        if rb < 16:
            sc.wait("gpsimd", "sv", B[f"xy_ev_{rb}_{3}"])
            nz = sc.dma("gpsimd", "xrz", B["zxrT"][rb, :, :], B["fpp"][1][:])
            B[f"xr_z_{rb}"] = nz

    sc.wait("vector", "sv", m8_done)
    sc.wait("vector", "st", B.get("xy_mm_31", 0))
    sc.wait("sync", "xrz", sc.cur("xrz"))
    for rc in range(16):
        if rc == 0:
            nmem = sc.op("vector", lambda e: e.memset(B["xc"][:, 13:16], 0.0), "sv")
        else:
            sc.wait("sync", "sv", B[f"rg_conv_{rc-1}"])
        nxl = sc.dma("sync", "xrl", B["xc"][:, 16:2064], B["zxrT"][rc, :, :])
        sc.wait("vector", "xrl", nxl)
        if rc >= 1:
            sc.wait("vector", "st", B.get(f"rg_gmm_{rc-1}", 0))
        a_, b_ = B["fpp"][0], B["fpp"][2]
        n_ = sc.op("vector", lambda e, rc=rc, a_=a_: e.tensor_scalar(
            a_[:], B["xc"][:, 13:13 + 2048], B["cwT"][:, rc * 4:rc * 4 + 1],
            B["cbT"][:, rc:rc + 1], ALU.mult, ALU.add), "sv")
        for k in range(1, 4):
            sc.wait("vector", "sv", n_)
            src, dst = (a_, b_) if k % 2 == 1 else (b_, a_)
            n_ = sc.op("vector", lambda e, k=k, rc=rc, src=src, dst=dst: e.scalar_tensor_tensor(
                dst[:], B["xc"][:, 13 + k:13 + k + 2048], B["cwT"][:, rc * 4 + k:rc * 4 + k + 1],
                src[:], ALU.mult, ALU.add), "sv")
        B[f"rg_conv_{rc}"] = n_
        xcc = b_
        sc.wait("vector", "sv", n_)
        nconv = n_
        if rc >= 1:
            sc.wait("sync", "st", B.get(f"rg_gmm_{rc-1}", 0))
        nwl = sc.dma("sync", "wil", B["wgf"][:, 0:128], B["in_wi"][rc, :, :])
        nwl = sc.dma("sync", "wil", B["wgf"][:, 128:256], B["in_wa"][rc, :, :])
        sc.wait("tensor", "wil", nwl)
        sc.wait("tensor", "sv", nconv)
        for ss in range(4):
            if not (rc == 0 and ss == 0):
                sc.wait("tensor", "sa", B.get("rg_sig", 0))
            sc.op("tensor", lambda e, rc=rc, ss=ss, xcc=xcc: e.matmul(
                B["pg2"][0][:], B["wgf"][:, 0:128],
                xcc[:, ss * 512:(ss + 1) * 512], start=True, stop=True), None)
            sc.op("tensor", lambda e, rc=rc, ss=ss, xcc=xcc: e.matmul(
                B["pg2"][1][:], B["wgf"][:, 128:256],
                xcc[:, ss * 512:(ss + 1) * 512], start=True, stop=True), "st")
            ng = sc.cur("st")
            sc.wait("scalar", "st", ng)
            sc.op("scalar", lambda e, rc=rc, ss=ss: e.activation(
                B["bfp"][1][:, ss * 512:(ss + 1) * 512], B["pg2"][0][:], AF.Sigmoid,
                bias=B["biT"][:, rc:rc + 1]), "sa")
            B["rg_sig"] = sc.op("scalar", lambda e, rc=rc, ss=ss: e.activation(
                B["fpp"][1][:, ss * 512:(ss + 1) * 512], B["pg2"][1][:], AF.Sigmoid,
                bias=B["baT"][:, rc:rc + 1]), "sa")
        B[f"rg_gmm_{rc}"] = sc.cur("st")
        nsig = B["rg_sig"]
        sc.wait("vector", "sa", nsig)
        n_ = sc.op("vector", lambda e, rc=rc: e.tensor_scalar(
            B["fpp"][0][:], B["fpp"][1][:], B["m8T"][:, rc:rc + 1], None, ALU.mult), "sv")
        sc.wait("scalar", "sv", n_)
        na = sc.op("scalar", lambda e: e.activation(B["fpp"][1][:], B["fpp"][0][:], AF.Exp), "sa")
        na = sc.op("scalar", lambda e: e.activation(B["fpp"][3][:], B["fpp"][0][:], AF.Exp, scale=2.0), "sa")
        sc.wait("vector", "sa", na)
        n_ = sc.op("vector", lambda e: e.tensor_scalar(
            B["fpp"][0][:], B["fpp"][3][:], -1.0, 1.0, ALU.mult, ALU.add), "sv")
        sc.wait("scalar", "sv", n_)
        na = sc.op("scalar", lambda e: e.activation(B["fpp"][3][:], B["fpp"][0][:], AF.Ln), "sa")
        sc.wait("scalar", "sa", na)
        na = sc.op("scalar", lambda e: e.activation(B["fpp"][0][:], B["fpp"][3][:], AF.Exp, scale=0.5), "sa")
        sc.wait("vector", "sa", na)
        n_ = sc.op("vector", lambda e, xcc=xcc: e.tensor_mul(B["fpp"][3][:], xcc[:], B["bfp"][1][:]), "sv")
        sc.wait("vector", "sv", n_)
        n_ = sc.op("vector", lambda e: e.tensor_mul(B["fpp"][3][:], B["fpp"][3][:], B["fpp"][0][:]), "sv")
        sc.wait("vector", "sv", n_)
        if debug:
            if "zdbg" not in B:
                B["zdbg"] = nc.dram_tensor("zdbg", [4, 16, 128, 2048], F32, kind="ExternalOutput").ap()
            sc.wait("gpsimd", "sv", sc.cur("sv"))
            sc.wait("gpsimd", "sa", sc.cur("sa"))
            nda = sc.dma("gpsimd", "dbg", B["zdbg"][0, rc, :, :], B["fpp"][1][:])
            nda = sc.dma("gpsimd", "dbg", B["zdbg"][1, rc, :, :], B["fpp"][3][:])
            nda = sc.dma("gpsimd", "dbg", B["zdbg"][2, rc, :, :], xcc[:])
            nda = sc.dma("gpsimd", "dbg", B["zdbg"][3, rc, :, :], B["bfp"][1][:])
            sc.wait("vector", "dbg", nda)
        n_ = sc.op("vector", lambda e: e.tensor_tensor_scan(
            B["fpp"][0][:], B["fpp"][1][:], B["fpp"][3][:], 0.0, ALU.mult, ALU.add), "sv")
        sc.wait("scalar", "sv", n_)
        na = sc.op("scalar", lambda e, rc=rc: e.activation(
            B["fpp"][1][:], B["yT"][:, rc * 2048:(rc + 1) * 2048], AF.Gelu), "sa")
        sc.wait("vector", "sa", na)
        ng_ = sc.op("vector", lambda e, rc=rc: e.tensor_mul(
            B["bigT"][:, rc * 2048:(rc + 1) * 2048], B["fpp"][1][:], B["fpp"][0][:]), "sv")
        B[f"rg_G_{rc}"] = ng_

    wosl = B.get("wosl") or (lambda rc: B["wos"][:, rc * 512:(rc + 1) * 512])
    sc.wait("sync", "sv", sc.cur("sv"))
    sc.wait("sync", "sa", sc.cur("sa"))
    for hhs in range(4):
        prev = B.get("wo2_consumed", B.get("rg_gmm_15", 0))
        if prev:
            sc.wait("sync", "st", prev)
        nw = 0
        for rc in range(16):
            nw = sc.dma("sync", "wol", wosl(rc),
                        B["zwoT"][rc, :, hhs * 512:(hhs + 1) * 512])
        for st16 in range(16):
            db = st16 % 2
            pq = B["pxy"][db]
            prev_ev = B.get(f"fin_ev_{hhs}_{st16-1}", B.get(f"fin_ev_{hhs-1}_{15}", 0))
            if prev_ev:
                sc.wait("sync", "sv", prev_ev)
            nxr = sc.dma("sync", "xres", B["xres"][0][:],
                         B["in_x"][st16 * 128:(st16 + 1) * 128, hhs * 512:(hhs + 1) * 512])
            sc.wait("tensor", "wol", nw)
            if st16 == 0 and hhs == 0:
                sc.wait("tensor", "sv", B.get("rg_G_15", 0))
            pv2 = B.get(f"fin_ev_{hhs}_{st16-2}", B.get(f"fin_ev_{hhs-1}_{14+st16}", 0))
            if pv2:
                sc.wait("tensor", "sv", pv2)
            for rc in range(16):
                sc.op("tensor", lambda e, pq=pq, rc=rc, st16=st16, wosl=wosl: e.matmul(
                    pq[:], B["bigT"][:, rc * 2048 + st16 * 128:rc * 2048 + (st16 + 1) * 128],
                    wosl(rc),
                    start=(rc == 0), stop=(rc == 15)), "st" if rc == 15 else None)
            nmm = sc.cur("st")
            B[f"fin_mm_{hhs}_{st16}"] = nmm
            sc.wait("vector", "st", nmm)
            sc.wait("vector", "xres", nxr)
            sc.wait("vector", "yqd", sc.cur("yqd"))
            nev = sc.op("vector", lambda e, pq=pq: e.tensor_add(
                B["yq"][0][:], pq[:], B["xres"][0][:]), "sv")
            B[f"fin_ev_{hhs}_{st16}"] = nev
            sc.wait("gpsimd", "sv", nev)
            sc.dma("gpsimd", "yqd", B["out_y"][st16 * 128:(st16 + 1) * 128,
                                               hhs * 512:(hhs + 1) * 512], B["yq"][0][:])
        B["wo2_consumed"] = B[f"fin_mm_{hhs}_{15}"]



def build_attn1(debug=False):
    nc = bass.Bass("TRN2", target_bir_lowering=False, detect_race_conditions=False)
    B = {}
    B["in_x"] = nc.dram_tensor("x", [S, H], F32, kind="ExternalInput").ap()
    B["in_wxy"] = nc.dram_tensor("wxy", [4096, H], F32, kind="ExternalInput").ap()
    B["in_wi"] = nc.dram_tensor("wi", [16, 128, 128], F32, kind="ExternalInput").ap()
    B["in_bi"] = nc.dram_tensor("bi", [16, 128], F32, kind="ExternalInput").ap()
    B["in_wa"] = nc.dram_tensor("wa", [16, 128, 128], F32, kind="ExternalInput").ap()
    B["in_ba"] = nc.dram_tensor("ba", [16, 128], F32, kind="ExternalInput").ap()
    B["in_ap"] = nc.dram_tensor("apar", [H], F32, kind="ExternalInput").ap()
    B["in_cw"] = nc.dram_tensor("cw", [4, H], F32, kind="ExternalInput").ap()
    B["in_cb"] = nc.dram_tensor("cb", [H], F32, kind="ExternalInput").ap()
    B["in_wo"] = nc.dram_tensor("wo", [H, H], F32, kind="ExternalInput").ap()
    B["in_idf"] = nc.dram_tensor("idf", [128, 128], F32, kind="ExternalInput").ap()
    B["out_y"] = nc.dram_tensor("y", [S, H], F32, kind="ExternalOutput").ap()
    zwxyT = nc.dram_tensor("zwxyT", [32, 128, H], BF16, kind="Internal").ap()
    zwoT = nc.dram_tensor("zwoT", [16, 128, H], BF16, kind="Internal").ap()
    zxrT = nc.dram_tensor("zxrT", [16, 128, S], F32, kind="Internal").ap()

    with ExitStack() as es:
        E = es.enter_context
        sb = lambda n, shp, dt: E(nc.sbuf_tensor(n, shp, dt))
        B["identf"] = sb("identf", [128, 128], F32)
        B["identb"] = sb("identb", [128, 128], BF16)
        fp = [sb(f"fp{i}", [128, H], F32) for i in range(4)]
        B["xin"] = [fp[0]]
        B["xin_nbuf"] = 1
        B["xlnb"] = sb("xlnb", [128, H], BF16)
        for nm in ("st_m", "st_s", "st_v", "st_t", "st_r"):
            B[nm] = sb(nm, [128, 1], F32)
        B["bigT"] = sb("bigT", [128, 16 * 2048], BF16)
        B["yT"] = sb("yT", [128, 16 * 2048], BF16)
        bf = [sb(f"bfp{i}", [128, H], BF16) for i in range(3)]
        B["xc"] = sb("xc", [128, 2064], F32)
        B["wos"] = sb("wos", [128, 16 * 512], BF16)
        B["wgf"] = sb("wgf", [128, 256], F32)
        for nm in ("cwT", "biT", "baT", "apT", "cbT", "m8T", "t16a", "t16b"):
            B[nm] = sb(nm, [128, 16 if nm != "cwT" else 64], F32)
        B["xres"] = [sb("xres0", [128, 512], F32)]
        B["yq"] = [sb("yq0", [128, 512], F32)]
        B["ptr"] = [E(nc.psum_tensor(f"ptr{i}", [128, 128], BF16)) for i in range(2)]
        B["pxy"] = [E(nc.psum_tensor(f"pxy{i}", [128, 512], F32)) for i in range(2)]
        B["pg2"] = [E(nc.psum_tensor(f"pg2{i}", [128, 512], F32)) for i in range(2)]
        B["psmall"] = E(nc.psum_tensor("psmall", [128, 16], F32))
        B["_ptr_evac"] = [0, 0]
        B["_zs_last"] = [0, 0]

        B["zwxyT"] = zwxyT
        B["zwoT"] = zwoT
        B["zxrT"] = zxrT
        B["fpp"] = fp
        B["bfp"] = bf
        sc = Sched()
        emit_attn1(sc, B, nc, debug=debug)
        for sem in sorted(sc.c.keys()):
            sc.wait("sync", sem, sc.cur(sem))
        sems = {name: E(nc.semaphore(name)) for name in sorted(sc.c.keys())}
        block = E(nc.Block())
        sc.emit(block, sems)
    return nc


def run_attn1(x_full, wxy, wi, bi, wa, ba, ap_, cw, cb, wo, cores=None):
    nc = get_nc("attn1", build_attn1)
    idf = np.eye(128, dtype=np.float32)
    cores = cores or list(range(NCORES))
    in_maps = []
    for c in cores:
        in_maps.append(dict(x=np.ascontiguousarray(x_full[c]), wxy=wxy, wi=wi, bi=bi,
                            wa=wa, ba=ba, apar=ap_, cw=cw, cb=cb, wo=wo, idf=idf))
    res = run_bass_kernel_spmd(nc, in_maps, core_ids=cores)
    return np.stack([res.results[i]["y"] for i in range(len(cores))], axis=0)


M_CP = 16777215
R_NUM = 130155
NCTRL = 130156
CP_FRONT = 96
CP_BACK = 256
CP_EXT_LEN = NCTRL + CP_FRONT + CP_BACK

def _keys_piece(s):
    if s == 1:
        return [2.0, -4.0, 2.5, -0.5]
    if s == 0:
        return [1.0, 0.0, -2.5, 1.5]
    if s == -1:
        return [1.0, 0.0, -2.5, -1.5]
    if s == -2:
        return [2.0, 4.0, 2.5, 0.5]
    return [0.0, 0.0, 0.0, 0.0]

def _gamma(delta):
    from math import comb
    G = np.zeros((5, 4, 4))
    for dd in range(5):
        p = _keys_piece(1 - dd + delta)
        E = 2.0 - dd
        pv = np.zeros(4)
        for k in range(4):
            for j in range(k + 1):
                pv[j] += p[k] * comb(k, j) * E ** (k - j)
        for j in range(4):
            for a in range(j + 1):
                G[dd, a, j - a] += pv[j] * comb(j, a)
    return G

def spline_la_tables():
    g0 = _gamma(0)
    g1 = _gamma(1)
    pp = np.arange(128, dtype=np.float64)
    u = pp * (R_NUM / M_CP)
    up = u - 0.5
    la0 = np.zeros((22, 128))
    la1 = np.zeros((22, 128))
    for b in range(4):
        for d in range(5):
            for a in range(4):
                la0[1 + b * 5 + d] += g0[d, a, b] * up ** a
                la1[1 + b * 5 + d] += g1[d, a, b] * up ** a
    return la0.astype(np.float32), la1.astype(np.float32)

def spline_la_quad():
    la0, la1 = spline_la_tables()
    q0 = np.zeros((128, 128), np.float32)
    q1 = np.zeros((128, 128), np.float32)
    for k in range(4):
        q0[32 * k:32 * k + 20] = la0[1:21]
        q1[32 * k:32 * k + 20] = la1[1:21]
    q03 = np.zeros((128, 128), np.float32)
    q13 = np.zeros((128, 128), np.float32)
    q03[96:116] = la0[1:21]
    q13[96:116] = la1[1:21]
    bf = ml_dtypes.bfloat16
    return q0.astype(bf), q1.astype(bf), q03.astype(bf), q13.astype(bf)


def spline_delta_tables():
    out = {}
    for cls, (A, ncols, ngroups) in (("gu", (2048, 8192, 1)), ("dn", (8192, 2048, 4))):
        nfc = ncols // 512
        tab = np.zeros((ngroups * nfc * 16, 128, 512), np.uint8)
        pp = np.arange(128, dtype=np.int64)
        for g in range(ngroups):
            base = 2048 * g
            for fc in range(nfc):
                f = np.arange(fc * 512, (fc + 1) * 512, dtype=np.int64)
                for hb in range(16):
                    mm = (A * f + base + 128 * hb) * R_NUM
                    rem = mm % M_CP
                    d = (rem[None, :] + pp[:, None] * R_NUM) >= M_CP
                    tab[(g * nfc + fc) * 16 + hb] = d.astype(np.uint8)
        out[cls] = tab
    return out["gu"], out["dn"]


def spline_class_consts(A, ncols, ngroups):
    nsub = ncols // 128
    S = 16 if A == 2048 else 64
    ND = 16 if A == 2048 else 60
    WOV = 21 + ND - 1
    G0 = np.zeros((ngroups, nsub), dtype=np.int64)
    DMASK = np.zeros((ngroups * nsub, 128, ND), np.float32)
    DL2 = np.zeros((ngroups * nsub, 128, 16), np.float32)
    PHIP = np.zeros((ngroups * nsub, 128, 64), np.float32)
    f_all = np.arange(ncols, dtype=np.int64)
    for g in range(ngroups):
        base = 2048 * g
        m_full = (A * f_all + base) * R_NUM
        W0 = m_full // M_CP
        for sub in range(nsub):
            sl = slice(sub * 128, (sub + 1) * 128)
            W0s = W0[sl]
            g0 = int(W0s[0])
            d_try = W0s - (g0 + S * np.arange(128))
            shift = int(d_try.min())
            g0 += shift
            d = W0s - (g0 + S * np.arange(128))
            assert d.min() >= 0 and d.max() < ND, (A, g, sub, d.min(), d.max())
            g0_dev = g0 + CP_FRONT - 1
            assert 0 <= g0_dev and g0_dev + S * 127 + WOV <= CP_EXT_LEN, (A, g, sub, g0_dev)
            G0[g, sub] = g0_dev
            DMASK[g * nsub + sub][np.arange(128), d] = 1.0
            for hb in range(16):
                mm = (A * f_all[sl] + base + 128 * hb) * R_NUM
                q = mm // M_CP - W0s
                phi = (mm % M_CP).astype(np.float64) / M_CP
                if hb == 0:
                    assert (q == 0).all()
                    DL2[g * nsub + sub][:, 0] = 1.0
                else:
                    d2 = q - (hb - 1)
                    assert set(np.unique(d2)) <= {0, 1}, (A, g, sub, hb)
                    DL2[g * nsub + sub][:, hb] = d2
                for b in range(4):
                    PHIP[g * nsub + sub][:, hb * 4 + b] = (phi - 0.5) ** b
    return dict(S=S, ND=ND, WOV=WOV, G0=G0, DMASK=DMASK, DL2=DL2, PHIP=PHIP, nsub=nsub)


def spline_emulate(cp, A, ncols, npart_blocks, consts, la0, la1, out_rows, out_cols):
    cpe = np.concatenate([np.full(CP_FRONT, cp[0]), cp,
                          np.full(CP_BACK, cp[-1])]).astype(np.float32)
    nsub = consts["nsub"]
    S, ND = consts["S"], consts["ND"]
    ngroups = npart_blocks // 16
    WT = np.zeros((npart_blocks * 128, ncols), np.float32)
    u = np.arange(128) * (R_NUM / M_CP)
    up = (u - 0.5).astype(np.float32)
    for g in range(ngroups):
        for sub in range(nsub):
            idx = g * nsub + sub
            g0 = int(consts["G0"][g, sub])
            OV = np.zeros((128, consts["WOV"]), np.float32)
            for pf in range(128):
                OV[pf] = cpe[g0 + S * pf: g0 + S * pf + consts["WOV"]]
            CPW21 = np.zeros((128, 21), np.float32)
            for dd in range(ND):
                CPW21 += consts["DMASK"][idx][:, dd:dd + 1] * OV[:, dd:dd + 21]
            CPW22 = np.concatenate([np.zeros((128, 1), np.float32), CPW21], axis=1)
            for hb in range(16):
                d2 = consts["DL2"][idx][:, hb:hb + 1]
                lo = CPW22[:, hb:hb + 5]
                hi = CPW22[:, hb + 1:hb + 6]
                cpw5 = lo + d2 * (hi - lo)
                OUT = np.zeros((128, 22), np.float32)
                for b in range(4):
                    OUT[:, 1 + b * 5:1 + (b + 1) * 5] = cpw5 * consts["PHIP"][idx][:, hb * 4 + b:hb * 4 + b + 1]
                OUT[:, 0] = consts["PHIP"][idx][:, hb * 4 + 1]
                P0 = la0.T @ OUT.T
                P1 = la1.T @ OUT.T
                phiB = OUT[:, 0][None, :].repeat(128, 0)
                delta = ((phiB + up[:, None]) >= 0).astype(np.float32)
                Wt = np.where(delta > 0, P1, P0)
                rows = (g * 16 + hb) * 128 + np.arange(128)
                WT[np.ix_(rows, np.arange(sub * 128, (sub + 1) * 128))] = Wt
    return WT


def emit_spline_gen(sc, B, nc, jobs):
    CL = {"gu": dict(A=2048, ncols=8192, ngroups=1, S=16, ND=16),
          "dn": dict(A=8192, ncols=2048, ngroups=4, S=64, ND=60)}
    for cls in CL:
        CL[cls]["WOV"] = 21 + CL[cls]["ND"] - 1
        CL[cls]["nsub"] = CL[cls]["ncols"] // 128
    V = bass.AP
    sc.op("vector", lambda e: e.memset(B["CPW"][:], 0.0), "sg")
    for i in range(2):
        OU = B["OUTS"][i][:]
        sc.op("vector", lambda e, OU=OU: e.memset(
            V(tensor=OU.tensor, offset=OU.offset + 20, ap=[OU.ap[0], [32, 64], [1, 12]]),
            0.0), "sg")
    dsrc = []
    for (cpe, cls, dst, g0tab, dtab) in jobs:
        C = CL[cls]
        nfc = C["ncols"] // 512
        for g in range(C["ngroups"]):
            for fc in range(nfc):
                for hb in range(16):
                    dsrc.append(dtab[(g * nfc + fc) * 16 + hb])
    ntiles = len(dsrc)
    NDLT = len(B["dlt3"])
    for t in range(min(NDLT, ntiles)):
        sc.dma("scalar", "dl", B["dlt3"][t % NDLT][:], dsrc[t])
    pend_dlt = []
    tix = 0
    hbgx = 0
    fcx = 0
    for (cpe, cls, dst, g0tab, dtab) in jobs:
        C = CL[cls]
        ND = C["ND"]
        nfc = C["ncols"] // 512
        for g in range(C["ngroups"]):
            for fc in range(nfc):
                par = fcx % 2
                OUTS = B["OUTS"][par][:]
                sc.wait("sync", "sg", B.get(f"gen_slab_{fcx-1}", 0))
                sc.dma("sync", "ovl", B["DM4"][:, 0:4 * ND].rearrange(
                    "p (s d) -> p s d", d=ND),
                    B[f"in_dmask_{cls}"][g * C["nsub"] + fc * 4:g * C["nsub"] + fc * 4 + 4].rearrange("s p d -> p s d"))
                sc.dma("sync", "ovl", B["DL4"][:].rearrange("p (s d) -> p s d", d=16),
                    B[f"in_dl2_{cls}"][g * C["nsub"] + fc * 4:g * C["nsub"] + fc * 4 + 4].rearrange("s p d -> p s d"))
                sc.dma("sync", "ovl", B["PH4"][:].rearrange("p (s d) -> p s d", d=64),
                    B[f"in_phip_{cls}"][g * C["nsub"] + fc * 4:g * C["nsub"] + fc * 4 + 4].rearrange("s p d -> p s d"))
                nld = 0
                for sub in range(4):
                    g0 = int(g0tab[g, fc * 4 + sub])
                    src = V(tensor=cpe.tensor, offset=g0,
                            ap=[[C["S"], 128], [1, C["WOV"]]])
                    nld = sc.dma("sync", "ovl", B["OV4"][:, sub * 80:sub * 80 + C["WOV"]], src)
                sc.wait("vector", "ovl", nld)
                sc.wait("vector", "st", B.get(f"gen_outs_{fcx-2}", 0))
                ov, dm, dl, ph = B["OV4"][:], B["DM4"][:], B["DL4"][:], B["PH4"][:]
                cpw, tb, d5, c5 = B["CPW"][:], B["TBIG"][:], B["DIF5"][:], B["CPW5S"][:]
                for sub in range(4):
                    sc.op("vector", lambda e, sub=sub, ND=ND: e.tensor_tensor(
                        V(tensor=tb.tensor, offset=tb.offset, ap=[tb.ap[0], [ND, 21], [1, ND]]),
                        V(tensor=ov.tensor, offset=ov.offset + sub * 80, ap=[ov.ap[0], [1, 21], [1, ND]]),
                        V(tensor=dm.tensor, offset=dm.offset + sub * ND, ap=[dm.ap[0], [0, 21], [1, ND]]),
                        ALU.mult), "sg")
                    sc.op("vector", lambda e, sub=sub, ND=ND: e.reduce_sum(
                        cpw[:, sub * 22 + 1:sub * 22 + 22],
                        V(tensor=tb.tensor, offset=tb.offset, ap=[tb.ap[0], [ND, 21], [1, ND]]),
                        AX.X), "sg")
                    sc.op("vector", lambda e, sub=sub: e.tensor_tensor(
                        V(tensor=d5.tensor, offset=d5.offset, ap=[d5.ap[0], [5, 16], [1, 5]]),
                        V(tensor=cpw.tensor, offset=cpw.offset + sub * 22 + 1, ap=[cpw.ap[0], [1, 16], [1, 5]]),
                        V(tensor=cpw.tensor, offset=cpw.offset + sub * 22, ap=[cpw.ap[0], [1, 16], [1, 5]]),
                        ALU.subtract), "sg")
                    sc.op("vector", lambda e, sub=sub: e.tensor_tensor(
                        V(tensor=d5.tensor, offset=d5.offset, ap=[d5.ap[0], [5, 16], [1, 5]]),
                        V(tensor=d5.tensor, offset=d5.offset, ap=[d5.ap[0], [5, 16], [1, 5]]),
                        V(tensor=dl.tensor, offset=dl.offset + sub * 16, ap=[dl.ap[0], [1, 16], [0, 5]]),
                        ALU.mult), "sg")
                    sc.op("vector", lambda e, sub=sub: e.tensor_tensor(
                        V(tensor=c5.tensor, offset=c5.offset, ap=[c5.ap[0], [5, 16], [1, 5]]),
                        V(tensor=d5.tensor, offset=d5.offset, ap=[d5.ap[0], [5, 16], [1, 5]]),
                        V(tensor=cpw.tensor, offset=cpw.offset + sub * 22, ap=[cpw.ap[0], [1, 16], [1, 5]]),
                        ALU.add), "sg")
                    sc.op("vector", lambda e, sub=sub, OUTS=OUTS: e.tensor_tensor(
                        V(tensor=OUTS.tensor, offset=OUTS.offset + sub * 512,
                          ap=[OUTS.ap[0], [32, 16], [5, 4], [1, 5]]),
                        V(tensor=c5.tensor, offset=c5.offset, ap=[c5.ap[0], [5, 16], [0, 4], [1, 5]]),
                        V(tensor=ph.tensor, offset=ph.offset + sub * 64, ap=[ph.ap[0], [4, 16], [1, 4], [0, 5]]),
                        ALU.mult), "sg")
                nslab = sc.cur("sg")
                B[f"gen_slab_{fcx}"] = nslab
                sc.wait("tensor", "sg", nslab)
                for hbg in range(4):
                    gp = hbgx % 2
                    TR = B["ptg"][gp]
                    RHS = B["RHSg"][gp]
                    sc.wait("tensor", "sa", B.get(f"gen_ev_{hbgx-2}", 0))
                    for sub in range(4):
                        sc.op("tensor", lambda e, sub=sub, hbg=hbg, TR=TR, OUTS=OUTS: e.transpose(
                            TR[:, sub * 128:(sub + 1) * 128],
                            OUTS[:, sub * 512 + hbg * 128:sub * 512 + hbg * 128 + 128],
                            B["identb"][:]), "st")
                    ntr4 = sc.cur("st")
                    if hbg == 3:
                        B[f"gen_outs_{fcx}"] = ntr4
                    sc.wait("scalar", "st", ntr4)
                    nev = sc.op("scalar", lambda e, TR=TR, RHS=RHS: e.copy(RHS[:], TR[:]), "sa")
                    for (dbuf, dsr, nwait) in pend_dlt:
                        sc.wait("scalar", "sv", nwait)
                        sc.dma("scalar", "dl", dbuf, dsr)
                    pend_dlt = []
                    B[f"gen_ev_{hbgx}"] = nev
                    sc.wait("tensor", "sa", nev)
                    for hh in range(4):
                        tpar = tix % 2
                        P0, P1 = B["pgen"][tpar]
                        sc.wait("tensor", "sv", B.get(f"gen_sel_{tix-2}", 0))
                        if hh < 3:
                            b0, bn, k0, k1 = hh * 32, 20, "LA0q", "LA1q"
                        else:
                            b0, bn, k0, k1 = 64, 64, "LA0q3", "LA1q3"
                        sc.op("tensor", lambda e, P0=P0, RHS=RHS, b0=b0, bn=bn, k0=k0: e.matmul(
                            P0[:], B[k0][b0:b0 + bn, :], RHS[b0:b0 + bn, :],
                            start=True, stop=True), None)
                        nmm = sc.op("tensor", lambda e, P1=P1, RHS=RHS, b0=b0, bn=bn, k1=k1: e.matmul(
                            P1[:], B[k1][b0:b0 + bn, :], RHS[b0:b0 + bn, :],
                            start=True, stop=True), "st")
                        hb = hbg * 4 + hh
                        wt = B["wt2"][tpar]
                        dlt = B["dlt3"][tix % NDLT][:]
                        sc.wait("vector", "st", nmm)
                        sc.wait("vector", "dl", 16 * (tix + 1))
                        sc.wait("vector", "wz", B.get(f"gen_wz_{tix-2}", 0))
                        nsel = sc.op("vector", lambda e, wt=wt, dlt=dlt, P0=P0, P1=P1: e.select(
                            wt[:], dlt, P1[:], P0[:]), "sv")
                        B[f"gen_sel_{tix}"] = nsel
                        if tix + NDLT < ntiles:
                            pend_dlt.append((B["dlt3"][(tix + NDLT) % NDLT][:],
                                             dsrc[tix + NDLT], nsel))
                        if cls == "gu":
                            dv = dst[fc * 4:(fc + 1) * 4, :, hb * 128:(hb + 1) * 128].rearrange(
                                "ib p c -> p ib c")
                            sv_ = wt[:].rearrange("p (ib c) -> p ib c", c=128)
                        else:
                            dv = dst[g * 16 + hb, fc, :, :]
                            sv_ = wt[:]
                        sc.wait("gpsimd", "sv", nsel)
                        B[f"gen_wz_{tix}"] = sc.dma("gpsimd", "wz", dv, sv_)
                        tix += 1
                    hbgx += 1
                fcx += 1
    return sc.cur("wz")


def build_ffn1():
    nc = bass.Bass("TRN2", target_bir_lowering=False, detect_race_conditions=False)
    B = {}
    B["in_x"] = nc.dram_tensor("x", [S, H], F32, kind="ExternalInput").ap()
    B["in_cpg"] = nc.dram_tensor("cpg", [CP_EXT_LEN], F32, kind="ExternalInput").ap()
    B["in_cpu"] = nc.dram_tensor("cpu", [CP_EXT_LEN], F32, kind="ExternalInput").ap()
    B["in_cpd"] = nc.dram_tensor("cpd", [CP_EXT_LEN], F32, kind="ExternalInput").ap()
    B["in_la0q"] = nc.dram_tensor("la0q", [128, 128], BF16, kind="ExternalInput").ap()
    B["in_la1q"] = nc.dram_tensor("la1q", [128, 128], BF16, kind="ExternalInput").ap()
    B["in_la0q3"] = nc.dram_tensor("la0q3", [128, 128], BF16, kind="ExternalInput").ap()
    B["in_la1q3"] = nc.dram_tensor("la1q3", [128, 128], BF16, kind="ExternalInput").ap()
    B["in_dltg"] = nc.dram_tensor("dltg", [256, 128, 512], U8, kind="ExternalInput").ap()
    B["in_dltd"] = nc.dram_tensor("dltd", [256, 128, 512], U8, kind="ExternalInput").ap()
    B["in_dmask_gu"] = nc.dram_tensor("dmg", [64, 128, 16], F32, kind="ExternalInput").ap()
    B["in_dl2_gu"] = nc.dram_tensor("dlg", [64, 128, 16], F32, kind="ExternalInput").ap()
    B["in_phip_gu"] = nc.dram_tensor("phg", [64, 128, 64], F32, kind="ExternalInput").ap()
    B["in_dmask_dn"] = nc.dram_tensor("dmd", [64, 128, 60], F32, kind="ExternalInput").ap()
    B["in_dl2_dn"] = nc.dram_tensor("dld", [64, 128, 16], F32, kind="ExternalInput").ap()
    B["in_phip_dn"] = nc.dram_tensor("phd", [64, 128, 64], F32, kind="ExternalInput").ap()
    B["in_idf"] = nc.dram_tensor("idf", [128, 128], F32, kind="ExternalInput").ap()
    B["out_y"] = nc.dram_tensor("y", [S, H], F32, kind="ExternalOutput").ap()
    zxlnT = nc.dram_tensor("zxlnT", [4, 16, 128, 512], BF16, kind="Internal").ap()
    zwgT = nc.dram_tensor("zwgT", [64, 128, H], BF16, kind="Internal").ap()
    zwuT = nc.dram_tensor("zwuT", [64, 128, H], BF16, kind="Internal").ap()
    zwdT = nc.dram_tensor("zwdT", [64, 4, 128, 512], BF16, kind="Internal").ap()

    cg = spline_class_consts(2048, 8192, 1)
    cd = spline_class_consts(8192, 2048, 4)

    with ExitStack() as es:
        E = es.enter_context
        sb = lambda n, shp, dt: E(nc.sbuf_tensor(n, shp, dt))
        B["identf"] = sb("identf", [128, 128], F32)
        B["identb"] = sb("identb", [128, 128], BF16)
        B["xin"] = [sb(f"xin{i}", [128, H], F32) for i in range(2)]
        B["sqtmp"] = sb("sqtmp", [128, H], BF16)
        B["xlnb"] = sb("xlnb", [128, H], BF16)
        for nm in ("st_m", "st_s", "st_v", "st_t", "st_r"):
            B[nm] = sb(nm, [128, 1], F32)
        B["sT"] = [sb(f"sT{i}", [128, H], BF16) for i in range(2)]
        B["xlnTq"] = sb("xlnTq", [128, 16 * 512], BF16)
        B["wT_sb"] = [[sb(f"wT{m}{i}", [128, H], BF16) for i in range(2)] for m in range(2)]
        B["actT"] = sb("actT", [128, 64 * 512], BF16)
        B["wdT_sb"] = [sb(f"wdT{i}", [128, 512], BF16) for i in range(4)]
        B["xres"] = [sb(f"xres{i}", [128, 512], F32) for i in range(4)]
        B["yq"] = [sb(f"yq{i}", [128, 512], F32) for i in range(4)]
        B["sact"] = [sb(f"sact{i}", [128, 512], BF16) for i in range(2)]
        B["OV4"] = sb("OV4", [128, 4 * 80], F32)
        B["DM4"] = sb("DM4", [128, 4 * 60], F32)
        B["DL4"] = sb("DL4", [128, 64], F32)
        B["PH4"] = sb("PH4", [128, 256], F32)
        B["CPW"] = sb("CPW", [128, 4 * 22], F32)[:]
        B["TBIG"] = sb("TBIG", [128, 1260], F32)[:]
        B["DIF5"] = sb("DIF5", [128, 80], F32)[:]
        B["CPW5S"] = sb("CPW5S", [128, 80], F32)[:]
        B["OUTS"] = [sb(f"OUTS{i}", [128, 2048], BF16)[:] for i in range(2)]
        B["RHSg"] = [sb(f"RHSg{i}", [128, 512], BF16) for i in range(2)]
        B["LA0q"] = sb("LA0q", [128, 128], BF16)
        B["LA1q"] = sb("LA1q", [128, 128], BF16)
        B["LA0q3"] = sb("LA0q3", [128, 128], BF16)
        B["LA1q3"] = sb("LA1q3", [128, 128], BF16)
        B["dlt3"] = [sb(f"dlt3{i}", [128, 512], U8) for i in range(6)]
        B["wt2"] = [sb(f"wt2{i}", [128, 512], BF16) for i in range(2)]
        B["ptr"] = [E(nc.psum_tensor(f"ptr{i}", [128, 128], BF16)) for i in range(2)]
        B["ptg"] = [E(nc.psum_tensor(f"ptg{i}", [128, 512], BF16)) for i in range(2)]
        B["pg"] = [E(nc.psum_tensor(f"pg{i}", [128, 512], F32)) for i in range(2)]
        B["pu"] = [E(nc.psum_tensor(f"pu{i}", [128, 512], F32)) for i in range(2)]
        B["pgen"] = [(B["pg"][0][:], B["pg"][1][:]), (B["pu"][0][:], B["pu"][1][:])]
        B["_ptr_evac"] = [0, 0]
        B["_zs_last"] = [0, 0]

        sc = Sched()
        n_id = sc.dma("sync", "idm", B["identf"][:], B["in_idf"])
        n_la = sc.dma("gpsimd", "idm2", B["LA0q"][:], B["in_la0q"])
        n_la = sc.dma("gpsimd", "idm2", B["LA1q"][:], B["in_la1q"])
        n_la = sc.dma("gpsimd", "idm2", B["LA0q3"][:], B["in_la0q3"])
        n_la = sc.dma("gpsimd", "idm2", B["LA1q3"][:], B["in_la1q3"])
        sc.wait("vector", "idm", n_id)
        sc.wait("vector", "idm2", n_la)
        nid_b = sc.op("vector", lambda e: e.tensor_copy(B["identb"][:], B["identf"][:]), "sv")

        emit_ln_pass(sc, B, B["in_x"], zxlnT, "ln1")

        jobs = [(B["in_cpg"], "gu", zwgT, cg["G0"], B["in_dltg"]),
                (B["in_cpu"], "gu", zwuT, cg["G0"], B["in_dltg"]),
                (B["in_cpd"], "dn", zwdT, cd["G0"], B["in_dltd"])]
        sc.wait("tensor", "sv", nid_b)
        sc.wait("tensor", "idm2", n_la)
        emit_spline_gen(sc, B, nc, jobs)
        for sem in ("wz", "zs0", "zs1"):
            sc.wait("sync", sem, sc.cur(sem))
        emit_ffn_matmuls(sc, B, zxlnT, zwgT, zwuT, zwdT, B["in_x"], B["out_y"], "f1")
        for sem in sorted(sc.c.keys()):
            sc.wait("sync", sem, sc.cur(sem))
        sems = {name: E(nc.semaphore(name)) for name in sorted(sc.c.keys())}
        block = E(nc.Block())
        sc.emit(block, sems)
    return nc


def ffn1_host_inputs():
    la0q, la1q, la0q3, la1q3 = spline_la_quad()
    dltg, dltd = spline_delta_tables()
    cg = spline_class_consts(2048, 8192, 1)
    cd = spline_class_consts(8192, 2048, 4)
    return dict(la0q=la0q, la1q=la1q, la0q3=la0q3, la1q3=la1q3, dltg=dltg, dltd=dltd,
                dmg=cg["DMASK"], dlg=cg["DL2"], phg=cg["PHIP"],
                dmd=cd["DMASK"], dld=cd["DL2"], phd=cd["PHIP"])


def pad_cp(cp):
    return np.concatenate([np.full(CP_FRONT, cp[0]), cp,
                           np.full(CP_BACK, cp[-1])]).astype(np.float32)


def run_ffn1(x_full, gcp, ucp, dcp, cores=None):
    nc = get_nc("ffn1", build_ffn1)
    idf = np.eye(128, dtype=np.float32)
    hi = ffn1_host_inputs()
    cores = cores or list(range(NCORES))
    in_maps = []
    for c in cores:
        im = dict(x=np.ascontiguousarray(x_full[c]), cpg=pad_cp(gcp), cpu=pad_cp(ucp),
                  cpd=pad_cp(dcp), idf=idf)
        im.update(hi)
        in_maps.append(im)
    res = run_bass_kernel_spmd(nc, in_maps, core_ids=cores)
    return np.stack([res.results[i]["y"] for i in range(len(cores))], axis=0)



def sbarrier(sc):
    for eng in Sched.ENGINES:
        for sem in list(sc.c.keys()):
            sc.wait(eng, sem, sc.cur(sem))


def build_fused():
    nc = bass.Bass("TRN2", target_bir_lowering=False, detect_race_conditions=False)
    B = {}
    dr = lambda n, shp, dt: nc.dram_tensor(n, shp, dt, kind="ExternalInput").ap()
    i_x = dr("x", [S, H], F32)
    i_gw = dr("gw", [II, H], I32); i_uw = dr("uw", [II, H], I32); i_dw = dr("dw", [H, II], I32)
    i_gs = dr("gs", [II], F32); i_us = dr("us", [II], F32); i_ds = dr("ds", [H], F32)
    i_wq = dr("wq", [H, 2048], F32); i_wk = dr("wk", [H, 512], F32)
    i_wv = dr("wv", [H, 512], F32); i_wo = dr("wo", [2048, H], F32)
    i_lam = dr("lamb", [128, 16], F32)
    i_wxy = dr("wxy", [4096, H], F32); i_wi = dr("wi", [16, 128, 128], F32)
    i_bi = dr("bi", [16, 128], F32); i_wa = dr("wa", [16, 128, 128], F32)
    i_ba = dr("ba", [16, 128], F32); i_ap = dr("apar", [H], F32)
    i_cw = dr("cw", [4, H], F32); i_cb = dr("cb", [H], F32); i_wo1 = dr("wo1", [H, H], F32)
    i_cpg = dr("cpg", [CP_EXT_LEN], F32); i_cpu = dr("cpu", [CP_EXT_LEN], F32)
    i_cpd = dr("cpd", [CP_EXT_LEN], F32)
    i_idf = dr("idf", [128, 128], F32)
    i_cq = dr("cq", [16, 128, 512], BF16); i_sq = dr("sq", [16, 128, 512], BF16)
    i_ck = dr("ck", [16, 128, 512], BF16); i_sk = dr("sk", [16, 128, 512], BF16)
    i_msk = dr("msk", [128, 640], BF16)
    i_la0q = dr("la0q", [128, 128], BF16); i_la1q = dr("la1q", [128, 128], BF16)
    i_la0q3 = dr("la0q3", [128, 128], BF16); i_la1q3 = dr("la1q3", [128, 128], BF16)
    i_dltg = dr("dltg", [256, 128, 512], U8); i_dltd = dr("dltd", [256, 128, 512], U8)
    i_dmg = dr("dmg", [64, 128, 16], F32); i_dlg = dr("dlg", [64, 128, 16], F32)
    i_phg = dr("phg", [64, 128, 64], F32)
    i_dmd = dr("dmd", [64, 128, 60], F32); i_dld = dr("dld", [64, 128, 16], F32)
    i_phd = dr("phd", [64, 128, 64], F32)
    out_y = nc.dram_tensor("y", [S, H], F32, kind="ExternalOutput").ap()
    z = [nc.dram_tensor(f"z{i}", [S, H], F32, kind="Internal").ap() for i in range(5)]
    zxlnT = nc.dram_tensor("zxlnT", [4, 16, 128, 512], BF16, kind="Internal").ap()
    zwg0 = nc.dram_tensor("zwg0", [64, 128, H], BF16, kind="Internal").ap()
    zwu0 = nc.dram_tensor("zwu0", [64, 128, H], BF16, kind="Internal").ap()
    zwd0 = nc.dram_tensor("zwd0", [64, 4, 128, 512], BF16, kind="Internal").ap()
    zwg1 = nc.dram_tensor("zwg1", [64, 128, H], BF16, kind="Internal").ap()
    zwu1 = nc.dram_tensor("zwu1", [64, 128, H], BF16, kind="Internal").ap()
    zwd1 = nc.dram_tensor("zwd1", [64, 4, 128, 512], BF16, kind="Internal").ap()
    zwxyT = nc.dram_tensor("zwxyT", [32, 128, H], BF16, kind="Internal").ap()
    zwoT = nc.dram_tensor("zwoT", [16, 128, H], BF16, kind="Internal").ap()
    zxrT = nc.dram_tensor("zxrT", [16, 128, S], F32, kind="Internal").ap()

    cg = spline_class_consts(2048, 8192, 1)
    cd = spline_class_consts(8192, 2048, 4)

    with ExitStack() as es:
        E = es.enter_context
        sb = lambda n, shp, dt: E(nc.sbuf_tensor(n, shp, dt))
        PA = sb("PA", [128, 32768], BF16)
        PB = sb("PB", [128, 32768], BF16)
        PC = sb("PC", [128, 24576], BF16)
        B["identf"] = sb("identf", [128, 128], F32)
        B["identb"] = sb("identb", [128, 128], BF16)
        xbig = sb("xbig", [128, H], F32)
        B["xin"] = [xbig]
        B["xin_nbuf"] = 1
        B["xlnb"] = sb("xlnb", [128, H], BF16)
        for nm in ("st_m", "st_s", "st_v", "st_t", "st_r"):
            B[nm] = sb(nm, [128, 1], F32)
        B["sT"] = [sb(f"sTf{i}", [128, H], BF16) for i in range(2)]
        B["stmp"] = sb("stmp", [128, 128], F32)
        B["gsT"] = sb("gsT", [128, 64], F32)
        B["usT"] = sb("usT", [128, 64], F32)
        B["dsT"] = sb("dsT", [128, 16], F32)
        B["ropc"] = sb("ropc", [128, 512], BF16)
        B["rops"] = sb("rops", [128, 512], BF16)
        B["qa"] = sb("qa", [128, 512], BF16)
        B["qb"] = sb("qb", [128, 512], BF16)
        B["qr"] = sb("qr", [128, 512], BF16)
        B["lamB"] = sb("lamB", [128, 16], F32)
        B["mskb"] = sb("mskb", [128, 640], BF16)
        B["sm"] = [sb(f"sm{i}", [128, 1], F32) for i in range(16)]
        B["zsb"] = PC[:, 16384:17664].bitcast(F32)
        B["esb"] = [PC[:, 17664:18304], PC[:, 18304:18944]]
        B["dif"] = PC[:, 18944:19584]
        B["aout"] = PC[:, 19584:19712]
        B["zsb2"] = PC[:, 19712:20992].bitcast(F32)
        B["esb2"] = [PC[:, 20992:21632], PC[:, 21632:22272]]
        B["dif2"] = PC[:, 22272:22912]
        B["aout2"] = PC[:, 22912:23040]
        for nm in ("cwT", "biT", "baT", "apT", "cbT", "m8T", "t16a", "t16b"):
            B[nm] = sb(nm, [128, 16 if nm != "cwT" else 64], F32)
        B["wgf"] = sb("wgf", [128, 256], F32)
        B["xc"] = PC[:, 12288:16416].bitcast(F32)
        B["OV4"] = PC[:, 0:640].bitcast(F32)
        B["DM4"] = PC[:, 640:1120].bitcast(F32)
        B["DL4"] = PC[:, 1120:1248].bitcast(F32)
        B["PH4"] = PC[:, 1248:1760].bitcast(F32)
        B["CPW"] = PC[:, 1760:1936].bitcast(F32)
        B["LA0q"] = PC[:, 2600:2728]
        B["LA1q"] = PC[:, 2728:2856]
        B["LA0q3"] = PC[:, 2856:2984]
        B["LA1q3"] = PC[:, 2984:3112]
        B["TBIG"] = PA[:, 0:2520].bitcast(F32)
        B["DIF5"] = PA[:, 2560:2720].bitcast(F32)
        B["CPW5S"] = PA[:, 2720:2880].bitcast(F32)
        B["OUTS"] = [PA[:, 4096 + i * 2048:4096 + (i + 1) * 2048] for i in range(2)]
        B["RHSg"] = [PA[:, 8192 + i * 512:8192 + (i + 1) * 512] for i in range(2)]
        B["dlt3"] = [PA[:, 9216 + i * 256:9216 + (i + 1) * 256].bitcast(U8) for i in range(6)]
        B["wt2"] = [PB[:, 28848 + i * 512:28848 + (i + 1) * 512] for i in range(2)]
        pbigA = E(nc.psum_tensor("pbigA", [128, 1024], F32))
        pbigB = E(nc.psum_tensor("pbigB", [128, 1024], F32))
        pbigC = E(nc.psum_tensor("pbigC", [128, 1024], F32))
        pbigD = E(nc.psum_tensor("pbigD", [128, 1024], F32))
        B["ptr"] = [pbigC[:, 0:64].bitcast(BF16), pbigD[:, 0:64].bitcast(BF16)]
        B["ptg"] = [pbigC[:, 512:768].bitcast(BF16), pbigD[:, 512:768].bitcast(BF16)]
        B["pav"] = pbigC[:, 128:256]
        B["pscl"] = B["pav"]
        B["psmall"] = B["pav"]
        B["pg"] = [pbigA[:, 0:512], pbigA[:, 512:1024]]
        B["pu"] = [pbigB[:, 0:512], pbigB[:, 512:1024]]
        B["po_sets"] = [
            [pbigA[:, 0:512], pbigA[:, 512:1024], pbigB[:, 0:512], pbigB[:, 512:1024]],
            [pbigC[:, 0:512], pbigC[:, 512:1024], pbigD[:, 0:512], pbigD[:, 512:1024]]]
        B["pgen"] = [(pbigA[:, 0:512], pbigA[:, 512:1024]),
                     (pbigB[:, 0:512], pbigB[:, 512:1024])]
        B["pqkv"] = B["pg"]
        B["pxy"] = B["pg"]
        B["pg2"] = B["pu"]
        B["psc"] = pbigB[:, 0:640]
        B["psc2"] = pbigD[:, 128:768]
        B["pav2"] = pbigD[:, 768:896]
        B["_ptr_evac"] = [0, 0]
        B["_zs_last"] = [0, 0]

        bc = lambda ap, dt: ap.bitcast(dt)

        def bind_ffn():
            B["actT"] = PA[:]
            B["xlnTq"] = PB[:, 0:8192]
            B["wT_sb"] = [[PB[:, 8192:10240], PB[:, 10240:12288]],
                          [PB[:, 12288:14336], PB[:, 14336:16384]]]
            B["wgu_int"] = [bc(PC[:, 0:4096], I32), bc(PC[:, 4096:8192], I32)]
            B["wdq"] = [PC[:, 8192:10240], PC[:, 10240:12288]]
            B["wd_int"] = bc(PC[:, 0:8192], I32)
            B["wdstg"] = PC[:, 8192:16384]
            B["wdT_sb"] = [PC[:, 18432 + i * 512:18944 + i * 512] for i in range(4)]
            B["sact"] = [PC[:, 17408:17920], PC[:, 17920:18432]]
            B["xres"] = [bc(PB[:, 20480 + i * 1024:20480 + (i + 1) * 1024], F32) for i in range(4)]
            B["yq"] = [bc(PB[:, 16384 + i * 1024:16384 + (i + 1) * 1024], F32) for i in range(4)]

        def bind_attn0():
            B["bigT"] = PA[:]
            B["qT"] = PB[:]
            B["kT"] = PC[:, 0:8192]
            B["vP"] = PC[:, 8192:16384]
            B["wsb"] = PC[:, 16384:24576]
            B["xres"] = [bc(B["sT"][0][:, 0:1024], F32)]
            B["yq"] = [bc(B["sT"][0][:, 1024:2048], F32)]
            B["in_wq"], B["in_wk"], B["in_wv"], B["in_wo"] = i_wq, i_wk, i_wv, i_wo
            B["in_lam"], B["in_idf"] = i_lam, i_idf
            B["in_cq"], B["in_sq"], B["in_ck"], B["in_sk"], B["in_msk"] = i_cq, i_sq, i_ck, i_sk, i_msk

        def bind_attn1():
            B["bigT"] = PA[:]
            B["yT"] = PB[:]
            B["fpp"] = [xbig, bc(PC[:, 0:4096], F32), bc(PC[:, 4096:8192], F32),
                        bc(PC[:, 8192:12288], F32)]
            B["bfp"] = [PC[:, 18464:20512], PC[:, 16416:18464], PC[:, 20512:22560]]
            def _wosl(rc):
                if rc < 3:
                    return PC[:, 22560 + rc * 512:22560 + (rc + 1) * 512]
                if rc < 7:
                    return B["sT"][1][:, (rc - 3) * 512:(rc - 2) * 512]
                if rc < 11:
                    return B["xlnb"][:, (rc - 7) * 512:(rc - 6) * 512]
                return [B["qa"], B["qb"], B["qr"], B["ropc"], B["rops"]][rc - 11][:]
            B["wosl"] = _wosl
            B["xres"] = [bc(B["sT"][0][:, 0:1024], F32)]
            B["yq"] = [bc(B["sT"][0][:, 1024:2048], F32)]
            B["in_wxy"], B["in_wi"], B["in_bi"], B["in_wa"], B["in_ba"] = i_wxy, i_wi, i_bi, i_wa, i_ba
            B["in_ap"], B["in_cw"], B["in_cb"], B["in_wo"], B["in_idf"] = i_ap, i_cw, i_cb, i_wo1, i_idf
            B["zwxyT"], B["zwoT"], B["zxrT"] = zwxyT, zwoT, zxrT

        sc = Sched()

        def snap():
            return dict(B)
        n_id = sc.dma("sync", "idm", B["identf"][:], i_idf)
        sc.wait("vector", "idm", n_id)
        nid_b = sc.op("vector", lambda e: e.tensor_copy(B["identb"][:], B["identf"][:]), "sv")
        sc.wait("tensor", "sv", nid_b)

        bind_ffn()
        B["in_x"], B["in_gw"], B["in_uw"], B["in_dw"] = i_x, i_gw, i_uw, i_dw
        B["in_gs"], B["in_us"], B["in_ds"], B["in_idf"] = i_gs, i_us, i_ds, i_idf
        emit_const_prep(sc, snap(), [("in_gs", "gsT", 64), ("in_us", "usT", 64), ("in_ds", "dsT", 16)])
        emit_ln_pass(sc, snap(), i_x, zxlnT, "p1")
        emit_ffn_weight_prep_int(sc, snap(), zwg0, zwu0, zwd0)
        for sem in ("zs0", "zs1", "zd"):
            sc.wait("sync", sem, sc.cur(sem))
        emit_ffn_matmuls(sc, snap(), zxlnT, zwg0, zwu0, zwd0, i_x, z[0], "p1")
        sbarrier(sc)

        bind_attn0()
        B["in_x"] = z[0]
        B["out_y"] = z[1]
        emit_attn0(sc, snap(), nc)
        sbarrier(sc)

        bind_ffn()
        B["in_x"] = z[1]
        emit_ln_pass(sc, snap(), z[1], zxlnT, "p3")
        for sem in ("zs0", "zs1"):
            sc.wait("sync", sem, sc.cur(sem))
        emit_ffn_matmuls(sc, snap(), zxlnT, zwg0, zwu0, zwd0, z[1], z[2], "p3")
        sbarrier(sc)

        bind_ffn()
        B["in_x"] = z[2]
        emit_ln_pass(sc, snap(), z[2], zxlnT, "p4")
        jobs = [(i_cpg, "gu", zwg1, cg["G0"], i_dltg), (i_cpu, "gu", zwu1, cg["G0"], i_dltg),
                (i_cpd, "dn", zwd1, cd["G0"], i_dltd)]
        n_la4 = sc.dma("gpsimd", "idm2", B["LA0q"][:], i_la0q)
        n_la4 = sc.dma("gpsimd", "idm2", B["LA1q"][:], i_la1q)
        n_la4 = sc.dma("gpsimd", "idm2", B["LA0q3"][:], i_la0q3)
        n_la4 = sc.dma("gpsimd", "idm2", B["LA1q3"][:], i_la1q3)
        sc.wait("tensor", "idm2", n_la4)
        B["in_dmask_gu"], B["in_dl2_gu"], B["in_phip_gu"] = i_dmg, i_dlg, i_phg
        B["in_dmask_dn"], B["in_dl2_dn"], B["in_phip_dn"] = i_dmd, i_dld, i_phd
        emit_spline_gen(sc, snap(), nc, jobs)
        for sem in ("wz", "zs0", "zs1"):
            sc.wait("sync", sem, sc.cur(sem))
        sc.wait("vector", "wz", sc.cur("wz"))
        emit_ffn_matmuls(sc, snap(), zxlnT, zwg1, zwu1, zwd1, z[2], z[3], "p4")
        sbarrier(sc)

        bind_attn1()
        B["in_x"] = z[3]
        B["out_y"] = z[4]
        emit_attn1(sc, snap(), nc)
        sbarrier(sc)

        bind_ffn()
        B["in_x"] = z[4]
        emit_ln_pass(sc, snap(), z[4], zxlnT, "p6")
        for sem in ("zs0", "zs1"):
            sc.wait("sync", sem, sc.cur(sem))
        emit_ffn_matmuls(sc, snap(), zxlnT, zwg1, zwu1, zwd1, z[4], out_y, "p6")

        for sem in sorted(sc.c.keys()):
            sc.wait("sync", sem, sc.cur(sem))
        sems = {name: E(nc.semaphore(name)) for name in sorted(sc.c.keys())}
        block = E(nc.Block())
        sc.emit(block, sems)
    return nc


def kernel_fused(inp):
    nc = get_nc("fused", build_fused)
    idf = np.eye(128, dtype=np.float32)
    cq, sq, ck, sk, msk = make_attn0_consts()
    lamB = np.tile(inp["L0_attn_lam"][None, :], (128, 1)).astype(np.float32)
    hi = ffn1_host_inputs()
    in_maps = []
    for c in range(NCORES):
        im = dict(x=np.ascontiguousarray(inp["L0_attn_x"][c].astype(np.float32, copy=False)),
                  gw=inp["L0_ffn_gate_w"], uw=inp["L0_ffn_up_w"], dw=inp["L0_ffn_down_w"],
                  gs=inp["L0_ffn_gate_scale"], us=inp["L0_ffn_up_scale"], ds=inp["L0_ffn_down_scale"],
                  wq=inp["L0_attn_Wq"], wk=inp["L0_attn_Wk"], wv=inp["L0_attn_Wv"],
                  wo=inp["L0_attn_Wo"], lamb=lamB,
                  wxy=inp["L1_attn_W_xy"], wi=inp["L1_attn_wi"], bi=inp["L1_attn_bi"],
                  wa=inp["L1_attn_wa"], ba=inp["L1_attn_ba"], apar=inp["L1_attn_a_param"],
                  cw=inp["L1_attn_conv_w"], cb=inp["L1_attn_conv_b"], wo1=inp["L1_attn_W_out"],
                  cpg=pad_cp(inp["L1_ffn_gate_cp"]), cpu=pad_cp(inp["L1_ffn_up_cp"]),
                  cpd=pad_cp(inp["L1_ffn_down_cp"]),
                  idf=idf, cq=cq, sq=sq, ck=ck, sk=sk, msk=msk)
        im.update(hi)
        in_maps.append(im)
    res = run_bass_kernel_spmd(nc, in_maps, core_ids=list(range(NCORES)))
    return np.stack([res.results[c]["y"] for c in range(NCORES)], axis=0)


def kernel(L0_attn_x, L0_attn_Wq, L0_attn_Wk, L0_attn_Wv, L0_attn_Wo, L0_attn_lam,
           L0_ffn_gate_w, L0_ffn_gate_scale, L0_ffn_up_w, L0_ffn_up_scale,
           L0_ffn_down_w, L0_ffn_down_scale,
           L1_attn_W_xy, L1_attn_wi, L1_attn_bi, L1_attn_wa, L1_attn_ba,
           L1_attn_a_param, L1_attn_conv_w, L1_attn_conv_b, L1_attn_W_out,
           L1_ffn_gate_cp, L1_ffn_up_cp, L1_ffn_down_cp):
    inp = dict(L0_attn_x=L0_attn_x, L0_attn_Wq=L0_attn_Wq, L0_attn_Wk=L0_attn_Wk,
               L0_attn_Wv=L0_attn_Wv, L0_attn_Wo=L0_attn_Wo, L0_attn_lam=L0_attn_lam,
               L0_ffn_gate_w=L0_ffn_gate_w, L0_ffn_gate_scale=L0_ffn_gate_scale,
               L0_ffn_up_w=L0_ffn_up_w, L0_ffn_up_scale=L0_ffn_up_scale,
               L0_ffn_down_w=L0_ffn_down_w, L0_ffn_down_scale=L0_ffn_down_scale,
               L1_attn_W_xy=L1_attn_W_xy, L1_attn_wi=L1_attn_wi, L1_attn_bi=L1_attn_bi,
               L1_attn_wa=L1_attn_wa, L1_attn_ba=L1_attn_ba,
               L1_attn_a_param=L1_attn_a_param, L1_attn_conv_w=L1_attn_conv_w,
               L1_attn_conv_b=L1_attn_conv_b, L1_attn_W_out=L1_attn_W_out,
               L1_ffn_gate_cp=L1_ffn_gate_cp, L1_ffn_up_cp=L1_ffn_up_cp,
               L1_ffn_down_cp=L1_ffn_down_cp)
    return kernel_fused(inp).astype(np.float32)


# version 09
# speedup vs baseline: 1.1742x; 1.1742x over previous
import numpy as np
import ml_dtypes
from contextlib import ExitStack
import concourse.bass as bass
import concourse.mybir as mybir
from concourse.bass_utils import run_bass_kernel_spmd

F32 = mybir.dt.float32
BF16 = mybir.dt.bfloat16
I32 = mybir.dt.int32
U8 = mybir.dt.uint8
AF = mybir.ActivationFunctionType
ALU = mybir.AluOpType
AX = mybir.AxisListType

S = 2048
H = 2048
II = 8192

NCORES = 8


class Sched:
    ENGINES = ("sync", "gpsimd", "vector", "scalar", "tensor")

    def __init__(self):
        self.q = {e: [] for e in self.ENGINES}
        self.c = {}

    def bump(self, sem, by=1):
        self.c[sem] = self.c.get(sem, 0) + by
        return self.c[sem]

    def cur(self, sem):
        return self.c.get(sem, 0)

    def dma(self, qeng, sem, out, in_, **kw):
        n = self.bump(sem, 16)
        def fn(e, SM, out=out, in_=in_, sem=sem, kw=kw):
            e.dma_start(out=out, in_=in_, **kw).then_inc(SM[sem], 16)
        self.q[qeng].append(fn)
        return n

    def wait(self, eng, sem, val):
        if val and val > 0:
            def fn(e, SM, sem=sem, val=val):
                e.wait_ge(SM[sem], val)
            self.q[eng].append(fn)

    def op(self, eng, emit, sem=None, inc=1):
        n = None
        if sem is not None:
            n = self.bump(sem, inc)
            def fn(e, SM, emit=emit, sem=sem, inc=inc):
                emit(e).then_inc(SM[sem], inc)
        else:
            def fn(e, SM, emit=emit):
                emit(e)
        self.q[eng].append(fn)
        return n

    def emit(self, block, sems):
        for eng in self.ENGINES:
            if not self.q[eng]:
                continue
            stream = self.q[eng]
            def body(e, stream=stream, sems=sems):
                for fn in stream:
                    fn(e, sems)
            getattr(block, eng)(body)



def t_row(sc, B, src_tile_fn, nk, dst_fn, ident, dst4_fn=None):
    if dst4_fn is not None and B.get("t_row4") and "ptg" in B and nk % 4 == 0:
        for g4 in range(0, nk, 4):
            gp = B["_ptg_ix"][0] % 2
            B["_ptg_ix"][0] += 1
            pt = B["ptg"][gp]
            prev = B["_ptg_evac"][gp]
            if prev:
                sc.wait("tensor", "sa", prev)
            for j in range(4):
                sc.op("tensor", lambda e, pt=pt, j=j, k=g4 + j: e.transpose(
                    pt[:, j * 128:(j + 1) * 128], src_tile_fn(k), ident), "st")
            ntr = sc.cur("st")
            sc.wait("scalar", "st", ntr)
            ne = sc.op("scalar", lambda e, pt=pt, g4=g4: e.copy(dst4_fn(g4), pt[:]), "sa")
            B["_ptg_evac"][gp] = ne
        return sc.cur("sa")
    evh = B.setdefault("_evh", [])
    for k in range(nk):
        pt = B["ptr"][k % 2]
        prev = B["_ptr_evac"][k % 2]
        if prev:
            sc.wait("tensor", "sa", prev)
        sc.op("tensor", lambda e, pt=pt, k=k: e.transpose(pt[:], src_tile_fn(k), ident), "st")
        ntr = sc.cur("st")
        sc.wait("scalar", "st", ntr)
        ne = sc.op("scalar", lambda e, pt=pt, k=k: e.copy(dst_fn(k), pt[:]), "sa")
        B["_ptr_evac"][k % 2] = ne
    return sc.cur("sa")


def emit_const_prep(sc, B, scales):
    n_id = sc.dma("sync", "idm", B["identf"][:], B["in_idf"])
    sc.wait("vector", "idm", n_id)
    nid_b = sc.op("vector", lambda e: e.tensor_copy(B["identb"][:], B["identf"][:]), "sv")
    last_tr = 0
    last_ev = 0
    for (name_in, name_out, nblk) in scales:
        if last_tr:
            sc.wait("sync", "st", last_tr)
        n = sc.dma("sync", "scl", B["stmp"][0:nblk, :],
                   B[name_in].rearrange("(b p) -> b p", p=128))
        sc.wait("tensor", "scl", n)
        sc.wait("tensor", "idm", n_id)
        if last_ev:
            sc.wait("tensor", "sv", last_ev)
        sc.op("tensor", lambda e, nb=nblk: e.transpose(
            B["pscl"][:, 0:nb], B["stmp"][0:nb, :], B["identf"][0:nb, 0:nb]), "st")
        last_tr = sc.cur("st")
        sc.wait("vector", "st", last_tr)
        last_ev = sc.op("vector", lambda e, no=name_out, nb=nblk: e.tensor_copy(
            B[no][:, 0:nb], B["pscl"][:, 0:nb]), "sv")
    return nid_b


def emit_ln_pass(sc, B, x_ap, zdst, tag):
    nb = B.get("xin_nbuf", 2)
    for st16 in range(16):
        db = st16 % nb
        xin = B["xin"][db]
        if st16 >= nb:
            sc.wait("sync", "sv", B.get(f"{tag}_lnv_{st16-nb}", 0))
            sc.wait("sync", "sa", B.get(f"{tag}_lna_{st16-nb}", 0))
        n_dma = sc.dma("sync", f"ldx{db}", xin[:], x_ap[st16 * 128:(st16 + 1) * 128, :])
        sc.wait("scalar", f"ldx{db}", n_dma)
        if st16 >= 1:
            sc.wait("scalar", "sv", B.get(f"{tag}_lnss_{st16-1}", 0))
        sqdst = B["sqtmp"] if "sqtmp" in B else B["xlnb"]
        if "sqtmp" not in B:
            sc.wait("scalar", "st", B.get(f"{tag}_lntr_{st16-1}", 0))
        na = sc.op("scalar", lambda e, xin=xin, sqdst=sqdst: e.activation(
            sqdst[:], xin[:], AF.Square, accum_out=B["st_s"][:]), "sa")
        B[f"{tag}_lna_{st16}"] = na
        sc.wait("vector", f"ldx{db}", n_dma)
        sc.op("vector", lambda e, xin=xin: e.reduce_sum(B["st_m"][:], xin[:], AX.X), "sv")
        sc.wait("vector", "sa", na)
        B[f"{tag}_lnss_{st16}"] = sc.op("vector", lambda e: e.tensor_scalar(
            B["st_v"][:], B["st_s"][:], 1.0 / H, None, ALU.mult), "sv")
        n_ = sc.op("vector", lambda e: e.tensor_scalar(B["st_m"][:], B["st_m"][:], 1.0 / H, None, ALU.mult), "sv")
        sc.wait("vector", "sv", n_)
        n_ = sc.op("vector", lambda e: e.tensor_mul(B["st_t"][:], B["st_m"][:], B["st_m"][:]), "sv")
        sc.wait("vector", "sv", n_)
        n_ = sc.op("vector", lambda e: e.tensor_sub(B["st_v"][:], B["st_v"][:], B["st_t"][:]), "sv")
        sc.wait("vector", "sv", n_)
        nv = sc.op("vector", lambda e: e.tensor_scalar(B["st_v"][:], B["st_v"][:], 1e-5, None, ALU.add), "sv")
        sc.wait("scalar", "sv", nv)
        sc.op("scalar", lambda e: e.activation(B["st_t"][:], B["st_v"][:], AF.Ln), "sa")
        nr = sc.op("scalar", lambda e: e.activation(B["st_r"][:], B["st_t"][:], AF.Exp, scale=-0.5), "sa")
        sc.wait("vector", "sa", nr)
        nl = sc.op("vector", lambda e, xin=xin: e.tensor_scalar(
            B["xlnb"][:], xin[:], B["st_m"][:, 0:1], B["st_r"][:, 0:1], ALU.subtract, ALU.mult), "sv")
        B[f"{tag}_lnv_{st16}"] = nl
        stg = B["sT"][st16 % 2]
        prev_zs = B.get(f"{tag}_lng_{st16-2}", B["_zs_last"][st16 % 2])
        if prev_zs:
            sc.wait("tensor", f"zs{st16 % 2}", prev_zs)
        sc.wait("tensor", "sv", nl)
        nev = t_row(sc, B, lambda k: B["xlnb"][:, k * 128:(k + 1) * 128], 16,
                    lambda k, stg=stg: stg[:, k * 128:(k + 1) * 128], B["identb"][:],
                    dst4_fn=lambda g4, stg=stg: stg[:, g4 * 128:(g4 + 4) * 128])
        B[f"{tag}_lntr_{st16}"] = sc.cur("st")
        sc.wait("gpsimd", "sa", nev)
        ng = sc.dma("gpsimd", f"zs{st16 % 2}",
                    zdst[st16 // 4].rearrange("hc p c -> p hc c")[
                        :, :, (st16 % 4) * 128:(st16 % 4 + 1) * 128],
                    stg[:].rearrange("p (hc c) -> p hc c", c=128))
        B[f"{tag}_lng_{st16}"] = ng
        B["_zs_last"][st16 % 2] = ng


def emit_ffn_weight_prep_int(sc, B, zwgT, zwuT, zwdT):
    mats = [("in_gw", "gsT", zwgT), ("in_uw", "usT", zwuT)]
    k = 0
    for ib in range(64):
        for mi, (wname, sname, zdst) in enumerate(mats):
            db = k % 2
            if k >= 2:
                sc.wait("sync", "sv", B.get(f"dq_{k-2}", 0))
            nd = sc.dma("sync", f"wgu{db}", B["wgu_int"][db][:], B[wname][ib * 128:(ib + 1) * 128, :])
            sc.wait("vector", f"wgu{db}", nd)
            if k >= 2:
                sc.wait("vector", "st", B.get(f"dqtr_{k-2}", 0))
            ndq = sc.op("vector", lambda e, db=db, sname=sname, ib=ib: e.tensor_scalar(
                B["wdq"][db][:], B["wgu_int"][db][:], 128.0, B[sname][:, ib:ib + 1],
                ALU.subtract, ALU.mult), "sv")
            B[f"dq_{k}"] = ndq
            slot = k % 2
            stg = B["sT"][slot]
            if B["_zs_last"][slot]:
                sc.wait("tensor", f"zs{slot}", B["_zs_last"][slot])
            sc.wait("tensor", "sv", ndq)
            nev = t_row(sc, B, lambda kk, db=db: B["wdq"][db][:, kk * 128:(kk + 1) * 128], 16,
                        lambda kk, stg=stg: stg[:, kk * 128:(kk + 1) * 128], B["identb"][:],
                        dst4_fn=lambda g4, stg=stg: stg[:, g4 * 128:(g4 + 4) * 128])
            B[f"dqtr_{k}"] = sc.cur("st")
            sc.wait("gpsimd", "sa", nev)
            nz = sc.dma("gpsimd", f"zs{slot}", zdst[ib, :, :], stg[:])
            B["_zs_last"][slot] = nz
            k += 1

    for hb in range(16):
        for half in range(2):
            if hb == 0 and half == 0:
                sc.wait("sync", "sv", B.get("dq_127", 0))
            if hb + half >= 1:
                sc.wait("sync", "sv", B.get(f"ddq_{hb}_{half-1}", B.get(f"ddq_{hb-1}_{1}", 0)))
            nd = sc.dma("sync", "wd", B["wd_int"][:],
                        B["in_dw"][hb * 128:(hb + 1) * 128, half * 4096:(half + 1) * 4096])
            sc.wait("vector", "wd", nd)
            if hb == 0 and half == 0:
                sc.wait("vector", "st", B.get("dqtr_127", 0))
            if hb >= 1 and half == 0:
                sc.wait("vector", "st", B.get(f"ddqtr_{hb-1}", 0))
            B[f"ddq_{hb}_{half}"] = sc.op("vector", lambda e, hb=hb, half=half: e.tensor_scalar(
                B["wdstg"][:, half * 4096:(half + 1) * 4096], B["wd_int"][:], 128.0,
                B["dsT"][:, hb:hb + 1], ALU.subtract, ALU.mult), "sv")
        ndq = B[f"ddq_{hb}_{1}"]
        if hb >= 1:
            sc.wait("tensor", "zd", B.get(f"dz_{hb-1}", 0))
        sc.wait("tensor", "sv", ndq)
        nev = t_row(sc, B, lambda kk: B["wdstg"][:, kk * 128:(kk + 1) * 128], 64,
                    lambda kk: B["xlnTq"][:, kk * 128:(kk + 1) * 128], B["identb"][:],
                    dst4_fn=lambda g4: B["xlnTq"][:, g4 * 128:(g4 + 4) * 128])
        B[f"ddqtr_{hb}"] = sc.cur("st")
        sc.wait("gpsimd", "sa", nev)
        nz = sc.dma("gpsimd", "zd",
                    zwdT[:, hb // 4].rearrange("ic p c -> p ic c")[
                        :, :, (hb % 4) * 128:(hb % 4 + 1) * 128],
                    B["xlnTq"][:].rearrange("p (ic c) -> p ic c", c=128))
        B[f"dz_{hb}"] = nz
    return sc.cur("zd")


def emit_ffn_matmuls(sc, B, zxlnT, zwgT, zwuT, zwdT, x_ap, y_ap, tag, out_scale=0.5):
    po_sets = B.get("po_sets") or [[B["pg"][0], B["pg"][1], B["pu"][0], B["pu"][1]]]
    nset = len(po_sets)
    nwd = len(B["wdT_sb"])
    WDQ = ("sync", "gpsimd", "scalar")
    gidx = 0
    for q in range(4):
        if q > 0:
            sc.wait("sync", "st", B.get(f"{tag}_amm_{q-1}_63", 0))
        nxq = sc.dma("sync", "xq", B["xlnTq"][:].rearrange("p (hc c) -> p hc c", c=512),
                     zxlnT[q].rearrange("hc p c -> p hc c"))

        for ib in range(64):
            db = ib % 2
            w_prev = B.get(f"{tag}_amm_{q}_{ib-2}", B.get(f"{tag}_amm_{q-1}_{62+ib}", 0))
            if w_prev:
                sc.wait("sync", "st", w_prev)
                sc.wait("gpsimd", "st", w_prev)
            n1 = sc.dma("sync", f"wg{db}", B["wT_sb"][0][db][:], zwgT[ib, :, :])
            n2 = sc.dma("gpsimd", f"wu{db}", B["wT_sb"][1][db][:], zwuT[ib, :, :])
            sc.wait("tensor", f"wg{db}", n1)
            sc.wait("tensor", f"wu{db}", n2)
            sc.wait("tensor", "xq", nxq)
            if ib < 2 and q >= 1:
                for sblk in range(4):
                    sc.wait("tensor", "sv", B.get(f"{tag}_bev_{q-1}_{4-nset}_{sblk}", 0))
            sc.wait("tensor", "sa", B.get(f"{tag}_asil_{q}_{ib-2}", B.get(f"{tag}_asil_{q-1}_{62+ib}", 0)))
            sc.wait("tensor", "sv", B.get(f"{tag}_amul_{q}_{ib-2}", B.get(f"{tag}_amul_{q-1}_{62+ib}", 0)))
            for hc in range(16):
                sc.op("tensor", lambda e, db=db, hc=hc: e.matmul(
                    B["pg"][db][:], B["wT_sb"][0][db][:, hc * 128:(hc + 1) * 128],
                    B["xlnTq"][:, hc * 512:(hc + 1) * 512],
                    start=(hc == 0), stop=(hc == 15)), "st" if hc == 15 else None)
            ng = sc.cur("st")
            for hc in range(16):
                sc.op("tensor", lambda e, db=db, hc=hc: e.matmul(
                    B["pu"][db][:], B["wT_sb"][1][db][:, hc * 128:(hc + 1) * 128],
                    B["xlnTq"][:, hc * 512:(hc + 1) * 512],
                    start=(hc == 0), stop=(hc == 15)), "st" if hc == 15 else None)
            nu = sc.cur("st")
            B[f"{tag}_amm_{q}_{ib}"] = nu
            sc.wait("scalar", "st", ng)
            sc.wait("scalar", "sv", B.get(f"{tag}_amul_{q}_{ib-2}", B.get(f"{tag}_amul_{q-1}_{62+ib}", 0)))
            nsil = sc.op("scalar", lambda e, db=db: e.activation(
                B["sact"][db][:], B["pg"][db][:], AF.Silu), "sa")
            B[f"{tag}_asil_{q}_{ib}"] = nsil
            sc.wait("vector", "st", nu)
            sc.wait("vector", "sa", nsil)
            nmul = sc.op("vector", lambda e, db=db, ib=ib: e.tensor_mul(
                B["actT"][:, ib * 512:(ib + 1) * 512], B["sact"][db][:], B["pu"][db][:]), "sv")
            B[f"{tag}_amul_{q}_{ib}"] = nmul

        for hhs in range(4):
            ps = po_sets[hhs % nset]
            for sblk in range(4):
                prev_ev = B.get(f"{tag}_xru_{gidx-1}", 0)
                if prev_ev:
                    sc.wait("scalar", "sv", prev_ev)
                sc.dma("scalar", "xres", B["xres"][sblk][:],
                       x_ap[q * 512 + sblk * 128: q * 512 + (sblk + 1) * 128,
                            hhs * 512:(hhs + 1) * 512])
            nxres = sc.cur("xres")
            for ic in range(64):
                db = ic % nwd
                qeng = WDQ[db % 3]
                wfree = B.get(f"{tag}_bmm_{q}_{hhs}_{ic-nwd}",
                              B.get(f"{tag}_bmm_{q}_{hhs-1}_{64-nwd+ic}",
                                    B.get(f"{tag}_bmm_{q-1}_3_{64-nwd+ic}", 0)))
                if wfree:
                    sc.wait(qeng, "st", wfree)
                ndw = sc.dma(qeng, f"wdt{db}", B["wdT_sb"][db][:], zwdT[ic, hhs, :, :])
                sc.wait("tensor", f"wdt{db}", ndw)
                sc.wait("tensor", "sv", B.get(f"{tag}_amul_{q}_{ic}", 0))
                if ic == 0 and hhs == 0:
                    sc.wait("tensor", "sa", B.get(f"{tag}_asil_{q}_{63}", 0))
                    sc.wait("tensor", "sv", B.get(f"{tag}_amul_{q}_{63}", 0))
                if ic == 0:
                    if hhs >= nset:
                        for sblk in range(4):
                            sc.wait("tensor", "sv", B.get(f"{tag}_bev_{q}_{hhs-nset}_{sblk}", 0))
                    elif hhs >= 1:
                        for sblk in range(4):
                            sc.wait("tensor", "sv", B.get(f"{tag}_bev_{q-1}_{4-nset+hhs}_{sblk}", 0))
                for sblk in range(4):
                    sc.op("tensor", lambda e, db=db, ic=ic, sblk=sblk, ps=ps: e.matmul(
                        ps[sblk][:], B["actT"][:, ic * 512 + sblk * 128: ic * 512 + (sblk + 1) * 128],
                        B["wdT_sb"][db][:], start=(ic == 0), stop=(ic == 63)),
                        "st" if sblk == 3 else None)
                B[f"{tag}_bmm_{q}_{hhs}_{ic}"] = sc.cur("st")
            nlast = sc.cur("st")
            for sblk in range(4):
                sc.wait("vector", "st", nlast)
                sc.wait("vector", "xres", nxres)
                if gidx > 0:
                    sc.wait("vector", "yqd", 16 * 4 * gidx)
                nev = sc.op("vector", lambda e, sblk=sblk, ps=ps: e.scalar_tensor_tensor(
                    B["yq"][sblk][:], ps[sblk][:], out_scale, B["xres"][sblk][:],
                    ALU.mult, ALU.add), "sv")
                B[f"{tag}_bev_{q}_{hhs}_{sblk}"] = nev
                sc.wait("gpsimd", "sv", nev)
                sc.dma("gpsimd", "yqd", y_ap[q * 512 + sblk * 128: q * 512 + (sblk + 1) * 128,
                                             hhs * 512:(hhs + 1) * 512], B["yq"][sblk][:])
            B[f"{tag}_xru_{gidx}"] = sc.cur("sv")
            gidx += 1



def build_ffn0():
    nc = bass.Bass("TRN2", target_bir_lowering=False, detect_race_conditions=False)
    B = {}
    B["in_x"] = nc.dram_tensor("x", [S, H], F32, kind="ExternalInput").ap()
    B["in_gw"] = nc.dram_tensor("gw", [II, H], I32, kind="ExternalInput").ap()
    B["in_uw"] = nc.dram_tensor("uw", [II, H], I32, kind="ExternalInput").ap()
    B["in_dw"] = nc.dram_tensor("dw", [H, II], I32, kind="ExternalInput").ap()
    B["in_gs"] = nc.dram_tensor("gs", [II], F32, kind="ExternalInput").ap()
    B["in_us"] = nc.dram_tensor("us", [II], F32, kind="ExternalInput").ap()
    B["in_ds"] = nc.dram_tensor("ds", [H], F32, kind="ExternalInput").ap()
    B["in_idf"] = nc.dram_tensor("idf", [128, 128], F32, kind="ExternalInput").ap()
    B["out_y"] = nc.dram_tensor("y", [S, H], F32, kind="ExternalOutput").ap()
    zxlnT = nc.dram_tensor("zxlnT", [4, 16, 128, 512], BF16, kind="Internal").ap()
    zwgT = nc.dram_tensor("zwgT", [64, 128, H], BF16, kind="Internal").ap()
    zwuT = nc.dram_tensor("zwuT", [64, 128, H], BF16, kind="Internal").ap()
    zwdT = nc.dram_tensor("zwdT", [64, 4, 128, 512], BF16, kind="Internal").ap()

    with ExitStack() as es:
        E = es.enter_context
        sb = lambda n, shp, dt: E(nc.sbuf_tensor(n, shp, dt))
        B["identf"] = sb("identf", [128, 128], F32)
        B["identb"] = sb("identb", [128, 128], BF16)
        B["stmp"] = sb("stmp", [128, 128], F32)
        B["gsT"] = sb("gsT", [128, 64], F32)
        B["usT"] = sb("usT", [128, 64], F32)
        B["dsT"] = sb("dsT", [128, 16], F32)
        B["xin"] = [sb(f"xin{i}", [128, H], F32) for i in range(2)]
        B["sqtmp"] = sb("sqtmp", [128, H], BF16)
        B["xlnb"] = sb("xlnb", [128, H], BF16)
        B["sT"] = [sb(f"sT{i}", [128, H], BF16) for i in range(2)]
        for nm in ("st_m", "st_s", "st_v", "st_t", "st_r"):
            B[nm] = sb(nm, [128, 1], F32)
        B["wgu_int"] = [sb(f"wgu{i}", [128, H], I32) for i in range(2)]
        B["wdq"] = [sb(f"wdq{i}", [128, H], BF16) for i in range(2)]
        B["wd_int"] = sb("wd_int", [128, II // 2], I32)
        B["wdstg"] = sb("wdstg", [128, II], BF16)
        B["xlnTq"] = sb("xlnTq", [128, 16 * 512], BF16)
        B["wT_sb"] = [[sb(f"wT{m}{i}", [128, H], BF16) for i in range(2)] for m in range(2)]
        B["actT"] = sb("actT", [128, 64 * 512], BF16)
        B["wdT_sb"] = [sb(f"wdT{i}", [128, 512], BF16) for i in range(2)]
        B["xres"] = [sb(f"xres{i}", [128, 512], F32) for i in range(4)]
        B["yq"] = [sb(f"yq{i}", [128, 512], F32) for i in range(4)]
        B["sact"] = [sb(f"sact{i}", [128, 512], BF16) for i in range(2)]
        B["ptr"] = [E(nc.psum_tensor(f"ptr{i}", [128, 128], BF16)) for i in range(2)]
        B["pscl"] = E(nc.psum_tensor("pscl", [128, 64], F32))
        B["pg"] = [E(nc.psum_tensor(f"pg{i}", [128, 512], F32)) for i in range(2)]
        B["pu"] = [E(nc.psum_tensor(f"pu{i}", [128, 512], F32)) for i in range(2)]
        B["_ptr_evac"] = [0, 0]
        B["_zs_last"] = [0, 0]

        sc = Sched()
        emit_const_prep(sc, B, [("in_gs", "gsT", 64), ("in_us", "usT", 64), ("in_ds", "dsT", 16)])
        emit_ln_pass(sc, B, B["in_x"], zxlnT, "ln0")
        emit_ffn_weight_prep_int(sc, B, zwgT, zwuT, zwdT)
        for sem in ("zs0", "zs1", "zd"):
            sc.wait("sync", sem, sc.cur(sem))
        emit_ffn_matmuls(sc, B, zxlnT, zwgT, zwuT, zwdT, B["in_x"], B["out_y"], "f0")
        for sem in sorted(sc.c.keys()):
            sc.wait("sync", sem, sc.cur(sem))
        sems = {name: E(nc.semaphore(name)) for name in sorted(sc.c.keys())}
        block = E(nc.Block())
        sc.emit(block, sems)
    return nc


_CACHE = {}

def get_nc(name, builder):
    if name not in _CACHE:
        _CACHE[name] = builder()
    return _CACHE[name]


def run_ffn0(x_full, gw, uw, dw, gs, us, ds, cores=None):
    nc = get_nc("ffn0", build_ffn0)
    idf = np.eye(128, dtype=np.float32)
    cores = cores or list(range(NCORES))
    in_maps = []
    for c in cores:
        in_maps.append(dict(x=np.ascontiguousarray(x_full[c]), gw=gw, uw=uw, dw=dw,
                            gs=gs, us=us, ds=ds, idf=idf))
    res = run_bass_kernel_spmd(nc, in_maps, core_ids=cores)
    return np.stack([res.results[i]["y"] for i in range(len(cores))], axis=0)



def emit_ln_sbuf(sc, B, x_ap, dstbuf, tag):
    nb = B.get("xin_nbuf", 2)
    for st16 in range(16):
        db = st16 % nb
        xin = B["xin"][db]
        if st16 >= nb:
            sc.wait("sync", "sv", B.get(f"{tag}_lnv_{st16-nb}", 0))
            sc.wait("sync", "sa", B.get(f"{tag}_lna_{st16-nb}", 0))
        n_dma = sc.dma("sync", f"ldx{db}", xin[:], x_ap[st16 * 128:(st16 + 1) * 128, :])
        sc.wait("scalar", f"ldx{db}", n_dma)
        if st16 >= 1:
            sc.wait("scalar", "sv", B.get(f"{tag}_lnss_{st16-1}", 0))
        sqdst = B["sqtmp"] if "sqtmp" in B else B["xlnb"]
        if "sqtmp" not in B:
            sc.wait("scalar", "st", B.get(f"{tag}_lntr_{st16-1}", 0))
        na = sc.op("scalar", lambda e, xin=xin, sqdst=sqdst: e.activation(
            sqdst[:], xin[:], AF.Square, accum_out=B["st_s"][:]), "sa")
        B[f"{tag}_lna_{st16}"] = na
        sc.wait("vector", f"ldx{db}", n_dma)
        sc.op("vector", lambda e, xin=xin: e.reduce_sum(B["st_m"][:], xin[:], AX.X), "sv")
        sc.wait("vector", "sa", na)
        B[f"{tag}_lnss_{st16}"] = sc.op("vector", lambda e: e.tensor_scalar(
            B["st_v"][:], B["st_s"][:], 1.0 / H, None, ALU.mult), "sv")
        n_ = sc.op("vector", lambda e: e.tensor_scalar(B["st_m"][:], B["st_m"][:], 1.0 / H, None, ALU.mult), "sv")
        sc.wait("vector", "sv", n_)
        n_ = sc.op("vector", lambda e: e.tensor_mul(B["st_t"][:], B["st_m"][:], B["st_m"][:]), "sv")
        sc.wait("vector", "sv", n_)
        n_ = sc.op("vector", lambda e: e.tensor_sub(B["st_v"][:], B["st_v"][:], B["st_t"][:]), "sv")
        sc.wait("vector", "sv", n_)
        nv = sc.op("vector", lambda e: e.tensor_scalar(B["st_v"][:], B["st_v"][:], 1e-5, None, ALU.add), "sv")
        sc.wait("scalar", "sv", nv)
        sc.op("scalar", lambda e: e.activation(B["st_t"][:], B["st_v"][:], AF.Ln), "sa")
        nr = sc.op("scalar", lambda e: e.activation(B["st_r"][:], B["st_t"][:], AF.Exp, scale=-0.5), "sa")
        sc.wait("vector", "sa", nr)
        nl = sc.op("vector", lambda e, xin=xin: e.tensor_scalar(
            B["xlnb"][:], xin[:], B["st_m"][:, 0:1], B["st_r"][:, 0:1], ALU.subtract, ALU.mult), "sv")
        B[f"{tag}_lnv_{st16}"] = nl
        sc.wait("tensor", "sv", nl)
        t_row(sc, B, lambda k: B["xlnb"][:, k * 128:(k + 1) * 128], 16,
              lambda k, st16=st16: dstbuf[:, k * 2048 + st16 * 128:k * 2048 + (st16 + 1) * 128],
              B["identb"][:])
        B[f"{tag}_lntr_{st16}"] = sc.cur("st")
    return sc.cur("sa")


def emit_attn0(sc, B, nc, debug=False):
    n_id = sc.dma("sync", "idm", B["identf"][:], B["in_idf"])
    n_lam = sc.dma("sync", "idm", B["lamB"][:], B["in_lam"])
    n_msk = sc.dma("sync", "idm", B["mskb"][:], B["in_msk"])
    sc.wait("vector", "idm", n_msk)
    sc.op("vector", lambda e: e.tensor_copy(B["identb"][:], B["identf"][:]), "sv")

    ln_tail_sa = emit_ln_sbuf(sc, B, B["in_x"], B["bigT"], "lna")
    sc.wait("tensor", "sa", ln_tail_sa)

    qkv_jobs = [("q", B["in_wq"], 4, True, "cq", "sq"),
                ("k", B["in_wk"], 1, True, "ck", "sk"),
                ("v", B["in_wv"], 1, False, None, None)]
    for jname, wap, nslab, rope, ctab, stab in qkv_jobs:
        for qds in range(nslab):
            prev = B.get("w_consumed", 0)
            if prev:
                sc.wait("gpsimd", "st", prev)
            nw = sc.dma("gpsimd", "wld", B["wsb"][:].rearrange("p (hc c) -> p hc c", c=512),
                        wap.rearrange("(hc p) c -> p hc c", p=128)[:, :, qds * 512:(qds + 1) * 512])
            for st16 in range(16):
                db = st16 % 2
                pq = B["pqkv"][db]
                if rope:
                    nrp = B.get("rope_used", 0)
                    if nrp:
                        sc.wait("sync", "sv", nrp)
                    nct = sc.dma("sync", "rpt", B["ropc"][:], B[f"in_{ctab}"][st16, :, :])
                    nst = sc.dma("sync", "rpt", B["rops"][:], B[f"in_{stab}"][st16, :, :])
                sc.wait("tensor", "wld", nw)
                prev_ev = B.get(f"qkv_ev_{jname}_{qds}_{st16-2}",
                                B.get(f"qkv_ev_{jname}_{qds-1}_{14+st16}", B.get("qkv_prev_tail", 0)))
                if prev_ev:
                    sc.wait("tensor", "sv", prev_ev)
                for hc in range(16):
                    sc.op("tensor", lambda e, pq=pq, hc=hc, st16=st16: e.matmul(
                        pq[:], B["bigT"][:, hc * 2048 + st16 * 128:hc * 2048 + (st16 + 1) * 128],
                        B["wsb"][:, hc * 512:(hc + 1) * 512],
                        start=(hc == 0), stop=(hc == 15)), "st" if hc == 15 else None)
                nmm = sc.cur("st")
                B[f"qkv_mm_{jname}_{qds}_{st16}"] = nmm
                sc.wait("vector", "st", nmm)
                if rope:
                    sc.wait("vector", "rpt", nst)
                    sc.op("vector", lambda e, pq=pq: e.tensor_mul(B["qa"][:], pq[:], B["ropc"][:]), "sv")
                    pqv = pq[:].rearrange("p (h d) -> p h d", d=128)
                    qbv = B["qb"][:].rearrange("p (h d) -> p h d", d=128)
                    snv = B["rops"][:].rearrange("p (h d) -> p h d", d=128)
                    sc.op("vector", lambda e, pqv=pqv, qbv=qbv, snv=snv: e.tensor_mul(
                        qbv[:, :, 0:64], pqv[:, :, 64:128], snv[:, :, 0:64]), "sv")
                    nrope = sc.op("vector", lambda e, pqv=pqv, qbv=qbv, snv=snv: e.tensor_mul(
                        qbv[:, :, 64:128], pqv[:, :, 0:64], snv[:, :, 64:128]), "sv")
                    B["rope_used"] = nrope
                    nqr = sc.op("vector", lambda e: e.tensor_add(B["qr"][:], B["qa"][:], B["qb"][:]), "sv")
                    B[f"qkv_ev_{jname}_{qds}_{st16}"] = nqr
                    sc.wait("tensor", "sv", nqr)
                    if jname == "q":
                        t_row(sc, B, lambda kk: B["qr"][:, kk * 128:(kk + 1) * 128], 4,
                              lambda kk, qds=qds, st16=st16: B["qT"][
                                  :, (qds * 4 + kk) * 2048 + st16 * 128:(qds * 4 + kk) * 2048 + (st16 + 1) * 128],
                              B["identb"][:])
                    else:
                        t_row(sc, B, lambda kk: B["qr"][:, kk * 128:(kk + 1) * 128], 4,
                              lambda kk, st16=st16: B["kT"][
                                  :, kk * 2048 + st16 * 128:kk * 2048 + (st16 + 1) * 128],
                              B["identb"][:])
                else:
                    for kvh in range(4):
                        nev = sc.op("vector", lambda e, pq=pq, kvh=kvh, st16=st16: e.tensor_copy(
                            B["vP"][:, kvh * 2048 + st16 * 128:kvh * 2048 + (st16 + 1) * 128],
                            pq[:, kvh * 128:(kvh + 1) * 128]), "sv")
                    B[f"qkv_ev_{jname}_{qds}_{st16}"] = nev
            B["w_consumed"] = B[f"qkv_mm_{jname}_{qds}_{15}"]
        B["qkv_prev_tail"] = max(B.get(f"qkv_ev_{jname}_{nslab-1}_{14}", 0),
                                 B.get(f"qkv_ev_{jname}_{nslab-1}_{15}", 0))

    qkv_tail_sv = sc.cur("sv")
    qkv_tail_sa = sc.cur("sa")

    chains = [dict(psc=B["psc"], pav=B["pav"], zsb=B["zsb"], esb=B["esb"],
                   dif=B["dif"], aout=B["aout"], sm=B["sm"][0:8], ci=0)]
    if "psc2" in B:
        chains.append(dict(psc=B["psc2"], pav=B["pav2"], zsb=B["zsb2"], esb=B["esb2"],
                           dif=B["dif2"], aout=B["aout2"], sm=B["sm"][8:16], ci=1))
    nch = len(chains)

    def seg1(CB, h, qb, first):
        ci = CB["ci"]
        kvh = h // 4
        c0 = max(0, 4 - qb)
        for half in range(2):
            base = half * 64
            pv = B.get(f"at_z_prev_{ci}", 0)
            if first and half == 0:
                sc.wait("tensor", "sv", qkv_tail_sv)
                sc.wait("tensor", "sa", qkv_tail_sa)
            if pv:
                sc.wait("tensor", "sv", pv)
            for c in range(c0, 5):
                sc.op("tensor", lambda e, base=base, h=h, qb=qb, kvh=kvh, c=c, CB=CB: e.matmul(
                    CB["psc"][:, c * 128:(c + 1) * 128],
                    B["qT"][base:base + 64, h * 2048 + qb * 128:h * 2048 + (qb + 1) * 128],
                    B["kT"][base:base + 64, kvh * 2048 + (qb + c - 4) * 128:kvh * 2048 + (qb + c - 3) * 128],
                    start=True, stop=True), "st" if c == 4 else None)
            nsc = sc.cur("st")
            sc.wait("vector", "st", nsc)
            sc.wait("vector", "sa", B.get(f"at_exp_prev_{ci}", 0))
            if c0 > 0:
                sc.op("vector", lambda e, c0=c0, CB=CB: e.memset(
                    CB["psc"][:, 0:c0 * 128], -1e30), "sv")
            nz = sc.op("vector", lambda e, CB=CB: e.tensor_add(
                CB["zsb"][:], CB["psc"][:], B["mskb"][:]), "sv")
            B[f"at_z_prev_{ci}"] = nz
            sc.wait("vector", "sv", nz)
            sc.op("vector", lambda e, half=half, CB=CB: e.reduce_max(
                CB["sm"][half][:], CB["zsb"][:], AX.X, negate=True), "sv")
            nmx = sc.cur("sv")
            sc.wait("scalar", "sv", nmx)
            ne = sc.op("scalar", lambda e, half=half, CB=CB: e.activation(
                CB["esb"][half][:], CB["zsb"][:], AF.Exp,
                bias=CB["sm"][half][:, 0:1], accum_out=CB["sm"][2 + half][:]), "sa")
            B[f"at_exp_prev_{ci}"] = ne

    def seg2(CB, h):
        ci = CB["ci"]
        sm = CB["sm"]
        sc.wait("vector", "sa", B[f"at_exp_prev_{ci}"])
        n_ = sc.op("vector", lambda e, sm=sm: e.reciprocal(sm[4][:], sm[2][:]), "sv")
        n_ = sc.op("vector", lambda e, sm=sm: e.reciprocal(sm[5][:], sm[3][:]), "sv")
        sc.wait("vector", "sv", n_)
        n_ = sc.op("vector", lambda e, h=h, sm=sm: e.tensor_scalar(
            sm[5][:], sm[5][:], B["lamB"][:, h:h + 1], None, ALU.mult), "sv")
        sc.wait("vector", "sv", n_)
        n_ = sc.op("vector", lambda e, CB=CB, sm=sm: e.tensor_scalar(
            CB["esb"][1][:], CB["esb"][1][:], sm[5][:, 0:1], None, ALU.mult), "sv")
        sc.wait("vector", "sv", n_)
        n_ = sc.op("vector", lambda e, CB=CB, sm=sm: e.scalar_tensor_tensor(
            CB["dif"][:], CB["esb"][0][:], sm[4][:, 0:1], CB["esb"][1][:],
            ALU.mult, ALU.subtract), "sv")
        sc.wait("vector", "sv", n_)
        n_ = sc.op("vector", lambda e, CB=CB: e.tensor_scalar(
            CB["dif"][:], CB["dif"][:], 0.0, None, ALU.max), "sv")
        sc.wait("vector", "sv", n_)
        n_ = sc.op("vector", lambda e, CB=CB, sm=sm: e.reduce_sum(
            sm[6][:], CB["dif"][:], AX.X), "sv")
        sc.wait("vector", "sv", n_)
        n_ = sc.op("vector", lambda e, sm=sm: e.tensor_scalar(
            sm[6][:], sm[6][:], 1e-6, None, ALU.add), "sv")
        sc.wait("vector", "sv", n_)
        ndif = sc.op("vector", lambda e, sm=sm: e.reciprocal(sm[7][:], sm[6][:]), "sv")
        B[f"at_dif_{ci}"] = ndif

    def seg3(CB, h, qb):
        ci = CB["ci"]
        kvh = h // 4
        c0 = max(0, 4 - qb)
        sc.wait("tensor", "sv", B[f"at_dif_{ci}"])
        ndt = t_row(sc, B, lambda kk, c0=c0, CB=CB: CB["dif"][:, (c0 + kk) * 128:(c0 + kk + 1) * 128],
                    5 - c0,
                    lambda kk, c0=c0, CB=CB: CB["dif"][:, (c0 + kk) * 128:(c0 + kk + 1) * 128],
                    B["identb"][:])
        sc.wait("tensor", "sa", ndt)
        pvav = B.get(f"at_av_prev_{ci}", 0)
        if pvav:
            sc.wait("tensor", "sv", pvav)
        for c in range(c0, 5):
            sc.op("tensor", lambda e, c=c, kvh=kvh, qb=qb, c0=c0, CB=CB: e.matmul(
                CB["pav"][:], CB["dif"][:, c * 128:(c + 1) * 128],
                B["vP"][:, kvh * 2048 + (qb + c - 4) * 128:kvh * 2048 + (qb + c - 3) * 128],
                start=(c == c0), stop=(c == 4)), "st" if c == 4 else None)
        nav = sc.cur("st")
        sc.wait("vector", "st", nav)
        nao = sc.op("vector", lambda e, CB=CB: e.tensor_scalar(
            CB["aout"][:], CB["pav"][:], CB["sm"][7][:, 0:1], None, ALU.mult), "sv")
        B[f"at_av_prev_{ci}"] = nao
        sc.wait("tensor", "sv", nao)
        t_row(sc, B, lambda kk, CB=CB: CB["aout"][:], 1,
              lambda kk, h=h, qb=qb: B["bigT"][:, h * 2048 + qb * 128:h * 2048 + (qb + 1) * 128],
              B["identb"][:])

    tiles = [(h, qb) for h in range(16) for qb in range(16)]
    i = 0
    while i < len(tiles):
        grp = [(chains[j], tiles[i + j]) for j in range(min(nch, len(tiles) - i))]
        for (CB, (h, qb)) in grp:
            seg1(CB, h, qb, first=(i == 0 and CB["ci"] == 0))
        for (CB, (h, qb)) in grp:
            seg2(CB, h)
        for (CB, (h, qb)) in grp:
            seg3(CB, h, qb)
        i += len(grp)

    attn_tail_sa = sc.cur("sa")

    sc.wait("sync", "sv", sc.cur("sv"))
    sc.wait("sync", "sa", sc.cur("sa"))
    sc.wait("gpsimd", "sv", sc.cur("sv"))
    sc.wait("gpsimd", "sa", sc.cur("sa"))
    sc.wait("gpsimd", "st", sc.cur("st"))
    for hs in range(4):
        prev = B.get("wo_consumed", B.get("w_consumed", 0))
        if prev:
            sc.wait("gpsimd", "st", prev)
        nw = sc.dma("gpsimd", "wld", B["wsb"][:].rearrange("p (hc c) -> p hc c", c=512),
                    B["in_wo"].rearrange("(hc p) c -> p hc c", p=128)[:, :, hs * 512:(hs + 1) * 512])
        for st16 in range(16):
            db = st16 % 2
            pq = B["pqkv"][db]
            prev_ev = B.get(f"wo_ev_{hs}_{st16-1}", B.get(f"wo_ev_{hs-1}_{15}", 0))
            if prev_ev:
                sc.wait("sync", "sv", prev_ev)
            nxr = sc.dma("sync", "xres", B["xres"][0][:],
                         B["in_x"][st16 * 128:(st16 + 1) * 128, hs * 512:(hs + 1) * 512])
            sc.wait("tensor", "wld", nw)
            if st16 < 2 and hs == 0:
                sc.wait("tensor", "sa", attn_tail_sa)
            pv2 = B.get(f"wo_ev_{hs}_{st16-2}", B.get(f"wo_ev_{hs-1}_{14+st16}", 0))
            if pv2:
                sc.wait("tensor", "sv", pv2)
            for hc in range(16):
                sc.op("tensor", lambda e, pq=pq, hc=hc, st16=st16: e.matmul(
                    pq[:], B["bigT"][:, hc * 2048 + st16 * 128:hc * 2048 + (st16 + 1) * 128],
                    B["wsb"][:, hc * 512:(hc + 1) * 512],
                    start=(hc == 0), stop=(hc == 15)), "st" if hc == 15 else None)
            nmm = sc.cur("st")
            B[f"wo_mm_{hs}_{st16}"] = nmm
            sc.wait("vector", "st", nmm)
            sc.wait("vector", "xres", nxr)
            sc.wait("vector", "yqd", sc.cur("yqd"))
            nev = sc.op("vector", lambda e, pq=pq: e.tensor_add(
                B["yq"][0][:], pq[:], B["xres"][0][:]), "sv")
            B[f"wo_ev_{hs}_{st16}"] = nev
            sc.wait("gpsimd", "sv", nev)
            sc.dma("gpsimd", "yqd", B["out_y"][st16 * 128:(st16 + 1) * 128, hs * 512:(hs + 1) * 512],
                   B["yq"][0][:])
        B["wo_consumed"] = B[f"wo_mm_{hs}_{15}"]



def build_attn0(debug=False):
    nc = bass.Bass("TRN2", target_bir_lowering=False, detect_race_conditions=False)
    B = {}
    B["in_x"] = nc.dram_tensor("x", [S, H], F32, kind="ExternalInput").ap()
    B["in_wq"] = nc.dram_tensor("wq", [H, 2048], F32, kind="ExternalInput").ap()
    B["in_wk"] = nc.dram_tensor("wk", [H, 512], F32, kind="ExternalInput").ap()
    B["in_wv"] = nc.dram_tensor("wv", [H, 512], F32, kind="ExternalInput").ap()
    B["in_wo"] = nc.dram_tensor("wo", [2048, H], F32, kind="ExternalInput").ap()
    B["in_lam"] = nc.dram_tensor("lamb", [128, 16], F32, kind="ExternalInput").ap()
    B["in_idf"] = nc.dram_tensor("idf", [128, 128], F32, kind="ExternalInput").ap()
    B["in_cq"] = nc.dram_tensor("cq", [16, 128, 512], BF16, kind="ExternalInput").ap()
    B["in_sq"] = nc.dram_tensor("sq", [16, 128, 512], BF16, kind="ExternalInput").ap()
    B["in_ck"] = nc.dram_tensor("ck", [16, 128, 512], BF16, kind="ExternalInput").ap()
    B["in_sk"] = nc.dram_tensor("sk", [16, 128, 512], BF16, kind="ExternalInput").ap()
    B["in_msk"] = nc.dram_tensor("msk", [128, 640], BF16, kind="ExternalInput").ap()
    B["out_y"] = nc.dram_tensor("y", [S, H], F32, kind="ExternalOutput").ap()

    with ExitStack() as es:
        E = es.enter_context
        sb = lambda n, shp, dt: E(nc.sbuf_tensor(n, shp, dt))
        B["identf"] = sb("identf", [128, 128], F32)
        B["identb"] = sb("identb", [128, 128], BF16)
        _xin = sb("xin0", [128, H], F32)
        B["xin"] = [_xin]
        B["xin_nbuf"] = 1
        B["xlnb"] = sb("xlnb", [128, H], BF16)
        for nm in ("st_m", "st_s", "st_v", "st_t", "st_r"):
            B[nm] = sb(nm, [128, 1], F32)
        B["bigT"] = sb("bigT", [128, 16 * 2048], BF16)
        B["qT"] = sb("qT", [128, 16 * 2048], BF16)
        B["kT"] = sb("kT", [128, 4 * 2048], BF16)
        B["vP"] = sb("vP", [128, 4 * 2048], BF16)
        B["wsb"] = sb("wsb", [128, 16 * 512], BF16)
        B["ropc"] = sb("ropc", [128, 512], BF16)
        B["rops"] = sb("rops", [128, 512], BF16)
        B["qa"] = sb("qa", [128, 512], BF16)
        B["qb"] = sb("qb", [128, 512], BF16)
        B["qr"] = sb("qr", [128, 512], BF16)
        B["lamB"] = sb("lamB", [128, 16], F32)
        B["mskb"] = sb("mskb", [128, 640], BF16)
        B["zsb"] = sb("zsb", [128, 640], F32)
        B["esb"] = [sb(f"esb{i}", [128, 640], BF16) for i in range(2)]

        B["dif"] = sb("dif", [128, 640], BF16)
        B["aout"] = sb("aout", [128, 128], BF16)
        B["sm"] = [sb(f"sm{i}", [128, 1], F32) for i in range(8)]
        B["xres"] = [sb("xres0", [128, 512], F32)]
        B["yq"] = [sb("yq0", [128, 512], F32)]
        B["ptr"] = [E(nc.psum_tensor(f"ptr{i}", [128, 128], BF16)) for i in range(2)]
        B["pqkv"] = [E(nc.psum_tensor(f"pqkv{i}", [128, 512], F32)) for i in range(2)]
        B["psc"] = E(nc.psum_tensor("psc", [128, 640], F32))
        B["pav"] = E(nc.psum_tensor("pav", [128, 128], F32))
        B["_ptr_evac"] = [0, 0]
        B["_zs_last"] = [0, 0]

        sc = Sched()
        emit_attn0(sc, B, nc, debug=debug)
        if debug:
            dbg = {}
            for nm, buf, w in [("dqT", "qT", 32768), ("dkT", "kT", 8192),
                               ("dvP", "vP", 8192), ("dbigT", "bigT", 32768)]:
                dap = nc.dram_tensor(nm, [128, w], BF16, kind="ExternalOutput").ap()
                sc.dma("gpsimd", "yqd", dap, B[buf][:])
        for sem in sorted(sc.c.keys()):
            sc.wait("sync", sem, sc.cur(sem))
        sems = {name: E(nc.semaphore(name)) for name in sorted(sc.c.keys())}
        block = E(nc.Block())
        sc.emit(block, sems)
    return nc


def make_attn0_consts():
    hd = 128
    inv = 1.0 / (10000.0 ** (np.arange(0, hd, 2, dtype=np.float64) / hd))
    t = np.arange(S, dtype=np.float64)
    fr = np.outer(t, inv)
    emb = np.concatenate([fr, fr], axis=-1)
    cos = np.cos(emb).astype(np.float32)
    sin = np.sin(emb).astype(np.float32)
    sneg = sin.copy()
    sneg[:, 0:64] *= -1.0
    scale = 1.0 / np.sqrt(64.0)
    def tile(tab, sc_):
        tt = (tab * sc_).astype(np.float32)
        rep = np.tile(tt, (1, 4))
        return rep.reshape(16, 128, 512)
    import ml_dtypes
    cq = tile(cos, scale).astype(ml_dtypes.bfloat16); sq = tile(sneg, scale).astype(ml_dtypes.bfloat16)
    ck = tile(cos, 1.0).astype(ml_dtypes.bfloat16); sk = tile(sneg, 1.0).astype(ml_dtypes.bfloat16)
    r = np.arange(128)[:, None]
    c = np.arange(640)[None, :]
    allowed = (c >= r + 1) & (c <= r + 512)
    msk = np.where(allowed, 0.0, -1e30).astype(np.float32)
    return cq, sq, ck, sk, msk.astype(ml_dtypes.bfloat16)


def run_attn0(x_full, wq, wk, wv, wo, lam, cores=None):
    nc = get_nc("attn0", build_attn0)
    idf = np.eye(128, dtype=np.float32)
    cq, sq, ck, sk, msk = make_attn0_consts()
    lamB = np.tile(lam[None, :], (128, 1)).astype(np.float32)
    cores = cores or list(range(NCORES))
    in_maps = []
    for c in cores:
        in_maps.append(dict(x=np.ascontiguousarray(x_full[c]), wq=wq, wk=wk, wv=wv,
                            wo=wo, lamb=lamB, idf=idf, cq=cq, sq=sq, ck=ck, sk=sk, msk=msk))
    res = run_bass_kernel_spmd(nc, in_maps, core_ids=cores)
    return np.stack([res.results[i]["y"] for i in range(len(cores))], axis=0)



def emit_attn1(sc, B, nc, debug=False):
    n_id = sc.dma("sync", "idm", B["identf"][:], B["in_idf"])
    sc.wait("vector", "idm", n_id)
    sc.op("vector", lambda e: e.tensor_copy(B["identb"][:], B["identf"][:]), "sv")

    ln_tail_sa = emit_ln_sbuf(sc, B, B["in_x"], B["bigT"], "lnr")

    for rb in range(32):
        db = rb % 2
        if rb >= 2:
            sc.wait("gpsimd", "st", B.get(f"p1tr_{rb-2}", 0))
        nw = sc.dma("gpsimd", f"wxl{db}", B["bfp"][db][:],
                    B["in_wxy"][rb * 128:(rb + 1) * 128, :])
        stg = B["bfp"][2]
        if rb >= 1:
            sc.wait("tensor", f"zs0", B["_zs_last"][0])
        sc.wait("tensor", f"wxl{db}", nw)
        nev = t_row(sc, B, lambda k, db=db: B["bfp"][db][:, k * 128:(k + 1) * 128], 16,
                    lambda k, stg=stg: stg[:, k * 128:(k + 1) * 128], B["identb"][:])
        B[f"p1tr_{rb}"] = sc.cur("st")
        sc.wait("gpsimd", "sa", nev)
        B["_zs_last"][0] = sc.dma("gpsimd", "zs0", B["zwxyT"][rb, :, :], stg[:])
    for hb in range(16):
        db = hb % 2
        if hb >= 2:
            sc.wait("gpsimd", "st", B.get(f"p2tr_{hb-2}", 0))
        nw = sc.dma("gpsimd", f"wxl{db}", B["bfp"][db][:],
                    B["in_wo"][hb * 128:(hb + 1) * 128, :])
        stg = B["bfp"][2]
        sc.wait("tensor", "zs0", B["_zs_last"][0])
        sc.wait("tensor", f"wxl{db}", nw)
        nev = t_row(sc, B, lambda k, db=db: B["bfp"][db][:, k * 128:(k + 1) * 128], 16,
                    lambda k, stg=stg: stg[:, k * 128:(k + 1) * 128], B["identb"][:])
        B[f"p2tr_{hb}"] = sc.cur("st")
        sc.wait("gpsimd", "sa", nev)
        B["_zs_last"][0] = sc.dma("gpsimd", "zs0",
                                  B["zwoT"].rearrange("rc p c -> p rc c")[:, :, hb * 128:(hb + 1) * 128],
                                  stg[:].rearrange("p (rc c) -> p rc c", c=128))
    n1 = sc.dma("sync", "stt", B["fpp"][1][0:4, :], B["in_cw"])
    sc.wait("tensor", "stt", n1)
    sc.wait("tensor", "idm", n_id)
    prevev = 0
    for rc in range(16):
        sc.wait("tensor", "sv", prevev)
        sc.op("tensor", lambda e, rc=rc: e.transpose(
            B["psmall"][:, 0:4], B["fpp"][1][0:4, rc * 128:(rc + 1) * 128], B["identf"][0:4, 0:4]), "st")
        ntr = sc.cur("st")
        sc.wait("vector", "st", ntr)
        prevev = sc.op("vector", lambda e, rc=rc: e.tensor_copy(
            B["cwT"][:, rc * 4:(rc + 1) * 4], B["psmall"][:, 0:4]), "sv")
    for (name, src) in (("biT", B["in_bi"]), ("baT", B["in_ba"]),
                        ("apT", B["in_ap"].rearrange("(a b) -> a b", b=128)),
                        ("cbT", B["in_cb"].rearrange("(a b) -> a b", b=128))):
        sc.wait("sync", "st", sc.cur("st"))
        nt = sc.dma("sync", "stt", B["fpp"][1][0:16, 0:128], src)
        sc.wait("tensor", "stt", nt)
        sc.wait("tensor", "sv", prevev)
        sc.op("tensor", lambda e: e.transpose(
            B["psmall"][:, 0:16], B["fpp"][1][0:16, 0:128], B["identf"][0:16, 0:16]), "st")
        ntr = sc.cur("st")
        sc.wait("vector", "st", ntr)
        prevev = sc.op("vector", lambda e, name=name: e.tensor_copy(
            B[name][:], B["psmall"][:, 0:16]), "sv")
    sc.wait("scalar", "sv", prevev)
    na = sc.op("scalar", lambda e: e.activation(B["t16a"][:], B["apT"][:], AF.Abs), "sa")
    sc.wait("scalar", "sa", na)
    na = sc.op("scalar", lambda e: e.activation(B["t16a"][:], B["t16a"][:], AF.Exp, scale=-1.0), "sa")
    sc.wait("scalar", "sa", na)
    na = sc.op("scalar", lambda e: e.activation(B["t16a"][:], B["t16a"][:], AF.Ln, bias=1.0), "sa")
    sc.wait("vector", "sa", na)
    nv = sc.op("vector", lambda e: e.tensor_scalar(B["t16b"][:], B["apT"][:], 0.0, None, ALU.max), "sv")
    sc.wait("vector", "sv", nv)
    nv = sc.op("vector", lambda e: e.tensor_add(B["t16b"][:], B["t16b"][:], B["t16a"][:]), "sv")
    sc.wait("vector", "sv", nv)
    nv = sc.op("vector", lambda e: e.tensor_scalar(B["m8T"][:], B["t16b"][:], -8.0, None, ALU.mult), "sv")
    m8_done = nv

    prep_st = sc.cur("st")
    sc.wait("tensor", "sa", ln_tail_sa)
    for rb in range(32):
        db = rb % 2
        wprev = B.get(f"xy_mm_{rb-2}", prep_st)
        if wprev:
            sc.wait("gpsimd", "st", wprev)
        nw = sc.dma("gpsimd", f"wxl{db}", B["bfp"][db][:], B["zwxyT"][rb, :, :])
        sc.wait("tensor", f"wxl{db}", nw)
        for ss in range(4):
            pdb = B["pxy"][ss % 2]
            pv = B.get(f"xy_ev_{rb}_{ss-2}", B.get(f"xy_ev_{rb-1}_{2+ss}", 0))
            if pv:
                sc.wait("tensor", "sv", pv)
            for hc in range(16):
                sc.op("tensor", lambda e, pdb=pdb, db=db, hc=hc, ss=ss: e.matmul(
                    pdb[:], B["bfp"][db][:, hc * 128:(hc + 1) * 128],
                    B["bigT"][:, hc * 2048 + ss * 512:hc * 2048 + (ss + 1) * 512],
                    start=(hc == 0), stop=(hc == 15)), "st" if hc == 15 else None)
            nmm = sc.cur("st")
            sc.wait("vector", "st", nmm)
            if rb < 16:
                if ss == 0:
                    sc.wait("vector", "xrz", B.get(f"xr_z_{rb-1}", 0))
                nev = sc.op("vector", lambda e, pdb=pdb, ss=ss: e.tensor_copy(
                    B["fpp"][1][:, ss * 512:(ss + 1) * 512], pdb[:]), "sv")
            else:
                nev = sc.op("vector", lambda e, pdb=pdb, rb=rb, ss=ss: e.tensor_copy(
                    B["yT"][:, (rb - 16) * 2048 + ss * 512:(rb - 16) * 2048 + (ss + 1) * 512],
                    pdb[:]), "sv")
            B[f"xy_ev_{rb}_{ss}"] = nev
        B[f"xy_mm_{rb}"] = sc.cur("st")
        if rb < 16:
            sc.wait("gpsimd", "sv", B[f"xy_ev_{rb}_{3}"])
            nz = sc.dma("gpsimd", "xrz", B["zxrT"][rb, :, :], B["fpp"][1][:])
            B[f"xr_z_{rb}"] = nz

    sc.wait("vector", "sv", m8_done)
    sc.wait("vector", "st", B.get("xy_mm_31", 0))
    sc.wait("sync", "xrz", sc.cur("xrz"))
    for rc in range(16):
        if rc == 0:
            nmem = sc.op("vector", lambda e: e.memset(B["xc"][:, 13:16], 0.0), "sv")
        else:
            sc.wait("sync", "sv", B[f"rg_conv_{rc-1}"])
        nxl = sc.dma("sync", "xrl", B["xc"][:, 16:2064], B["zxrT"][rc, :, :])
        sc.wait("vector", "xrl", nxl)
        if rc >= 1:
            sc.wait("vector", "st", B.get(f"rg_gmm_{rc-1}", 0))
        a_, b_ = B["fpp"][0], B["fpp"][2]
        n_ = sc.op("vector", lambda e, rc=rc, a_=a_: e.tensor_scalar(
            a_[:], B["xc"][:, 13:13 + 2048], B["cwT"][:, rc * 4:rc * 4 + 1],
            B["cbT"][:, rc:rc + 1], ALU.mult, ALU.add), "sv")
        for k in range(1, 4):
            sc.wait("vector", "sv", n_)
            src, dst = (a_, b_) if k % 2 == 1 else (b_, a_)
            n_ = sc.op("vector", lambda e, k=k, rc=rc, src=src, dst=dst: e.scalar_tensor_tensor(
                dst[:], B["xc"][:, 13 + k:13 + k + 2048], B["cwT"][:, rc * 4 + k:rc * 4 + k + 1],
                src[:], ALU.mult, ALU.add), "sv")
        B[f"rg_conv_{rc}"] = n_
        xcc = b_
        sc.wait("vector", "sv", n_)
        nconv = n_
        if rc >= 1:
            sc.wait("sync", "st", B.get(f"rg_gmm_{rc-1}", 0))
        nwl = sc.dma("sync", "wil", B["wgf"][:, 0:128], B["in_wi"][rc, :, :])
        nwl = sc.dma("sync", "wil", B["wgf"][:, 128:256], B["in_wa"][rc, :, :])
        sc.wait("tensor", "wil", nwl)
        sc.wait("tensor", "sv", nconv)
        for ss in range(4):
            if not (rc == 0 and ss == 0):
                sc.wait("tensor", "sa", B.get("rg_sig", 0))
            sc.op("tensor", lambda e, rc=rc, ss=ss, xcc=xcc: e.matmul(
                B["pg2"][0][:], B["wgf"][:, 0:128],
                xcc[:, ss * 512:(ss + 1) * 512], start=True, stop=True), None)
            sc.op("tensor", lambda e, rc=rc, ss=ss, xcc=xcc: e.matmul(
                B["pg2"][1][:], B["wgf"][:, 128:256],
                xcc[:, ss * 512:(ss + 1) * 512], start=True, stop=True), "st")
            ng = sc.cur("st")
            sc.wait("scalar", "st", ng)
            sc.op("scalar", lambda e, rc=rc, ss=ss: e.activation(
                B["bfp"][1][:, ss * 512:(ss + 1) * 512], B["pg2"][0][:], AF.Sigmoid,
                bias=B["biT"][:, rc:rc + 1]), "sa")
            B["rg_sig"] = sc.op("scalar", lambda e, rc=rc, ss=ss: e.activation(
                B["fpp"][1][:, ss * 512:(ss + 1) * 512], B["pg2"][1][:], AF.Sigmoid,
                bias=B["baT"][:, rc:rc + 1]), "sa")
        B[f"rg_gmm_{rc}"] = sc.cur("st")
        nsig = B["rg_sig"]
        sc.wait("vector", "sa", nsig)
        n_ = sc.op("vector", lambda e, rc=rc: e.tensor_scalar(
            B["fpp"][0][:], B["fpp"][1][:], B["m8T"][:, rc:rc + 1], None, ALU.mult), "sv")
        sc.wait("scalar", "sv", n_)
        na = sc.op("scalar", lambda e: e.activation(B["fpp"][1][:], B["fpp"][0][:], AF.Exp), "sa")
        na = sc.op("scalar", lambda e: e.activation(B["fpp"][3][:], B["fpp"][0][:], AF.Exp, scale=2.0), "sa")
        sc.wait("vector", "sa", na)
        n_ = sc.op("vector", lambda e: e.tensor_scalar(
            B["fpp"][0][:], B["fpp"][3][:], -1.0, 1.0, ALU.mult, ALU.add), "sv")
        sc.wait("scalar", "sv", n_)
        na = sc.op("scalar", lambda e: e.activation(B["fpp"][3][:], B["fpp"][0][:], AF.Ln), "sa")
        sc.wait("scalar", "sa", na)
        na = sc.op("scalar", lambda e: e.activation(B["fpp"][0][:], B["fpp"][3][:], AF.Exp, scale=0.5), "sa")
        sc.wait("vector", "sa", na)
        n_ = sc.op("vector", lambda e, xcc=xcc: e.tensor_mul(B["fpp"][3][:], xcc[:], B["bfp"][1][:]), "sv")
        sc.wait("vector", "sv", n_)
        n_ = sc.op("vector", lambda e: e.tensor_mul(B["fpp"][3][:], B["fpp"][3][:], B["fpp"][0][:]), "sv")
        sc.wait("vector", "sv", n_)
        if debug:
            if "zdbg" not in B:
                B["zdbg"] = nc.dram_tensor("zdbg", [4, 16, 128, 2048], F32, kind="ExternalOutput").ap()
            sc.wait("gpsimd", "sv", sc.cur("sv"))
            sc.wait("gpsimd", "sa", sc.cur("sa"))
            nda = sc.dma("gpsimd", "dbg", B["zdbg"][0, rc, :, :], B["fpp"][1][:])
            nda = sc.dma("gpsimd", "dbg", B["zdbg"][1, rc, :, :], B["fpp"][3][:])
            nda = sc.dma("gpsimd", "dbg", B["zdbg"][2, rc, :, :], xcc[:])
            nda = sc.dma("gpsimd", "dbg", B["zdbg"][3, rc, :, :], B["bfp"][1][:])
            sc.wait("vector", "dbg", nda)
        n_ = sc.op("vector", lambda e: e.tensor_tensor_scan(
            B["fpp"][0][:], B["fpp"][1][:], B["fpp"][3][:], 0.0, ALU.mult, ALU.add), "sv")
        sc.wait("scalar", "sv", n_)
        na = sc.op("scalar", lambda e, rc=rc: e.activation(
            B["fpp"][1][:], B["yT"][:, rc * 2048:(rc + 1) * 2048], AF.Gelu), "sa")
        sc.wait("vector", "sa", na)
        ng_ = sc.op("vector", lambda e, rc=rc: e.tensor_mul(
            B["bigT"][:, rc * 2048:(rc + 1) * 2048], B["fpp"][1][:], B["fpp"][0][:]), "sv")
        B[f"rg_G_{rc}"] = ng_

    wosl = B.get("wosl") or (lambda rc: B["wos"][:, rc * 512:(rc + 1) * 512])
    sc.wait("sync", "sv", sc.cur("sv"))
    sc.wait("sync", "sa", sc.cur("sa"))
    for hhs in range(4):
        prev = B.get("wo2_consumed", B.get("rg_gmm_15", 0))
        if prev:
            sc.wait("sync", "st", prev)
        nw = 0
        for rc in range(16):
            nw = sc.dma("sync", "wol", wosl(rc),
                        B["zwoT"][rc, :, hhs * 512:(hhs + 1) * 512])
        for st16 in range(16):
            db = st16 % 2
            pq = B["pxy"][db]
            prev_ev = B.get(f"fin_ev_{hhs}_{st16-1}", B.get(f"fin_ev_{hhs-1}_{15}", 0))
            if prev_ev:
                sc.wait("sync", "sv", prev_ev)
            nxr = sc.dma("sync", "xres", B["xres"][0][:],
                         B["in_x"][st16 * 128:(st16 + 1) * 128, hhs * 512:(hhs + 1) * 512])
            sc.wait("tensor", "wol", nw)
            if st16 == 0 and hhs == 0:
                sc.wait("tensor", "sv", B.get("rg_G_15", 0))
            pv2 = B.get(f"fin_ev_{hhs}_{st16-2}", B.get(f"fin_ev_{hhs-1}_{14+st16}", 0))
            if pv2:
                sc.wait("tensor", "sv", pv2)
            for rc in range(16):
                sc.op("tensor", lambda e, pq=pq, rc=rc, st16=st16, wosl=wosl: e.matmul(
                    pq[:], B["bigT"][:, rc * 2048 + st16 * 128:rc * 2048 + (st16 + 1) * 128],
                    wosl(rc),
                    start=(rc == 0), stop=(rc == 15)), "st" if rc == 15 else None)
            nmm = sc.cur("st")
            B[f"fin_mm_{hhs}_{st16}"] = nmm
            sc.wait("vector", "st", nmm)
            sc.wait("vector", "xres", nxr)
            sc.wait("vector", "yqd", sc.cur("yqd"))
            nev = sc.op("vector", lambda e, pq=pq: e.tensor_add(
                B["yq"][0][:], pq[:], B["xres"][0][:]), "sv")
            B[f"fin_ev_{hhs}_{st16}"] = nev
            sc.wait("gpsimd", "sv", nev)
            sc.dma("gpsimd", "yqd", B["out_y"][st16 * 128:(st16 + 1) * 128,
                                               hhs * 512:(hhs + 1) * 512], B["yq"][0][:])
        B["wo2_consumed"] = B[f"fin_mm_{hhs}_{15}"]



def build_attn1(debug=False):
    nc = bass.Bass("TRN2", target_bir_lowering=False, detect_race_conditions=False)
    B = {}
    B["in_x"] = nc.dram_tensor("x", [S, H], F32, kind="ExternalInput").ap()
    B["in_wxy"] = nc.dram_tensor("wxy", [4096, H], F32, kind="ExternalInput").ap()
    B["in_wi"] = nc.dram_tensor("wi", [16, 128, 128], F32, kind="ExternalInput").ap()
    B["in_bi"] = nc.dram_tensor("bi", [16, 128], F32, kind="ExternalInput").ap()
    B["in_wa"] = nc.dram_tensor("wa", [16, 128, 128], F32, kind="ExternalInput").ap()
    B["in_ba"] = nc.dram_tensor("ba", [16, 128], F32, kind="ExternalInput").ap()
    B["in_ap"] = nc.dram_tensor("apar", [H], F32, kind="ExternalInput").ap()
    B["in_cw"] = nc.dram_tensor("cw", [4, H], F32, kind="ExternalInput").ap()
    B["in_cb"] = nc.dram_tensor("cb", [H], F32, kind="ExternalInput").ap()
    B["in_wo"] = nc.dram_tensor("wo", [H, H], F32, kind="ExternalInput").ap()
    B["in_idf"] = nc.dram_tensor("idf", [128, 128], F32, kind="ExternalInput").ap()
    B["out_y"] = nc.dram_tensor("y", [S, H], F32, kind="ExternalOutput").ap()
    zwxyT = nc.dram_tensor("zwxyT", [32, 128, H], BF16, kind="Internal").ap()
    zwoT = nc.dram_tensor("zwoT", [16, 128, H], BF16, kind="Internal").ap()
    zxrT = nc.dram_tensor("zxrT", [16, 128, S], F32, kind="Internal").ap()

    with ExitStack() as es:
        E = es.enter_context
        sb = lambda n, shp, dt: E(nc.sbuf_tensor(n, shp, dt))
        B["identf"] = sb("identf", [128, 128], F32)
        B["identb"] = sb("identb", [128, 128], BF16)
        fp = [sb(f"fp{i}", [128, H], F32) for i in range(4)]
        B["xin"] = [fp[0]]
        B["xin_nbuf"] = 1
        B["xlnb"] = sb("xlnb", [128, H], BF16)
        for nm in ("st_m", "st_s", "st_v", "st_t", "st_r"):
            B[nm] = sb(nm, [128, 1], F32)
        B["bigT"] = sb("bigT", [128, 16 * 2048], BF16)
        B["yT"] = sb("yT", [128, 16 * 2048], BF16)
        bf = [sb(f"bfp{i}", [128, H], BF16) for i in range(3)]
        B["xc"] = sb("xc", [128, 2064], F32)
        B["wos"] = sb("wos", [128, 16 * 512], BF16)
        B["wgf"] = sb("wgf", [128, 256], F32)
        for nm in ("cwT", "biT", "baT", "apT", "cbT", "m8T", "t16a", "t16b"):
            B[nm] = sb(nm, [128, 16 if nm != "cwT" else 64], F32)
        B["xres"] = [sb("xres0", [128, 512], F32)]
        B["yq"] = [sb("yq0", [128, 512], F32)]
        B["ptr"] = [E(nc.psum_tensor(f"ptr{i}", [128, 128], BF16)) for i in range(2)]
        B["pxy"] = [E(nc.psum_tensor(f"pxy{i}", [128, 512], F32)) for i in range(2)]
        B["pg2"] = [E(nc.psum_tensor(f"pg2{i}", [128, 512], F32)) for i in range(2)]
        B["psmall"] = E(nc.psum_tensor("psmall", [128, 16], F32))
        B["_ptr_evac"] = [0, 0]
        B["_zs_last"] = [0, 0]

        B["zwxyT"] = zwxyT
        B["zwoT"] = zwoT
        B["zxrT"] = zxrT
        B["fpp"] = fp
        B["bfp"] = bf
        sc = Sched()
        emit_attn1(sc, B, nc, debug=debug)
        for sem in sorted(sc.c.keys()):
            sc.wait("sync", sem, sc.cur(sem))
        sems = {name: E(nc.semaphore(name)) for name in sorted(sc.c.keys())}
        block = E(nc.Block())
        sc.emit(block, sems)
    return nc


def run_attn1(x_full, wxy, wi, bi, wa, ba, ap_, cw, cb, wo, cores=None):
    nc = get_nc("attn1", build_attn1)
    idf = np.eye(128, dtype=np.float32)
    cores = cores or list(range(NCORES))
    in_maps = []
    for c in cores:
        in_maps.append(dict(x=np.ascontiguousarray(x_full[c]), wxy=wxy, wi=wi, bi=bi,
                            wa=wa, ba=ba, apar=ap_, cw=cw, cb=cb, wo=wo, idf=idf))
    res = run_bass_kernel_spmd(nc, in_maps, core_ids=cores)
    return np.stack([res.results[i]["y"] for i in range(len(cores))], axis=0)


M_CP = 16777215
R_NUM = 130155
NCTRL = 130156
CP_FRONT = 96
CP_BACK = 256
CP_EXT_LEN = NCTRL + CP_FRONT + CP_BACK

def _keys_piece(s):
    if s == 1:
        return [2.0, -4.0, 2.5, -0.5]
    if s == 0:
        return [1.0, 0.0, -2.5, 1.5]
    if s == -1:
        return [1.0, 0.0, -2.5, -1.5]
    if s == -2:
        return [2.0, 4.0, 2.5, 0.5]
    return [0.0, 0.0, 0.0, 0.0]

def _gamma(delta):
    from math import comb
    G = np.zeros((5, 4, 4))
    for dd in range(5):
        p = _keys_piece(1 - dd + delta)
        E = 2.0 - dd
        pv = np.zeros(4)
        for k in range(4):
            for j in range(k + 1):
                pv[j] += p[k] * comb(k, j) * E ** (k - j)
        for j in range(4):
            for a in range(j + 1):
                G[dd, a, j - a] += pv[j] * comb(j, a)
    return G

def spline_la_tables():
    g0 = _gamma(0)
    g1 = _gamma(1)
    pp = np.arange(128, dtype=np.float64)
    u = pp * (R_NUM / M_CP)
    up = u - 0.5
    la0 = np.zeros((22, 128))
    la1 = np.zeros((22, 128))
    for b in range(4):
        for d in range(5):
            for a in range(4):
                la0[1 + b * 5 + d] += g0[d, a, b] * up ** a
                la1[1 + b * 5 + d] += g1[d, a, b] * up ** a
    return la0.astype(np.float32), la1.astype(np.float32)

def spline_la_quad():
    la0, la1 = spline_la_tables()
    q0 = np.zeros((128, 128), np.float32)
    q1 = np.zeros((128, 128), np.float32)
    for k in range(4):
        q0[32 * k:32 * k + 20] = la0[1:21]
        q1[32 * k:32 * k + 20] = la1[1:21]
    q03 = np.zeros((128, 128), np.float32)
    q13 = np.zeros((128, 128), np.float32)
    q03[96:116] = la0[1:21]
    q13[96:116] = la1[1:21]
    bf = ml_dtypes.bfloat16
    return q0.astype(bf), q1.astype(bf), q03.astype(bf), q13.astype(bf)


def spline_delta_tables():
    out = {}
    for cls, (A, ncols, ngroups) in (("gu", (2048, 8192, 1)), ("dn", (8192, 2048, 4))):
        nfc = ncols // 512
        tab = np.zeros((ngroups * nfc * 16, 128, 512), np.uint8)
        pp = np.arange(128, dtype=np.int64)
        for g in range(ngroups):
            base = 2048 * g
            for fc in range(nfc):
                f = np.arange(fc * 512, (fc + 1) * 512, dtype=np.int64)
                for hb in range(16):
                    mm = (A * f + base + 128 * hb) * R_NUM
                    rem = mm % M_CP
                    d = (rem[None, :] + pp[:, None] * R_NUM) >= M_CP
                    tab[(g * nfc + fc) * 16 + hb] = d.astype(np.uint8)
        out[cls] = tab
    return out["gu"], out["dn"]


def spline_class_consts(A, ncols, ngroups):
    nsub = ncols // 128
    S = 16 if A == 2048 else 64
    ND = 16 if A == 2048 else 60
    WOV = 21 + ND - 1
    G0 = np.zeros((ngroups, nsub), dtype=np.int64)
    DMASK = np.zeros((ngroups * nsub, 128, ND), np.float32)
    DL2 = np.zeros((ngroups * nsub, 128, 16), np.float32)
    PHIP = np.zeros((ngroups * nsub, 128, 64), np.float32)
    f_all = np.arange(ncols, dtype=np.int64)
    for g in range(ngroups):
        base = 2048 * g
        m_full = (A * f_all + base) * R_NUM
        W0 = m_full // M_CP
        for sub in range(nsub):
            sl = slice(sub * 128, (sub + 1) * 128)
            W0s = W0[sl]
            g0 = int(W0s[0])
            d_try = W0s - (g0 + S * np.arange(128))
            shift = int(d_try.min())
            g0 += shift
            d = W0s - (g0 + S * np.arange(128))
            assert d.min() >= 0 and d.max() < ND, (A, g, sub, d.min(), d.max())
            g0_dev = g0 + CP_FRONT - 1
            assert 0 <= g0_dev and g0_dev + S * 127 + WOV <= CP_EXT_LEN, (A, g, sub, g0_dev)
            G0[g, sub] = g0_dev
            DMASK[g * nsub + sub][np.arange(128), d] = 1.0
            for hb in range(16):
                mm = (A * f_all[sl] + base + 128 * hb) * R_NUM
                q = mm // M_CP - W0s
                phi = (mm % M_CP).astype(np.float64) / M_CP
                if hb == 0:
                    assert (q == 0).all()
                    DL2[g * nsub + sub][:, 0] = 1.0
                else:
                    d2 = q - (hb - 1)
                    assert set(np.unique(d2)) <= {0, 1}, (A, g, sub, hb)
                    DL2[g * nsub + sub][:, hb] = d2
                for b in range(4):
                    PHIP[g * nsub + sub][:, hb * 4 + b] = (phi - 0.5) ** b
    return dict(S=S, ND=ND, WOV=WOV, G0=G0, DMASK=DMASK, DL2=DL2, PHIP=PHIP, nsub=nsub)


def spline_emulate(cp, A, ncols, npart_blocks, consts, la0, la1, out_rows, out_cols):
    cpe = np.concatenate([np.full(CP_FRONT, cp[0]), cp,
                          np.full(CP_BACK, cp[-1])]).astype(np.float32)
    nsub = consts["nsub"]
    S, ND = consts["S"], consts["ND"]
    ngroups = npart_blocks // 16
    WT = np.zeros((npart_blocks * 128, ncols), np.float32)
    u = np.arange(128) * (R_NUM / M_CP)
    up = (u - 0.5).astype(np.float32)
    for g in range(ngroups):
        for sub in range(nsub):
            idx = g * nsub + sub
            g0 = int(consts["G0"][g, sub])
            OV = np.zeros((128, consts["WOV"]), np.float32)
            for pf in range(128):
                OV[pf] = cpe[g0 + S * pf: g0 + S * pf + consts["WOV"]]
            CPW21 = np.zeros((128, 21), np.float32)
            for dd in range(ND):
                CPW21 += consts["DMASK"][idx][:, dd:dd + 1] * OV[:, dd:dd + 21]
            CPW22 = np.concatenate([np.zeros((128, 1), np.float32), CPW21], axis=1)
            for hb in range(16):
                d2 = consts["DL2"][idx][:, hb:hb + 1]
                lo = CPW22[:, hb:hb + 5]
                hi = CPW22[:, hb + 1:hb + 6]
                cpw5 = lo + d2 * (hi - lo)
                OUT = np.zeros((128, 22), np.float32)
                for b in range(4):
                    OUT[:, 1 + b * 5:1 + (b + 1) * 5] = cpw5 * consts["PHIP"][idx][:, hb * 4 + b:hb * 4 + b + 1]
                OUT[:, 0] = consts["PHIP"][idx][:, hb * 4 + 1]
                P0 = la0.T @ OUT.T
                P1 = la1.T @ OUT.T
                phiB = OUT[:, 0][None, :].repeat(128, 0)
                delta = ((phiB + up[:, None]) >= 0).astype(np.float32)
                Wt = np.where(delta > 0, P1, P0)
                rows = (g * 16 + hb) * 128 + np.arange(128)
                WT[np.ix_(rows, np.arange(sub * 128, (sub + 1) * 128))] = Wt
    return WT


def emit_spline_gen(sc, B, nc, jobs):
    CL = {"gu": dict(A=2048, ncols=8192, ngroups=1, S=16, ND=16),
          "dn": dict(A=8192, ncols=2048, ngroups=4, S=64, ND=60)}
    for cls in CL:
        CL[cls]["WOV"] = 21 + CL[cls]["ND"] - 1
        CL[cls]["nsub"] = CL[cls]["ncols"] // 128
    V = bass.AP
    sc.op("vector", lambda e: e.memset(B["CPW"][:], 0.0), "sg")
    for i in range(2):
        OU = B["OUTS"][i][:]
        sc.op("vector", lambda e, OU=OU: e.memset(
            V(tensor=OU.tensor, offset=OU.offset + 20, ap=[OU.ap[0], [32, 64], [1, 12]]),
            0.0), "sg")
    dsrc = []
    for (cpe, cls, dst, g0tab, dtab) in jobs:
        C = CL[cls]
        nfc = C["ncols"] // 512
        for g in range(C["ngroups"]):
            for fc in range(nfc):
                for hb in range(16):
                    dsrc.append(dtab[(g * nfc + fc) * 16 + hb])
    ntiles = len(dsrc)
    NDLT = len(B["dlt3"])
    for t in range(min(NDLT, ntiles)):
        sc.dma("scalar", "dl", B["dlt3"][t % NDLT][:], dsrc[t])
    pend_dlt = []
    tix = 0
    hbgx = 0
    fcx = 0
    for (cpe, cls, dst, g0tab, dtab) in jobs:
        C = CL[cls]
        ND = C["ND"]
        nfc = C["ncols"] // 512
        for g in range(C["ngroups"]):
            for fc in range(nfc):
                par = fcx % 2
                OUTS = B["OUTS"][par][:]
                sc.wait("sync", "sg", B.get(f"gen_slab_{fcx-1}", 0))
                sc.dma("sync", "ovl", B["DM4"][:, 0:4 * ND].rearrange(
                    "p (s d) -> p s d", d=ND),
                    B[f"in_dmask_{cls}"][g * C["nsub"] + fc * 4:g * C["nsub"] + fc * 4 + 4].rearrange("s p d -> p s d"))
                sc.dma("sync", "ovl", B["DL4"][:].rearrange("p (s d) -> p s d", d=16),
                    B[f"in_dl2_{cls}"][g * C["nsub"] + fc * 4:g * C["nsub"] + fc * 4 + 4].rearrange("s p d -> p s d"))
                sc.dma("sync", "ovl", B["PH4"][:].rearrange("p (s d) -> p s d", d=64),
                    B[f"in_phip_{cls}"][g * C["nsub"] + fc * 4:g * C["nsub"] + fc * 4 + 4].rearrange("s p d -> p s d"))
                nld = 0
                for sub in range(4):
                    g0 = int(g0tab[g, fc * 4 + sub])
                    src = V(tensor=cpe.tensor, offset=g0,
                            ap=[[C["S"], 128], [1, C["WOV"]]])
                    nld = sc.dma("sync", "ovl", B["OV4"][:, sub * 80:sub * 80 + C["WOV"]], src)
                sc.wait("vector", "ovl", nld)
                sc.wait("vector", "st", B.get(f"gen_outs_{fcx-2}", 0))
                ov, dm, dl, ph = B["OV4"][:], B["DM4"][:], B["DL4"][:], B["PH4"][:]
                cpw, tb, d5, c5 = B["CPW"][:], B["TBIG"][:], B["DIF5"][:], B["CPW5S"][:]
                for sub in range(4):
                    sc.op("vector", lambda e, sub=sub, ND=ND: e.tensor_tensor(
                        V(tensor=tb.tensor, offset=tb.offset, ap=[tb.ap[0], [ND, 21], [1, ND]]),
                        V(tensor=ov.tensor, offset=ov.offset + sub * 80, ap=[ov.ap[0], [1, 21], [1, ND]]),
                        V(tensor=dm.tensor, offset=dm.offset + sub * ND, ap=[dm.ap[0], [0, 21], [1, ND]]),
                        ALU.mult), "sg")
                    sc.op("vector", lambda e, sub=sub, ND=ND: e.reduce_sum(
                        cpw[:, sub * 22 + 1:sub * 22 + 22],
                        V(tensor=tb.tensor, offset=tb.offset, ap=[tb.ap[0], [ND, 21], [1, ND]]),
                        AX.X), "sg")
                    sc.op("vector", lambda e, sub=sub: e.tensor_tensor(
                        V(tensor=d5.tensor, offset=d5.offset, ap=[d5.ap[0], [5, 16], [1, 5]]),
                        V(tensor=cpw.tensor, offset=cpw.offset + sub * 22 + 1, ap=[cpw.ap[0], [1, 16], [1, 5]]),
                        V(tensor=cpw.tensor, offset=cpw.offset + sub * 22, ap=[cpw.ap[0], [1, 16], [1, 5]]),
                        ALU.subtract), "sg")
                    sc.op("vector", lambda e, sub=sub: e.tensor_tensor(
                        V(tensor=d5.tensor, offset=d5.offset, ap=[d5.ap[0], [5, 16], [1, 5]]),
                        V(tensor=d5.tensor, offset=d5.offset, ap=[d5.ap[0], [5, 16], [1, 5]]),
                        V(tensor=dl.tensor, offset=dl.offset + sub * 16, ap=[dl.ap[0], [1, 16], [0, 5]]),
                        ALU.mult), "sg")
                    sc.op("vector", lambda e, sub=sub: e.tensor_tensor(
                        V(tensor=c5.tensor, offset=c5.offset, ap=[c5.ap[0], [5, 16], [1, 5]]),
                        V(tensor=d5.tensor, offset=d5.offset, ap=[d5.ap[0], [5, 16], [1, 5]]),
                        V(tensor=cpw.tensor, offset=cpw.offset + sub * 22, ap=[cpw.ap[0], [1, 16], [1, 5]]),
                        ALU.add), "sg")
                    sc.op("vector", lambda e, sub=sub, OUTS=OUTS: e.tensor_tensor(
                        V(tensor=OUTS.tensor, offset=OUTS.offset + sub * 512,
                          ap=[OUTS.ap[0], [32, 16], [5, 4], [1, 5]]),
                        V(tensor=c5.tensor, offset=c5.offset, ap=[c5.ap[0], [5, 16], [0, 4], [1, 5]]),
                        V(tensor=ph.tensor, offset=ph.offset + sub * 64, ap=[ph.ap[0], [4, 16], [1, 4], [0, 5]]),
                        ALU.mult), "sg")
                nslab = sc.cur("sg")
                B[f"gen_slab_{fcx}"] = nslab
                sc.wait("tensor", "sg", nslab)
                for hbg in range(4):
                    gp = hbgx % 2
                    TR = B["ptg"][gp]
                    RHS = B["RHSg"][gp]
                    sc.wait("tensor", "sa", B.get(f"gen_ev_{hbgx-2}", 0))
                    for sub in range(4):
                        sc.op("tensor", lambda e, sub=sub, hbg=hbg, TR=TR, OUTS=OUTS: e.transpose(
                            TR[:, sub * 128:(sub + 1) * 128],
                            OUTS[:, sub * 512 + hbg * 128:sub * 512 + hbg * 128 + 128],
                            B["identb"][:]), "st")
                    ntr4 = sc.cur("st")
                    if hbg == 3:
                        B[f"gen_outs_{fcx}"] = ntr4
                    sc.wait("scalar", "st", ntr4)
                    nev = sc.op("scalar", lambda e, TR=TR, RHS=RHS: e.copy(RHS[:], TR[:]), "sa")
                    for (dbuf, dsr, nwait) in pend_dlt:
                        sc.wait("scalar", "sv", nwait)
                        sc.dma("scalar", "dl", dbuf, dsr)
                    pend_dlt = []
                    B[f"gen_ev_{hbgx}"] = nev
                    sc.wait("tensor", "sa", nev)
                    for hh in range(4):
                        tpar = tix % 2
                        P0, P1 = B["pgen"][tpar]
                        sc.wait("tensor", "sv", B.get(f"gen_sel_{tix-2}", 0))
                        if hh < 3:
                            b0, bn, k0, k1 = hh * 32, 20, "LA0q", "LA1q"
                        else:
                            b0, bn, k0, k1 = 64, 64, "LA0q3", "LA1q3"
                        sc.op("tensor", lambda e, P0=P0, RHS=RHS, b0=b0, bn=bn, k0=k0: e.matmul(
                            P0[:], B[k0][b0:b0 + bn, :], RHS[b0:b0 + bn, :],
                            start=True, stop=True), None)
                        nmm = sc.op("tensor", lambda e, P1=P1, RHS=RHS, b0=b0, bn=bn, k1=k1: e.matmul(
                            P1[:], B[k1][b0:b0 + bn, :], RHS[b0:b0 + bn, :],
                            start=True, stop=True), "st")
                        hb = hbg * 4 + hh
                        wt = B["wt2"][tpar]
                        dlt = B["dlt3"][tix % NDLT][:]
                        sc.wait("vector", "st", nmm)
                        sc.wait("vector", "dl", 16 * (tix + 1))
                        sc.wait("vector", "wz", B.get(f"gen_wz_{tix-2}", 0))
                        nsel = sc.op("vector", lambda e, wt=wt, dlt=dlt, P0=P0, P1=P1: e.select(
                            wt[:], dlt, P1[:], P0[:]), "sv")
                        B[f"gen_sel_{tix}"] = nsel
                        if tix + NDLT < ntiles:
                            pend_dlt.append((B["dlt3"][(tix + NDLT) % NDLT][:],
                                             dsrc[tix + NDLT], nsel))
                        if cls == "gu":
                            dv = dst[fc * 4:(fc + 1) * 4, :, hb * 128:(hb + 1) * 128].rearrange(
                                "ib p c -> p ib c")
                            sv_ = wt[:].rearrange("p (ib c) -> p ib c", c=128)
                        else:
                            dv = dst[g * 16 + hb, fc, :, :]
                            sv_ = wt[:]
                        sc.wait("gpsimd", "sv", nsel)
                        B[f"gen_wz_{tix}"] = sc.dma("gpsimd", "wz", dv, sv_)
                        tix += 1
                    hbgx += 1
                fcx += 1
    return sc.cur("wz")


def build_ffn1():
    nc = bass.Bass("TRN2", target_bir_lowering=False, detect_race_conditions=False)
    B = {}
    B["in_x"] = nc.dram_tensor("x", [S, H], F32, kind="ExternalInput").ap()
    B["in_cpg"] = nc.dram_tensor("cpg", [CP_EXT_LEN], F32, kind="ExternalInput").ap()
    B["in_cpu"] = nc.dram_tensor("cpu", [CP_EXT_LEN], F32, kind="ExternalInput").ap()
    B["in_cpd"] = nc.dram_tensor("cpd", [CP_EXT_LEN], F32, kind="ExternalInput").ap()
    B["in_la0q"] = nc.dram_tensor("la0q", [128, 128], BF16, kind="ExternalInput").ap()
    B["in_la1q"] = nc.dram_tensor("la1q", [128, 128], BF16, kind="ExternalInput").ap()
    B["in_la0q3"] = nc.dram_tensor("la0q3", [128, 128], BF16, kind="ExternalInput").ap()
    B["in_la1q3"] = nc.dram_tensor("la1q3", [128, 128], BF16, kind="ExternalInput").ap()
    B["in_dltg"] = nc.dram_tensor("dltg", [256, 128, 512], U8, kind="ExternalInput").ap()
    B["in_dltd"] = nc.dram_tensor("dltd", [256, 128, 512], U8, kind="ExternalInput").ap()
    B["in_dmask_gu"] = nc.dram_tensor("dmg", [64, 128, 16], F32, kind="ExternalInput").ap()
    B["in_dl2_gu"] = nc.dram_tensor("dlg", [64, 128, 16], F32, kind="ExternalInput").ap()
    B["in_phip_gu"] = nc.dram_tensor("phg", [64, 128, 64], F32, kind="ExternalInput").ap()
    B["in_dmask_dn"] = nc.dram_tensor("dmd", [64, 128, 60], F32, kind="ExternalInput").ap()
    B["in_dl2_dn"] = nc.dram_tensor("dld", [64, 128, 16], F32, kind="ExternalInput").ap()
    B["in_phip_dn"] = nc.dram_tensor("phd", [64, 128, 64], F32, kind="ExternalInput").ap()
    B["in_idf"] = nc.dram_tensor("idf", [128, 128], F32, kind="ExternalInput").ap()
    B["out_y"] = nc.dram_tensor("y", [S, H], F32, kind="ExternalOutput").ap()
    zxlnT = nc.dram_tensor("zxlnT", [4, 16, 128, 512], BF16, kind="Internal").ap()
    zwgT = nc.dram_tensor("zwgT", [64, 128, H], BF16, kind="Internal").ap()
    zwuT = nc.dram_tensor("zwuT", [64, 128, H], BF16, kind="Internal").ap()
    zwdT = nc.dram_tensor("zwdT", [64, 4, 128, 512], BF16, kind="Internal").ap()

    cg = spline_class_consts(2048, 8192, 1)
    cd = spline_class_consts(8192, 2048, 4)

    with ExitStack() as es:
        E = es.enter_context
        sb = lambda n, shp, dt: E(nc.sbuf_tensor(n, shp, dt))
        B["identf"] = sb("identf", [128, 128], F32)
        B["identb"] = sb("identb", [128, 128], BF16)
        B["xin"] = [sb(f"xin{i}", [128, H], F32) for i in range(2)]
        B["sqtmp"] = sb("sqtmp", [128, H], BF16)
        B["xlnb"] = sb("xlnb", [128, H], BF16)
        for nm in ("st_m", "st_s", "st_v", "st_t", "st_r"):
            B[nm] = sb(nm, [128, 1], F32)
        B["sT"] = [sb(f"sT{i}", [128, H], BF16) for i in range(2)]
        B["xlnTq"] = sb("xlnTq", [128, 16 * 512], BF16)
        B["wT_sb"] = [[sb(f"wT{m}{i}", [128, H], BF16) for i in range(2)] for m in range(2)]
        B["actT"] = sb("actT", [128, 64 * 512], BF16)
        B["wdT_sb"] = [sb(f"wdT{i}", [128, 512], BF16) for i in range(4)]
        B["xres"] = [sb(f"xres{i}", [128, 512], F32) for i in range(4)]
        B["yq"] = [sb(f"yq{i}", [128, 512], F32) for i in range(4)]
        B["sact"] = [sb(f"sact{i}", [128, 512], BF16) for i in range(2)]
        B["OV4"] = sb("OV4", [128, 4 * 80], F32)
        B["DM4"] = sb("DM4", [128, 4 * 60], F32)
        B["DL4"] = sb("DL4", [128, 64], F32)
        B["PH4"] = sb("PH4", [128, 256], F32)
        B["CPW"] = sb("CPW", [128, 4 * 22], F32)[:]
        B["TBIG"] = sb("TBIG", [128, 1260], F32)[:]
        B["DIF5"] = sb("DIF5", [128, 80], F32)[:]
        B["CPW5S"] = sb("CPW5S", [128, 80], F32)[:]
        B["OUTS"] = [sb(f"OUTS{i}", [128, 2048], BF16)[:] for i in range(2)]
        B["RHSg"] = [sb(f"RHSg{i}", [128, 512], BF16) for i in range(2)]
        B["LA0q"] = sb("LA0q", [128, 128], BF16)
        B["LA1q"] = sb("LA1q", [128, 128], BF16)
        B["LA0q3"] = sb("LA0q3", [128, 128], BF16)
        B["LA1q3"] = sb("LA1q3", [128, 128], BF16)
        B["dlt3"] = [sb(f"dlt3{i}", [128, 512], U8) for i in range(6)]
        B["wt2"] = [sb(f"wt2{i}", [128, 512], BF16) for i in range(2)]
        B["ptr"] = [E(nc.psum_tensor(f"ptr{i}", [128, 128], BF16)) for i in range(2)]
        B["ptg"] = [E(nc.psum_tensor(f"ptg{i}", [128, 512], BF16)) for i in range(2)]
        B["pg"] = [E(nc.psum_tensor(f"pg{i}", [128, 512], F32)) for i in range(2)]
        B["pu"] = [E(nc.psum_tensor(f"pu{i}", [128, 512], F32)) for i in range(2)]
        B["pgen"] = [(B["pg"][0][:], B["pg"][1][:]), (B["pu"][0][:], B["pu"][1][:])]
        B["_ptr_evac"] = [0, 0]
        B["_zs_last"] = [0, 0]

        sc = Sched()
        n_id = sc.dma("sync", "idm", B["identf"][:], B["in_idf"])
        n_la = sc.dma("gpsimd", "idm2", B["LA0q"][:], B["in_la0q"])
        n_la = sc.dma("gpsimd", "idm2", B["LA1q"][:], B["in_la1q"])
        n_la = sc.dma("gpsimd", "idm2", B["LA0q3"][:], B["in_la0q3"])
        n_la = sc.dma("gpsimd", "idm2", B["LA1q3"][:], B["in_la1q3"])
        sc.wait("vector", "idm", n_id)
        sc.wait("vector", "idm2", n_la)
        nid_b = sc.op("vector", lambda e: e.tensor_copy(B["identb"][:], B["identf"][:]), "sv")

        emit_ln_pass(sc, B, B["in_x"], zxlnT, "ln1")

        jobs = [(B["in_cpg"], "gu", zwgT, cg["G0"], B["in_dltg"]),
                (B["in_cpu"], "gu", zwuT, cg["G0"], B["in_dltg"]),
                (B["in_cpd"], "dn", zwdT, cd["G0"], B["in_dltd"])]
        sc.wait("tensor", "sv", nid_b)
        sc.wait("tensor", "idm2", n_la)
        emit_spline_gen(sc, B, nc, jobs)
        for sem in ("wz", "zs0", "zs1"):
            sc.wait("sync", sem, sc.cur(sem))
        emit_ffn_matmuls(sc, B, zxlnT, zwgT, zwuT, zwdT, B["in_x"], B["out_y"], "f1")
        for sem in sorted(sc.c.keys()):
            sc.wait("sync", sem, sc.cur(sem))
        sems = {name: E(nc.semaphore(name)) for name in sorted(sc.c.keys())}
        block = E(nc.Block())
        sc.emit(block, sems)
    return nc


def ffn1_host_inputs():
    la0q, la1q, la0q3, la1q3 = spline_la_quad()
    dltg, dltd = spline_delta_tables()
    cg = spline_class_consts(2048, 8192, 1)
    cd = spline_class_consts(8192, 2048, 4)
    return dict(la0q=la0q, la1q=la1q, la0q3=la0q3, la1q3=la1q3, dltg=dltg, dltd=dltd,
                dmg=cg["DMASK"], dlg=cg["DL2"], phg=cg["PHIP"],
                dmd=cd["DMASK"], dld=cd["DL2"], phd=cd["PHIP"])


def pad_cp(cp):
    return np.concatenate([np.full(CP_FRONT, cp[0]), cp,
                           np.full(CP_BACK, cp[-1])]).astype(np.float32)


def run_ffn1(x_full, gcp, ucp, dcp, cores=None):
    nc = get_nc("ffn1", build_ffn1)
    idf = np.eye(128, dtype=np.float32)
    hi = ffn1_host_inputs()
    cores = cores or list(range(NCORES))
    in_maps = []
    for c in cores:
        im = dict(x=np.ascontiguousarray(x_full[c]), cpg=pad_cp(gcp), cpu=pad_cp(ucp),
                  cpd=pad_cp(dcp), idf=idf)
        im.update(hi)
        in_maps.append(im)
    res = run_bass_kernel_spmd(nc, in_maps, core_ids=cores)
    return np.stack([res.results[i]["y"] for i in range(len(cores))], axis=0)



def sbarrier(sc):
    for eng in Sched.ENGINES:
        for sem in list(sc.c.keys()):
            sc.wait(eng, sem, sc.cur(sem))


def build_fused():
    nc = bass.Bass("TRN2", target_bir_lowering=False, detect_race_conditions=False)
    B = {}
    dr = lambda n, shp, dt: nc.dram_tensor(n, shp, dt, kind="ExternalInput").ap()
    i_x = dr("x", [S, H], F32)
    i_gw = dr("gw", [II, H], I32); i_uw = dr("uw", [II, H], I32); i_dw = dr("dw", [H, II], I32)
    i_gs = dr("gs", [II], F32); i_us = dr("us", [II], F32); i_ds = dr("ds", [H], F32)
    i_wq = dr("wq", [H, 2048], F32); i_wk = dr("wk", [H, 512], F32)
    i_wv = dr("wv", [H, 512], F32); i_wo = dr("wo", [2048, H], F32)
    i_lam = dr("lamb", [128, 16], F32)
    i_wxy = dr("wxy", [4096, H], F32); i_wi = dr("wi", [16, 128, 128], F32)
    i_bi = dr("bi", [16, 128], F32); i_wa = dr("wa", [16, 128, 128], F32)
    i_ba = dr("ba", [16, 128], F32); i_ap = dr("apar", [H], F32)
    i_cw = dr("cw", [4, H], F32); i_cb = dr("cb", [H], F32); i_wo1 = dr("wo1", [H, H], F32)
    i_cpg = dr("cpg", [CP_EXT_LEN], F32); i_cpu = dr("cpu", [CP_EXT_LEN], F32)
    i_cpd = dr("cpd", [CP_EXT_LEN], F32)
    i_idf = dr("idf", [128, 128], F32)
    i_cq = dr("cq", [16, 128, 512], BF16); i_sq = dr("sq", [16, 128, 512], BF16)
    i_ck = dr("ck", [16, 128, 512], BF16); i_sk = dr("sk", [16, 128, 512], BF16)
    i_msk = dr("msk", [128, 640], BF16)
    i_la0q = dr("la0q", [128, 128], BF16); i_la1q = dr("la1q", [128, 128], BF16)
    i_la0q3 = dr("la0q3", [128, 128], BF16); i_la1q3 = dr("la1q3", [128, 128], BF16)
    i_dltg = dr("dltg", [256, 128, 512], U8); i_dltd = dr("dltd", [256, 128, 512], U8)
    i_dmg = dr("dmg", [64, 128, 16], F32); i_dlg = dr("dlg", [64, 128, 16], F32)
    i_phg = dr("phg", [64, 128, 64], F32)
    i_dmd = dr("dmd", [64, 128, 60], F32); i_dld = dr("dld", [64, 128, 16], F32)
    i_phd = dr("phd", [64, 128, 64], F32)
    out_y = nc.dram_tensor("y", [S, H], F32, kind="ExternalOutput").ap()
    z = [nc.dram_tensor(f"z{i}", [S, H], F32, kind="Internal").ap() for i in range(5)]
    zxlnT = nc.dram_tensor("zxlnT", [4, 16, 128, 512], BF16, kind="Internal").ap()
    zwg0 = nc.dram_tensor("zwg0", [64, 128, H], BF16, kind="Internal").ap()
    zwu0 = nc.dram_tensor("zwu0", [64, 128, H], BF16, kind="Internal").ap()
    zwd0 = nc.dram_tensor("zwd0", [64, 4, 128, 512], BF16, kind="Internal").ap()
    zwg1 = nc.dram_tensor("zwg1", [64, 128, H], BF16, kind="Internal").ap()
    zwu1 = nc.dram_tensor("zwu1", [64, 128, H], BF16, kind="Internal").ap()
    zwd1 = nc.dram_tensor("zwd1", [64, 4, 128, 512], BF16, kind="Internal").ap()
    zwxyT = nc.dram_tensor("zwxyT", [32, 128, H], BF16, kind="Internal").ap()
    zwoT = nc.dram_tensor("zwoT", [16, 128, H], BF16, kind="Internal").ap()
    zxrT = nc.dram_tensor("zxrT", [16, 128, S], F32, kind="Internal").ap()

    cg = spline_class_consts(2048, 8192, 1)
    cd = spline_class_consts(8192, 2048, 4)

    with ExitStack() as es:
        E = es.enter_context
        sb = lambda n, shp, dt: E(nc.sbuf_tensor(n, shp, dt))
        PA = sb("PA", [128, 32768], BF16)
        PB = sb("PB", [128, 32768], BF16)
        PC = sb("PC", [128, 24576], BF16)
        B["identf"] = sb("identf", [128, 128], F32)
        B["identb"] = sb("identb", [128, 128], BF16)
        xbig = sb("xbig", [128, H], F32)
        B["xin"] = [xbig]
        B["xin_nbuf"] = 1
        B["xlnb"] = sb("xlnb", [128, H], BF16)
        for nm in ("st_m", "st_s", "st_v", "st_t", "st_r"):
            B[nm] = sb(nm, [128, 1], F32)
        B["sT"] = [sb(f"sTf{i}", [128, H], BF16) for i in range(2)]
        B["stmp"] = sb("stmp", [128, 128], F32)
        B["gsT"] = sb("gsT", [128, 64], F32)
        B["usT"] = sb("usT", [128, 64], F32)
        B["dsT"] = sb("dsT", [128, 16], F32)
        B["ropc"] = sb("ropc", [128, 512], BF16)
        B["rops"] = sb("rops", [128, 512], BF16)
        B["qa"] = sb("qa", [128, 512], BF16)
        B["qb"] = sb("qb", [128, 512], BF16)
        B["qr"] = sb("qr", [128, 512], BF16)
        B["lamB"] = sb("lamB", [128, 16], F32)
        B["mskb"] = sb("mskb", [128, 640], BF16)
        B["sm"] = [sb(f"sm{i}", [128, 1], F32) for i in range(16)]
        B["zsb"] = PC[:, 16384:17664].bitcast(F32)
        B["esb"] = [PC[:, 17664:18304], PC[:, 18304:18944]]
        B["dif"] = PC[:, 18944:19584]
        B["aout"] = PC[:, 19584:19712]
        B["zsb2"] = PC[:, 19712:20992].bitcast(F32)
        B["esb2"] = [PC[:, 20992:21632], PC[:, 21632:22272]]
        B["dif2"] = PC[:, 22272:22912]
        B["aout2"] = PC[:, 22912:23040]
        for nm in ("cwT", "biT", "baT", "apT", "cbT", "m8T", "t16a", "t16b"):
            B[nm] = sb(nm, [128, 16 if nm != "cwT" else 64], F32)
        B["wgf"] = sb("wgf", [128, 256], F32)
        B["xc"] = PC[:, 12288:16416].bitcast(F32)
        B["OV4"] = PC[:, 0:640].bitcast(F32)
        B["DM4"] = PC[:, 640:1120].bitcast(F32)
        B["DL4"] = PC[:, 1120:1248].bitcast(F32)
        B["PH4"] = PC[:, 1248:1760].bitcast(F32)
        B["CPW"] = PC[:, 1760:1936].bitcast(F32)
        B["LA0q"] = PC[:, 2600:2728]
        B["LA1q"] = PC[:, 2728:2856]
        B["LA0q3"] = PC[:, 2856:2984]
        B["LA1q3"] = PC[:, 2984:3112]
        B["TBIG"] = PA[:, 0:2520].bitcast(F32)
        B["DIF5"] = PA[:, 2560:2720].bitcast(F32)
        B["CPW5S"] = PA[:, 2720:2880].bitcast(F32)
        B["OUTS"] = [PA[:, 4096 + i * 2048:4096 + (i + 1) * 2048] for i in range(2)]
        B["RHSg"] = [PA[:, 8192 + i * 512:8192 + (i + 1) * 512] for i in range(2)]
        B["dlt3"] = [PA[:, 9216 + i * 256:9216 + (i + 1) * 256].bitcast(U8) for i in range(6)]
        B["wt2"] = [PB[:, 28848 + i * 512:28848 + (i + 1) * 512] for i in range(2)]
        pbigA = E(nc.psum_tensor("pbigA", [128, 1024], F32))
        pbigB = E(nc.psum_tensor("pbigB", [128, 1024], F32))
        pbigC = E(nc.psum_tensor("pbigC", [128, 1024], F32))
        pbigD = E(nc.psum_tensor("pbigD", [128, 1024], F32))
        B["ptr"] = [pbigC[:, 0:64].bitcast(BF16), pbigD[:, 0:64].bitcast(BF16)]
        B["ptg"] = [pbigC[:, 512:768].bitcast(BF16), pbigD[:, 512:768].bitcast(BF16)]
        B["pav"] = pbigC[:, 128:256]
        B["pscl"] = B["pav"]
        B["psmall"] = B["pav"]
        B["pg"] = [pbigA[:, 0:512], pbigA[:, 512:1024]]
        B["pu"] = [pbigB[:, 0:512], pbigB[:, 512:1024]]
        B["po_sets"] = [
            [pbigA[:, 0:512], pbigA[:, 512:1024], pbigB[:, 0:512], pbigB[:, 512:1024]],
            [pbigC[:, 0:512], pbigC[:, 512:1024], pbigD[:, 0:512], pbigD[:, 512:1024]]]
        B["pgen"] = [(pbigA[:, 0:512], pbigA[:, 512:1024]),
                     (pbigB[:, 0:512], pbigB[:, 512:1024])]
        B["pqkv"] = B["pg"]
        B["pxy"] = B["pg"]
        B["pg2"] = B["pu"]
        B["psc"] = pbigB[:, 0:640]
        B["psc2"] = pbigD[:, 128:768]
        B["pav2"] = pbigD[:, 768:896]
        B["_ptr_evac"] = [0, 0]
        B["_zs_last"] = [0, 0]
        B["_ptg_evac"] = [0, 0]
        B["_ptg_ix"] = [0]
        B["t_row4"] = True

        bc = lambda ap, dt: ap.bitcast(dt)

        def bind_ffn():
            B["actT"] = PA[:]
            B["xlnTq"] = PB[:, 0:8192]
            B["wT_sb"] = [[PB[:, 8192:10240], PB[:, 10240:12288]],
                          [PB[:, 12288:14336], PB[:, 14336:16384]]]
            B["wgu_int"] = [bc(PC[:, 0:4096], I32), bc(PC[:, 4096:8192], I32)]
            B["wdq"] = [PC[:, 8192:10240], PC[:, 10240:12288]]
            B["wd_int"] = bc(PC[:, 0:8192], I32)
            B["wdstg"] = PC[:, 8192:16384]
            B["wdT_sb"] = [PC[:, 18432 + i * 512:18944 + i * 512] for i in range(4)]
            B["sact"] = [PC[:, 17408:17920], PC[:, 17920:18432]]
            B["xres"] = [bc(PB[:, 20480 + i * 1024:20480 + (i + 1) * 1024], F32) for i in range(4)]
            B["yq"] = [bc(PB[:, 16384 + i * 1024:16384 + (i + 1) * 1024], F32) for i in range(4)]

        def bind_attn0():
            B["bigT"] = PA[:]
            B["qT"] = PB[:]
            B["kT"] = PC[:, 0:8192]
            B["vP"] = PC[:, 8192:16384]
            B["wsb"] = PC[:, 16384:24576]
            B["xres"] = [bc(B["sT"][0][:, 0:1024], F32)]
            B["yq"] = [bc(B["sT"][0][:, 1024:2048], F32)]
            B["in_wq"], B["in_wk"], B["in_wv"], B["in_wo"] = i_wq, i_wk, i_wv, i_wo
            B["in_lam"], B["in_idf"] = i_lam, i_idf
            B["in_cq"], B["in_sq"], B["in_ck"], B["in_sk"], B["in_msk"] = i_cq, i_sq, i_ck, i_sk, i_msk

        def bind_attn1():
            B["bigT"] = PA[:]
            B["yT"] = PB[:]
            B["fpp"] = [xbig, bc(PC[:, 0:4096], F32), bc(PC[:, 4096:8192], F32),
                        bc(PC[:, 8192:12288], F32)]
            B["bfp"] = [PC[:, 18464:20512], PC[:, 16416:18464], PC[:, 20512:22560]]
            def _wosl(rc):
                if rc < 3:
                    return PC[:, 22560 + rc * 512:22560 + (rc + 1) * 512]
                if rc < 7:
                    return B["sT"][1][:, (rc - 3) * 512:(rc - 2) * 512]
                if rc < 11:
                    return B["xlnb"][:, (rc - 7) * 512:(rc - 6) * 512]
                return [B["qa"], B["qb"], B["qr"], B["ropc"], B["rops"]][rc - 11][:]
            B["wosl"] = _wosl
            B["xres"] = [bc(B["sT"][0][:, 0:1024], F32)]
            B["yq"] = [bc(B["sT"][0][:, 1024:2048], F32)]
            B["in_wxy"], B["in_wi"], B["in_bi"], B["in_wa"], B["in_ba"] = i_wxy, i_wi, i_bi, i_wa, i_ba
            B["in_ap"], B["in_cw"], B["in_cb"], B["in_wo"], B["in_idf"] = i_ap, i_cw, i_cb, i_wo1, i_idf
            B["zwxyT"], B["zwoT"], B["zxrT"] = zwxyT, zwoT, zxrT

        sc = Sched()

        def snap():
            return dict(B)
        n_id = sc.dma("sync", "idm", B["identf"][:], i_idf)
        sc.wait("vector", "idm", n_id)
        nid_b = sc.op("vector", lambda e: e.tensor_copy(B["identb"][:], B["identf"][:]), "sv")
        sc.wait("tensor", "sv", nid_b)

        bind_ffn()
        B["in_x"], B["in_gw"], B["in_uw"], B["in_dw"] = i_x, i_gw, i_uw, i_dw
        B["in_gs"], B["in_us"], B["in_ds"], B["in_idf"] = i_gs, i_us, i_ds, i_idf
        emit_const_prep(sc, snap(), [("in_gs", "gsT", 64), ("in_us", "usT", 64), ("in_ds", "dsT", 16)])
        emit_ln_pass(sc, snap(), i_x, zxlnT, "p1")
        emit_ffn_weight_prep_int(sc, snap(), zwg0, zwu0, zwd0)
        for sem in ("zs0", "zs1", "zd"):
            sc.wait("sync", sem, sc.cur(sem))
        emit_ffn_matmuls(sc, snap(), zxlnT, zwg0, zwu0, zwd0, i_x, z[0], "p1")
        sbarrier(sc)

        bind_attn0()
        B["in_x"] = z[0]
        B["out_y"] = z[1]
        emit_attn0(sc, snap(), nc)
        sbarrier(sc)

        bind_ffn()
        B["in_x"] = z[1]
        emit_ln_pass(sc, snap(), z[1], zxlnT, "p3")
        for sem in ("zs0", "zs1"):
            sc.wait("sync", sem, sc.cur(sem))
        emit_ffn_matmuls(sc, snap(), zxlnT, zwg0, zwu0, zwd0, z[1], z[2], "p3")
        sbarrier(sc)

        bind_ffn()
        B["in_x"] = z[2]
        B["t_row4"] = False
        emit_ln_pass(sc, snap(), z[2], zxlnT, "p4")
        jobs = [(i_cpg, "gu", zwg1, cg["G0"], i_dltg), (i_cpu, "gu", zwu1, cg["G0"], i_dltg),
                (i_cpd, "dn", zwd1, cd["G0"], i_dltd)]
        n_la4 = sc.dma("gpsimd", "idm2", B["LA0q"][:], i_la0q)
        n_la4 = sc.dma("gpsimd", "idm2", B["LA1q"][:], i_la1q)
        n_la4 = sc.dma("gpsimd", "idm2", B["LA0q3"][:], i_la0q3)
        n_la4 = sc.dma("gpsimd", "idm2", B["LA1q3"][:], i_la1q3)
        sc.wait("tensor", "idm2", n_la4)
        B["in_dmask_gu"], B["in_dl2_gu"], B["in_phip_gu"] = i_dmg, i_dlg, i_phg
        B["in_dmask_dn"], B["in_dl2_dn"], B["in_phip_dn"] = i_dmd, i_dld, i_phd
        emit_spline_gen(sc, snap(), nc, jobs)
        for sem in ("wz", "zs0", "zs1"):
            sc.wait("sync", sem, sc.cur(sem))
        sc.wait("vector", "wz", sc.cur("wz"))
        B["t_row4"] = True
        emit_ffn_matmuls(sc, snap(), zxlnT, zwg1, zwu1, zwd1, z[2], z[3], "p4")
        sbarrier(sc)

        bind_attn1()
        B["in_x"] = z[3]
        B["out_y"] = z[4]
        emit_attn1(sc, snap(), nc)
        sbarrier(sc)

        bind_ffn()
        B["in_x"] = z[4]
        emit_ln_pass(sc, snap(), z[4], zxlnT, "p6")
        for sem in ("zs0", "zs1"):
            sc.wait("sync", sem, sc.cur(sem))
        emit_ffn_matmuls(sc, snap(), zxlnT, zwg1, zwu1, zwd1, z[4], out_y, "p6")

        for sem in sorted(sc.c.keys()):
            sc.wait("sync", sem, sc.cur(sem))
        sems = {name: E(nc.semaphore(name)) for name in sorted(sc.c.keys())}
        block = E(nc.Block())
        sc.emit(block, sems)
    return nc


def kernel_fused(inp):
    nc = get_nc("fused", build_fused)
    idf = np.eye(128, dtype=np.float32)
    cq, sq, ck, sk, msk = make_attn0_consts()
    lamB = np.tile(inp["L0_attn_lam"][None, :], (128, 1)).astype(np.float32)
    hi = ffn1_host_inputs()
    in_maps = []
    for c in range(NCORES):
        im = dict(x=np.ascontiguousarray(inp["L0_attn_x"][c].astype(np.float32, copy=False)),
                  gw=inp["L0_ffn_gate_w"], uw=inp["L0_ffn_up_w"], dw=inp["L0_ffn_down_w"],
                  gs=inp["L0_ffn_gate_scale"], us=inp["L0_ffn_up_scale"], ds=inp["L0_ffn_down_scale"],
                  wq=inp["L0_attn_Wq"], wk=inp["L0_attn_Wk"], wv=inp["L0_attn_Wv"],
                  wo=inp["L0_attn_Wo"], lamb=lamB,
                  wxy=inp["L1_attn_W_xy"], wi=inp["L1_attn_wi"], bi=inp["L1_attn_bi"],
                  wa=inp["L1_attn_wa"], ba=inp["L1_attn_ba"], apar=inp["L1_attn_a_param"],
                  cw=inp["L1_attn_conv_w"], cb=inp["L1_attn_conv_b"], wo1=inp["L1_attn_W_out"],
                  cpg=pad_cp(inp["L1_ffn_gate_cp"]), cpu=pad_cp(inp["L1_ffn_up_cp"]),
                  cpd=pad_cp(inp["L1_ffn_down_cp"]),
                  idf=idf, cq=cq, sq=sq, ck=ck, sk=sk, msk=msk)
        im.update(hi)
        in_maps.append(im)
    res = run_bass_kernel_spmd(nc, in_maps, core_ids=list(range(NCORES)))
    return np.stack([res.results[c]["y"] for c in range(NCORES)], axis=0)


def kernel(L0_attn_x, L0_attn_Wq, L0_attn_Wk, L0_attn_Wv, L0_attn_Wo, L0_attn_lam,
           L0_ffn_gate_w, L0_ffn_gate_scale, L0_ffn_up_w, L0_ffn_up_scale,
           L0_ffn_down_w, L0_ffn_down_scale,
           L1_attn_W_xy, L1_attn_wi, L1_attn_bi, L1_attn_wa, L1_attn_ba,
           L1_attn_a_param, L1_attn_conv_w, L1_attn_conv_b, L1_attn_W_out,
           L1_ffn_gate_cp, L1_ffn_up_cp, L1_ffn_down_cp):
    inp = dict(L0_attn_x=L0_attn_x, L0_attn_Wq=L0_attn_Wq, L0_attn_Wk=L0_attn_Wk,
               L0_attn_Wv=L0_attn_Wv, L0_attn_Wo=L0_attn_Wo, L0_attn_lam=L0_attn_lam,
               L0_ffn_gate_w=L0_ffn_gate_w, L0_ffn_gate_scale=L0_ffn_gate_scale,
               L0_ffn_up_w=L0_ffn_up_w, L0_ffn_up_scale=L0_ffn_up_scale,
               L0_ffn_down_w=L0_ffn_down_w, L0_ffn_down_scale=L0_ffn_down_scale,
               L1_attn_W_xy=L1_attn_W_xy, L1_attn_wi=L1_attn_wi, L1_attn_bi=L1_attn_bi,
               L1_attn_wa=L1_attn_wa, L1_attn_ba=L1_attn_ba,
               L1_attn_a_param=L1_attn_a_param, L1_attn_conv_w=L1_attn_conv_w,
               L1_attn_conv_b=L1_attn_conv_b, L1_attn_W_out=L1_attn_W_out,
               L1_ffn_gate_cp=L1_ffn_gate_cp, L1_ffn_up_cp=L1_ffn_up_cp,
               L1_ffn_down_cp=L1_ffn_down_cp)
    return kernel_fused(inp).astype(np.float32)
